# version 02
# baseline (speedup 1.0000x reference)
import numpy as np
import ml_dtypes
from contextlib import ExitStack

import concourse.bass as bass
import concourse.mybir as mybir
from concourse.bass_utils import run_bass_kernel_spmd

F32 = mybir.dt.float32
BF16 = mybir.dt.bfloat16
U8 = mybir.dt.uint8
AF = mybir.ActivationFunctionType
ALU = mybir.AluOpType

NCORE = 8
DM = 2048
NKC = DM // 128
H = 16
QKV = 3 * DM
FF = 8192
NIC = FF // 128
EPS = 1e-5
EPS_DEN = 1e-6


def build_nc(tok=2048, dbg=()):
    NTC = tok // 128
    nft = max(1, tok // 512)
    ftok = tok // nft
    nfc = ftok // 128
    ngrp = H // 2

    nc = bass.Bass("TRN2", target_bir_lowering=False, debug=False)

    x_d = nc.dram_tensor("x", [tok, DM], F32, kind="ExternalInput").ap()
    wq_d = nc.dram_tensor("wq", [DM, QKV], F32, kind="ExternalInput").ap()
    wo_d = nc.dram_tensor("wo", [DM, DM], F32, kind="ExternalInput").ap()
    dec_d = nc.dram_tensor("dec", [128, H], F32, kind="ExternalInput").ap()
    oh_d = nc.dram_tensor("oh", [NCORE, 1], F32, kind="ExternalInput").ap()
    gw_d = nc.dram_tensor("gw", [DM, FF], U8, kind="ExternalInput").ap()
    uw_d = nc.dram_tensor("uw", [DM, FF], U8, kind="ExternalInput").ap()
    dw_d = nc.dram_tensor("dw", [FF, DM], U8, kind="ExternalInput").ap()
    gs_d = nc.dram_tensor("gs", [128, NIC], F32, kind="ExternalInput").ap()
    us_d = nc.dram_tensor("us", [128, NIC], F32, kind="ExternalInput").ap()
    dsb_d = nc.dram_tensor("dsb", [128, DM], F32, kind="ExternalInput").ap()
    iota_d = nc.dram_tensor("iota", [128, tok], F32, kind="ExternalInput").ap()
    ident_d = nc.dram_tensor("ident", [128, 128], BF16, kind="ExternalInput").ap()
    n128q_d = nc.dram_tensor("n128q", [1, 128], BF16, kind="ExternalInput").ap()
    n128r_d = nc.dram_tensor("n128r", [1, 512], BF16, kind="ExternalInput").ap()
    ones_d = nc.dram_tensor("ones", [128, 1], BF16, kind="ExternalInput").ap()
    onesr_d = nc.dram_tensor("onesr", [1, 128], BF16, kind="ExternalInput").ap()
    out_d = nc.dram_tensor("out", [tok, DM], F32, kind="ExternalOutput").ap()

    attn_sp = nc.dram_tensor("attn_sp", [DM, tok], BF16)
    xp_sp = nc.dram_tensor("xp_sp", [tok, DM], F32)
    cc_in = [nc.dram_tensor(f"cc_in{g}", [128, 4], F32) for g in range(ngrp)]
    cc_out = [nc.dram_tensor(f"cc_out{g}", [NCORE * 128, 4], F32)
              for g in range(ngrp)]

    DBG = {}
    def dbg_t(name, shape, dt=F32):
        if name in dbg:
            DBG[name] = nc.dram_tensor("dbg_" + name, shape, dt,
                                       kind="ExternalOutput").ap()
    dbg_t("xln", [128, NKC * tok], BF16)
    dbg_t("q0", [128, tok], BF16); dbg_t("k0", [128, tok], BF16)
    dbg_t("kv0", [128, tok], BF16)
    dbg_t("y0", [128, tok], BF16); dbg_t("ys0", [128, tok], BF16)
    dbg_t("at0", [128, tok], BF16)
    dbg_t("den", [128, NTC]); dbg_t("xp", [tok, DM])
    dbg_t("qy", [128, tok], BF16)
    dbg_t("yl0", [128, tok], BF16); dbg_t("lamt", [128, tok])
    dbg_t("init", [128, 4]); dbg_t("gath", [NCORE, 512])
    dbg_t("ccin", [128, 4])
    dbg_t("wo0", [128, DM], BF16); dbg_t("att0", [128, NKC * 512], BF16)
    dbg_t("v1", [128, DM])
    dbg_t("o0", [128, DM]); dbg_t("st3", [128, 16])
    dbg_t("snap", [128, 8]); dbg_t("v1b", [128, DM])
    dbg_t("x30", [128, NKC * 512], BF16); dbg_t("h0", [128, 512], BF16)
    dbg_t("stat", [128, 16]); dbg_t("rden", [128, NTC])

    wq_r = wq_d.rearrange("(kc p) c -> p kc c", p=128)
    wo_r = wo_d.rearrange("(kc p) c -> p kc c", p=128)
    gw_r = gw_d.rearrange("(kc p) c -> p kc c", p=128)
    uw_r = uw_d.rearrange("(kc p) c -> p kc c", p=128)
    dw_r = dw_d.rearrange("(ic p) c -> p ic c", p=128)
    attn_r = attn_sp.ap().rearrange("(fc p) t -> p fc t", p=128)
    xp_r = xp_sp.ap()

    st = ExitStack()
    with st:
        def sb(stk, name, shape, dt=F32):
            return stk.enter_context(nc.sbuf_tensor(name, shape, dt))
        def psm(stk, name, shape, dt=F32):
            return stk.enter_context(nc.psum_tensor(name, shape, dt))

        class _S:
            def __init__(self, name):
                self.h = st.enter_context(nc.semaphore(name))
                self.n = 0
            def inc(self, instr, k=1):
                instr.then_inc(self.h, k); self.n += k; return self.n
            def wait(self, eng, val=None):
                v = self.n if val is None else val
                if v > 0:
                    eng.wait_ge(self.h, v)

        sLd = _S("sLd"); sW = _S("sW"); sPE = _S("sPE"); sDV = _S("sDV")
        sAC = _S("sAC"); sSp = _S("sSp"); sCc = _S("sCc"); sG = _S("sG")
        sOut = _S("sOut")

        s_x = [sb(st, f"s_x{i}", [128, DM]) for i in range(2)]
        s_xlntm = sb(st, "s_xlntm", [128, DM], BF16)
        s_stat = sb(st, "s_stat", [128, 16])
        s_dec = sb(st, "s_dec", [128, H]); s_lam = sb(st, "s_lam", [128, H])
        s_lam2 = sb(st, "s_lam2", [128, H]); s_junk = sb(st, "s_junk", [128, 1])
        s_lnl = sb(st, "s_lnl", [128, H])
        s_oh = sb(st, "s_oh", [NCORE, 1]); s_ohbf = sb(st, "s_ohbf", [NCORE, 1], BF16)
        s_ident = sb(st, "s_ident", [128, 128], BF16)
        s_ones = sb(st, "s_ones", [128, 1], BF16)
        s_onesr = sb(st, "s_onesr", [1, 128], BF16)
        s_n128q = sb(st, "s_n128q", [1, 128], BF16)
        s_n128r = sb(st, "s_n128r", [1, 512], BF16)
        s_gs = sb(st, "s_gs", [128, NIC]); s_us = sb(st, "s_us", [128, NIC])
        s_den = sb(st, "s_den", [128, NTC]); s_rden = sb(st, "s_rden", [128, NTC])

        block = st.enter_context(nc.Block())

        @block.sync
        def _(sync):
            for tile, src in ((s_dec, dec_d), (s_oh, oh_d), (s_ident, ident_d),
                              (s_ones, ones_d), (s_n128q, n128q_d),
                              (s_n128r, n128r_d), (s_gs, gs_d), (s_us, us_d),
                              (s_onesr, onesr_d)):
                sLd.inc(sync.dma_start(out=tile[:], in_=src), 16)

        @block.scalar
        def _(scalar):
            sLd.wait(scalar)
            sAC.inc(scalar.activation(s_lam[:], s_dec[:], AF.Exp, bias=0.0, scale=-1.0))

        @block.vector
        def _(vector):
            vector.memset(s_stat[:], 0.0)
            sAC.wait(vector)
            sDV.inc(vector.tensor_scalar_add(s_lam[:], s_lam[:], 1.0))
            sDV.inc(vector.tensor_copy(s_ohbf[:], s_oh[:]))

        @block.scalar
        def _(scalar):
            sDV.wait(scalar)
            scalar.activation(s_lnl[:], s_lam[:], AF.Ln, bias=0.0, scale=1.0)
            sAC.inc(scalar.activation(s_lam2[:], s_lnl[:], AF.Exp, bias=0.0, scale=-1.0))
        lam_ac = sAC.n
        lam_dv = sDV.n
        stat_free = {0: 0, 1: 0}

        def ln_stats(src_tile, scr_tile, pp, pre_wait=()):
            b = 8 * pp

            @block.scalar
            def _(scalar, src=src_tile, scr=scr_tile, b=b, pp=pp, pw=tuple(pre_wait)):
                for sem, val in pw:
                    sem.wait(scalar, val)
                sDV.wait(scalar, stat_free[pp])
                scalar.activation(scr[:], src[:], AF.Square, bias=0.0,
                                  scale=1.0, accum_out=s_stat[:, b + 1:b + 2])
                scalar.activation(scr[:], src[:], AF.Identity, bias=0.0,
                                  scale=1.0, accum_out=s_stat[:, b + 0:b + 1])
                sAC.inc(scalar.copy(s_stat[:, b + 5:b + 7], s_stat[:, b:b + 2]))

            @block.vector
            def _(vector, b=b):
                sAC.wait(vector)
                vector.tensor_scalar_mul(s_stat[:, b + 2:b + 3], s_stat[:, b:b + 1], -1.0 / DM)
                vector.tensor_scalar_mul(s_stat[:, b + 7:b + 8], s_stat[:, b:b + 1], 1.0 / DM)
                vector.tensor_scalar(s_stat[:, b + 3:b + 4], s_stat[:, b + 1:b + 2],
                                     1.0 / DM, EPS, ALU.mult, ALU.add)
                vector.tensor_tensor(s_stat[:, b + 4:b + 5], s_stat[:, b + 7:b + 8],
                                     s_stat[:, b + 7:b + 8], ALU.mult)
                vector.tensor_copy(s_junk[:], s_stat[:, b + 2:b + 3])
                sDV.inc(vector.tensor_tensor(s_stat[:, b + 3:b + 4], s_stat[:, b + 3:b + 4],
                                             s_stat[:, b + 4:b + 5], ALU.subtract))

            @block.scalar
            def _(scalar, b=b):
                sDV.wait(scalar)
                scalar.activation(s_stat[:, b + 5:b + 6], s_stat[:, b + 3:b + 4],
                                  AF.Ln, bias=0.0, scale=1.0)
                sAC.inc(scalar.activation(s_stat[:, b + 6:b + 7], s_stat[:, b + 5:b + 6],
                                          AF.Exp, bias=0.0, scale=-0.5))

        p123 = st.enter_context(ExitStack())
        s_big = sb(p123, "s_big", [128, NKC, max(tok, DM)], BF16)
        s_scr = sb(p123, "s_scr", [128, DM])

        with ExitStack() as p1:
            pTr = psm(p1, "pTr", [128, 2048], BF16)
            x_free = [0, 0]
            tr_free = [0, 0]
            xlntm_free = [0]
            for tc in range(NTC):
                par = tc % 2
                pp = tc % 2

                @block.sync
                def _(sync, tc=tc, par=par):
                    sDV.wait(sync, x_free[par])
                    sLd.inc(sync.dma_start(out=s_x[par][:],
                                           in_=x_d[tc * 128:(tc + 1) * 128, :]), 16)
                x_ld = sLd.n
                ln_stats(s_x[par], s_scr, pp, pre_wait=[(sLd, x_ld)])

                @block.vector
                def _(vector, par=par, pp=pp):
                    sAC.wait(vector)
                    sPE.wait(vector, xlntm_free[0])
                    b = 8 * pp
                    n = sDV.inc(vector.tensor_scalar(
                        s_xlntm[:], s_x[par][:], s_stat[:, b + 2:b + 3],
                        s_stat[:, b + 6:b + 7], ALU.add, ALU.mult))
                    x_free[par] = n
                    stat_free[pp] = n

                for fq in range(4):
                    tp = fq % 2

                    @block.tensor
                    def _(tensor, fq=fq, tp=tp):
                        sDV.wait(tensor)
                        sDV.wait(tensor, tr_free[tp])
                        ins = None
                        for j in range(4):
                            fc = fq * 4 + j
                            ins = tensor.transpose(
                                pTr[:, tp * 1024 + j * 128: tp * 1024 + (j + 1) * 128],
                                s_xlntm[:, fc * 128:(fc + 1) * 128], s_ident[:])
                        sPE.inc(ins)

                    @block.vector
                    def _(vector, tc=tc, fq=fq, tp=tp):
                        sPE.wait(vector)
                        n = sDV.inc(vector.tensor_copy(
                            s_big[:, fq * 4:(fq + 1) * 4, tc * 128:(tc + 1) * 128],
                            pTr[:, tp * 1024:tp * 1024 + 512]
                                .rearrange("p (a b) -> p a b", a=4)))
                        tr_free[tp] = n
                xlntm_free[0] = sPE.n
        xln_dv = sDV.n

        if "xln" in DBG:
            @block.gpsimd
            def _(gpsimd):
                sDV.wait(gpsimd)
                sW.inc(gpsimd.dma_start(
                    out=DBG["xln"].rearrange("p (a b) -> p a b", a=NKC),
                    in_=s_big[:, :, 0:tok]), 16)

        p2 = ExitStack()
        with p2:
            pQ = [psm(p2, f"pQ{i}", [128, 1024]) for i in range(2)]
            pD = psm(p2, "pD", [128, 1024])
            pR = psm(p2, "pR", [128, 1024])
            s_denr = sb(p2, "s_denr", [1, min(1024, tok)])
            s_denrb = sb(p2, "s_denrb", [1, min(1024, tok)], BF16)
            s_wqb = [sb(p2, f"s_wqb{i}", [128, NKC, 128], BF16) for i in range(4)]
            s_q = [sb(p2, f"s_q{i}", [128, tok], BF16) for i in range(4)]
            s_k = [sb(p2, f"s_k{i}", [128, tok], BF16) for i in range(2)]
            s_kv = [sb(p2, f"s_kv{i}", [128, tok], BF16) for i in range(2)]
            s_y = [sb(p2, f"s_y{i}", [128, tok], BF16) for i in range(6)]
            s_attn = sb(p2, "s_attn", [128, tok], BF16)
            s_qy = sb(p2, "s_qy", [128, tok], BF16)
            s_lamt = sb(p2, "s_lamt", [128, tok])
            s_iota = sb(p2, "s_iota", [128, tok])
            s_phr = sb(p2, "s_phr", [128, 1024], BF16)
            s_phe = sb(p2, "s_phe", [128, 1024], BF16)
            s_sloc = sb(p2, "s_sloc", [128, 4])
            s_gath = sb(p2, "s_gath", [NCORE, 512])
            s_gathbf = sb(p2, "s_gathbf", [NCORE, 512], BF16)
            s_init = [sb(p2, f"s_init{i}", [128, 4]) for i in range(2)]

            @block.sync
            def _(sync):
                sLd.inc(sync.dma_start(out=s_iota[:], in_=iota_d), 16)
            iota_ld = sLd.n

            @block.vector
            def _(vector):
                vector.memset(s_sloc[:], 0.0)

            wq_free = [0] * 4
            pq_free_dv = [0, 0]
            pq_free_ac = [0, 0]
            ph_free = [0]
            scan_dv = [0] * H
            grp_gath = [0] * ngrp
            grp_init = [0] * ngrp
            denr_free = [0]
            pr_free = [0]
            denr_ac_gate = [0]
            denrb_free = [0]
            gath_free = [0]
            sloc_free = [0]
            lamt_free = [0]
            qy_free = [0]
            attn_free = [0]
            pq_par = [0]

            def emit_strip_load(hi, tp):
                slot = (hi * 3 + tp) % 4
                cbase = tp * DM + hi * 128

                @block.gpsimd
                def _(gpsimd, slot=slot, cbase=cbase):
                    sPE.wait(gpsimd, wq_free[slot])
                    ins = None
                    for kq in range(4):
                        ins = gpsimd.dma_start(
                            out=s_wqb[slot][:, kq * 4:(kq + 1) * 4, :],
                            in_=wq_r[:, kq * 4:(kq + 1) * 4, cbase:cbase + 128])
                        sW.inc(ins, 16)
                return slot, sW.n

            nhalf = max(1, tok // 1024)
            hw = min(1024, tok)

            def emit_qkv(hi, tp, slot, wq_rdy):
                for half in range(nhalf):
                    par = pq_par[0]
                    pq = pQ[par]

                    @block.tensor
                    def _(tensor, slot=slot, wq_rdy=wq_rdy, half=half, pq=pq, par=par):
                        sW.wait(tensor, wq_rdy)
                        sDV.wait(tensor, xln_dv)
                        sAC.wait(tensor, pq_free_ac[par])
                        sDV.wait(tensor, pq_free_dv[par])
                        ins = None
                        for ns in range(hw // 512):
                            t0 = half * hw + ns * 512
                            for kc in range(NKC):
                                ins = tensor.matmul(
                                    pq[:, ns * 512:(ns + 1) * 512],
                                    s_wqb[slot][:, kc, :],
                                    s_big[:, kc, t0:t0 + 512],
                                    start=(kc == 0), stop=(kc == NKC - 1))
                        sPE.inc(ins)
                    pe_done = sPE.n

                    if tp < 2:
                        dst = s_q[hi % 4] if tp == 0 else s_k[hi % 2]

                        @block.scalar
                        def _(scalar, pq=pq, pe=pe_done):
                            sPE.wait(scalar, pe)
                            sDV.wait(scalar, ph_free[0])
                            scalar.activation(s_phr[:, 0:hw], pq[:, 0:hw], AF.Relu,
                                              bias=0.0, scale=-1.0)
                            sAC.inc(scalar.activation(s_phe[:, 0:hw], s_phr[:, 0:hw],
                                                      AF.Exp, bias=0.0, scale=-1.0))
                        ac_done = sAC.n

                        @block.vector
                        def _(vector, pq=pq, dst=dst, half=half, ac=ac_done, par=par):
                            sAC.wait(vector, ac)
                            vector.tensor_tensor(s_phr[:, 0:hw], pq[:, 0:hw],
                                                 s_phr[:, 0:hw], ALU.add)
                            n = sDV.inc(vector.tensor_tensor(
                                dst[:, half * hw:(half + 1) * hw],
                                s_phr[:, 0:hw], s_phe[:, 0:hw], ALU.add))
                            pq_free_dv[par] = n
                            ph_free[0] = n
                        pq_free_ac[par] = ac_done
                    else:
                        @block.vector
                        def _(vector, pq=pq, hi=hi, half=half, pe=pe_done, par=par):
                            sPE.wait(vector, pe)
                            n = sDV.inc(vector.tensor_tensor(
                                s_kv[hi % 2][:, half * hw:(half + 1) * hw],
                                s_k[hi % 2][:, half * hw:(half + 1) * hw],
                                pq[:, 0:hw], ALU.mult))
                            pq_free_dv[par] = n
                            pq_free_ac[par] = sAC.n
                    pq_par[0] = 1 - par

            def emit_scans(hi):
                ykv = s_y[(hi % 3) * 2]
                yks = s_y[(hi % 3) * 2 + 1]

                @block.vector
                def _(vector, hi=hi, ykv=ykv, yks=yks):
                    sAC.wait(vector, lam_ac)
                    sG.wait(vector, sloc_free[0])
                    if DBG:
                        sW.wait(vector)
                    lamc = s_lam2[:, hi:hi + 1].broadcast_to([128, tok])
                    vector.tensor_tensor_scan(ykv[:], lamc, s_kv[hi % 2][:],
                                              0.0, ALU.mult, ALU.add)
                    vector.tensor_tensor_scan(yks[:], lamc, s_k[hi % 2][:],
                                              0.0, ALU.mult, ALU.add)
                    hl = hi % 2
                    vector.tensor_copy(s_sloc[:, 2 * hl:2 * hl + 1], ykv[:, tok - 1:tok])
                    vector.tensor_copy(s_sloc[:, 2 * hl + 1:2 * hl + 2],
                                       yks[:, tok - 1:tok])
                    sDV.inc(vector.reduce_sum(s_junk[:], s_sloc[:],
                                              axis=mybir.AxisListType.X))
                scan_dv[hi] = sDV.n
                return ykv, yks

            def emit_collective(g):
                @block.gpsimd
                def _(gpsimd, g=g):
                    sDV.wait(gpsimd, scan_dv[2 * g + 1])
                    n = sG.inc(gpsimd.dma_start(out=cc_in[g].ap(), in_=s_sloc[:]), 16)
                    sloc_free[0] = n
                    sG.wait(gpsimd, n)
                    sCc.inc(gpsimd.collective_compute(
                        "AllGather", ALU.bypass,
                        replica_groups=[list(range(NCORE))],
                        ins=[cc_in[g].ap().opt()],
                        outs=[cc_out[g].ap().opt()]), 1)
                    sCc.wait(gpsimd)
                    sPE.wait(gpsimd, gath_free[0])
                    sG.inc(gpsimd.dma_start(
                        out=s_gath[:],
                        in_=cc_out[g].ap().rearrange("(r p) g -> r (p g)", p=128)), 16)
                grp_gath[g] = sG.n

            def emit_select(g):
                @block.vector
                def _(vector, g=g):
                    sG.wait(vector, grp_gath[g])
                    sDV.inc(vector.tensor_copy(
                        s_gathbf[:].rearrange("r (g p) -> r g p", g=4),
                        s_gath[:, :].rearrange("r (p g) -> r g p", g=4)))
                dv = sDV.n

                @block.tensor
                def _(tensor, dv=dv, g=g):
                    sDV.wait(tensor, dv)
                    if g > 0:
                        sDV.wait(tensor, grp_init[g - 1])
                    ins = None
                    for gc in range(4):
                        ins = tensor.matmul(pD[:, 512 + gc:513 + gc],
                                            s_gathbf[:, gc * 128:(gc + 1) * 128],
                                            s_ohbf[:],
                                            start=True, stop=True)
                    sPE.inc(ins)
                gath_free[0] = sPE.n

                @block.vector
                def _(vector, g=g):
                    sPE.wait(vector)
                    sDV.inc(vector.tensor_copy(s_init[g % 2][:], pD[:, 512:516]))
                grp_init[g] = sDV.n
                if g == 0 and "init" in DBG:
                    @block.gpsimd
                    def _(gpsimd):
                        sDV.wait(gpsimd)
                        sW.inc(gpsimd.dma_start(out=DBG["init"], in_=s_init[0][:]), 16)
                        sW.inc(gpsimd.dma_start(out=DBG["gath"], in_=s_gath[:]), 16)
                        if "ccin" in DBG:
                            sW.inc(gpsimd.dma_start(out=DBG["ccin"],
                                                    in_=cc_in[0].ap()), 16)

            def emit_epilogue(g, heads_y, last=False):
                for hl, (hi, ykv, yks) in enumerate(heads_y):
                    @block.scalar
                    def _(scalar, hi=hi):
                        sAC.wait(scalar, lam_ac)
                        sLd.wait(scalar, iota_ld)
                        sDV.wait(scalar, lamt_free[0])
                        sAC.inc(scalar.activation(s_lamt[:], s_iota[:], AF.Exp,
                                                  bias=0.0, scale=s_lnl[:, hi:hi + 1]))
                    ac_done = sAC.n

                    if hi == 0 and "lamt" in DBG:
                        @block.gpsimd
                        def _(gpsimd, ac=ac_done):
                            sAC.wait(gpsimd, ac)
                            sW.inc(gpsimd.dma_start(out=DBG["lamt"], in_=s_lamt[:]), 16)

                    @block.vector
                    def _(vector, g=g, hl=hl, hi=hi, ykv=ykv, yks=yks, ac=ac_done):
                        sAC.wait(vector, ac)
                        sDV.wait(vector, grp_init[g])
                        if DBG:
                            sW.wait(vector)
                        ini = s_init[g % 2]
                        vector.scalar_tensor_tensor(ykv[:], s_lamt[:],
                                                    ini[:, 2 * hl:2 * hl + 1],
                                                    ykv[:], ALU.mult, ALU.add)
                        n = sDV.inc(vector.scalar_tensor_tensor(
                            yks[:], s_lamt[:], ini[:, 2 * hl + 1:2 * hl + 2],
                            yks[:], ALU.mult, ALU.add))
                        lamt_free[0] = n
                        sPE.wait(vector, qy_free[0])
                        sDV.inc(vector.tensor_tensor(s_qy[:], s_q[hi % 4][:],
                                                     yks[:], ALU.mult))
                    qy_dv = sDV.n

                    @block.vector
                    def _(vector, hi=hi, ykv=ykv):
                        sSp.wait(vector, attn_free[0])
                        if DBG:
                            sW.wait(vector)
                        sDV.inc(vector.tensor_tensor(s_attn[:], s_q[hi % 4][:],
                                                     ykv[:], ALU.mult))
                    atm_dv = sDV.n

                    hw2 = min(1024, tok)
                    for half in range(max(1, tok // 1024)):
                        for qr in range(hw2 // 512):
                            @block.tensor
                            def _(tensor, half=half, qr=qr, qy_dv=qy_dv):
                                if half == 0 and qr == 0:
                                    sDV.wait(tensor, qy_dv)
                                sDV.wait(tensor, denr_free[0])
                                sAC.wait(tensor, denr_ac_gate[0])
                                t0 = half * hw2 + qr * 512
                                sPE.inc(tensor.matmul(pD[0:1, 0:512], s_ones[:],
                                                      s_qy[:, t0:t0 + 512],
                                                      start=True, stop=True))
                            pe_n = sPE.n

                            @block.vector
                            def _(vector, qr=qr, pe_n=pe_n):
                                sPE.wait(vector, pe_n)
                                denr_free[0] = sDV.inc(vector.tensor_copy(
                                    s_denr[0:1, qr * 512:(qr + 1) * 512],
                                    pD[0:1, 0:512]))
                        if half + 1 == max(1, tok // 1024):
                            qy_free[0] = sPE.n

                        @block.vector
                        def _(vector):
                            vector.tensor_copy(s_junk[:], s_ones[:])
                            sDV.inc(vector.tensor_scalar_max(s_denr[:], s_denr[:],
                                                             EPS_DEN))
                        dmax_dv = sDV.n

                        @block.scalar
                        def _(scalar, dmax_dv=dmax_dv):
                            sDV.wait(scalar, dmax_dv)
                            sPE.wait(scalar, denrb_free[0])
                            scalar.activation(s_denr[:], s_denr[:], AF.Ln,
                                              bias=0.0, scale=1.0)
                            sAC.inc(scalar.activation(s_denrb[:], s_denr[:], AF.Exp,
                                                      bias=0.0, scale=-1.0))
                        rdenr_ac = sAC.n
                        denr_ac_gate[0] = rdenr_ac

                        @block.tensor
                        def _(tensor, half=half, hw2=hw2, rdenr_ac=rdenr_ac, g=g):
                            sAC.wait(tensor, rdenr_ac)
                            sDV.wait(tensor, grp_init[g])
                            sDV.wait(tensor, pr_free[0])
                            ins = None
                            for ns in range(hw2 // 512):
                                ins = tensor.matmul(
                                    pR[:, ns * 512:(ns + 1) * 512], s_onesr[:],
                                    s_denrb[0:1, ns * 512:(ns + 1) * 512],
                                    start=True, stop=True)
                            sPE.inc(ins)
                        bc_pe = sPE.n
                        denrb_free[0] = bc_pe

                        @block.vector
                        def _(vector, half=half, hw2=hw2, bc_pe=bc_pe):
                            sPE.wait(vector, bc_pe)
                            pr_free[0] = sDV.inc(vector.tensor_tensor(
                                s_attn[:, half * hw2:(half + 1) * hw2],
                                s_attn[:, half * hw2:(half + 1) * hw2],
                                pR[:, 0:hw2], ALU.mult))
                    at_dv = sDV.n

                    if hi == 0 and "at0" in DBG:
                        @block.gpsimd
                        def _(gpsimd, at_dv=at_dv):
                            sDV.wait(gpsimd, at_dv)
                            sW.inc(gpsimd.dma_start(out=DBG["at0"], in_=s_attn[:]), 16)
                    if hi == 0 and "y0" in DBG:
                        @block.gpsimd
                        def _(gpsimd, at_dv=at_dv, ykv=ykv, yks=yks):
                            sDV.wait(gpsimd, at_dv)
                            sW.inc(gpsimd.dma_start(out=DBG["y0"], in_=ykv[:]), 16)
                            sW.inc(gpsimd.dma_start(out=DBG["ys0"], in_=yks[:]), 16)

                    @block.sync
                    def _(sync, hi=hi, at_dv=at_dv):
                        sDV.wait(sync, at_dv)
                        attn_free[0] = sSp.inc(sync.dma_start(
                            out=attn_r[:, hi, :], in_=s_attn[:]), 16)

            pend = []
            for g in range(ngrp):
                heads_y = []
                for hl in range(2):
                    hi = 2 * g + hl
                    strips = [emit_strip_load(hi, tp) for tp in range(3)]
                    for tp in range(3):
                        slot, rdy = strips[tp]
                        emit_qkv(hi, tp, slot, rdy)
                        wq_free[slot] = sPE.n
                    heads_y.append((hi,) + emit_scans(hi))

                    if hi == 0 and "yl0" in DBG:
                        @block.gpsimd
                        def _(gpsimd):
                            sDV.wait(gpsimd)
                            sW.inc(gpsimd.dma_start(out=DBG["yl0"],
                                                    in_=s_y[0][:]), 16)
                    if hi == 0:
                        for nm, tile in (("q0", s_q[0]), ("k0", s_k[0]), ("kv0", s_kv[0])):
                            if nm in DBG:
                                @block.gpsimd
                                def _(gpsimd, nm=nm, tile=tile):
                                    sDV.wait(gpsimd)
                                    sW.inc(gpsimd.dma_start(out=DBG[nm], in_=tile[:]), 16)

                    if pend and hl == 0:
                        emit_epilogue(*pend.pop(0))
                emit_collective(g)
                emit_select(g)
                pend.append((g, heads_y))
            while pend:
                pg, ph = pend.pop(0)
                emit_epilogue(pg, ph, last=not pend)

            rden_ac = sAC.n

            if "qy" in DBG:
                @block.gpsimd
                def _(gpsimd):
                    sDV.wait(gpsimd)
                    sW.inc(gpsimd.dma_start(out=DBG["qy"], in_=s_qy[:]), 16)

            attn_spilled = sSp.n
            last_qkv_pe = sPE.n

        with ExitStack() as p3:
            pW = [psm(p3, f"pW{i}", [128, 1024]) for i in range(2)]
            s_snap = sb(p3, "s_snap", [128, 8]) if "snap" in DBG else None
            s_v1b = sb(p3, "s_v1b", [128, DM]) if "snap" in DBG else None
            s_att = [sb(p3, f"s_att{i}", [128, NKC, 512], BF16) for i in range(2)]
            s_v1 = sb(p3, "s_v1", [128, DM])
            s_o = [sb(p3, f"s_o{i}", [128, DM]) for i in range(2)]

            @block.gpsimd
            def _(gpsimd):
                sPE.wait(gpsimd, last_qkv_pe)
                sW.wait(gpsimd)
                ins = None
                for kc in range(NKC):
                    ins = gpsimd.dma_start(out=s_big[:, kc:kc + 1, 0:DM],
                                           in_=wo_r[:, kc:kc + 1, :])
                    sW.inc(ins, 16)
            wo_ld = sW.n
            if "wo0" in DBG:
                @block.gpsimd
                def _(gpsimd):
                    sW.wait(gpsimd)
                    sW.inc(gpsimd.dma_start(out=DBG["wo0"], in_=s_big[:, 0, 0:DM]), 16)

            att_free = [0, 0]
            pw_free = [0, 0]
            v1_ac_free = [0]
            o_free = [0, 0]
            x_free3 = [0, 0]
            for tt in range((NTC + 3) // 4):
                ab = tt % 2

                @block.sync
                def _(sync, tt=tt, ab=ab):
                    sSp.wait(sync, attn_spilled)
                    sPE.wait(sync, att_free[ab])
                    for fq in range(4):
                        sLd.inc(sync.dma_start(
                            out=s_att[ab][:, fq * 4:(fq + 1) * 4, :],
                            in_=attn_r[:, fq * 4:(fq + 1) * 4,
                                       tt * 512:(tt + 1) * 512]), 16)
                att_ld = sLd.n
                if tt == 0 and "att0" in DBG:
                    @block.gpsimd
                    def _(gpsimd, ab=ab, att_ld=att_ld):
                        sLd.wait(gpsimd, att_ld)
                        sW.inc(gpsimd.dma_start(
                            out=DBG["att0"].rearrange("p (a b) -> p a b", a=NKC),
                            in_=s_att[ab][:, :, :]), 16)

                for tcl in range(min(4, NTC - tt * 4)):
                    tc = tt * 4 + tcl
                    xp = tc % 2

                    @block.sync
                    def _(sync, tc=tc, xp=xp):
                        sDV.wait(sync, x_free3[xp])
                        sLd.inc(sync.dma_start(out=s_x[xp][:],
                                               in_=x_d[tc * 128:(tc + 1) * 128, :]), 16)
                    x_ld = sLd.n

                    for half in range(2):
                        @block.tensor
                        def _(tensor, tcl=tcl, half=half, ab=ab, att_ld=att_ld):
                            sLd.wait(tensor, att_ld)
                            sW.wait(tensor, wo_ld)
                            sAC.wait(tensor, rden_ac)
                            sDV.wait(tensor, pw_free[half])
                            ins = None
                            for ns in range(2):
                                c0 = half * 1024 + ns * 512
                                for kc in range(NKC):
                                    ins = tensor.matmul(
                                        pW[half][:, ns * 512:(ns + 1) * 512],
                                        s_att[ab][:, kc, tcl * 128:(tcl + 1) * 128],
                                        s_big[:, kc, c0:c0 + 512],
                                        start=(kc == 0), stop=(kc == NKC - 1))
                            sPE.inc(ins)
                        pe_done = sPE.n
                        if half == 1:
                            att_free[ab] = pe_done

                        @block.vector
                        def _(vector, tc=tc, half=half, pe=pe_done):
                            sPE.wait(vector, pe)
                            sAC.wait(vector, v1_ac_free[0])
                            pw_free[half] = sDV.inc(vector.tensor_copy(
                                s_v1[:, half * 1024:(half + 1) * 1024], pW[half][:]))

                    if tc == 0 and "v1" in DBG:
                        @block.gpsimd
                        def _(gpsimd):
                            sDV.wait(gpsimd)
                            sW.inc(gpsimd.dma_start(out=DBG["v1"], in_=s_v1[:]), 16)
                    pp = tc % 2
                    ln_stats(s_v1, s_scr, pp,
                             pre_wait=[(sDV, sDV.n)] + ([(sW, sW.n)] if DBG else []))
                    v1_ac_free[0] = sAC.n
                    op = tc % 2

                    @block.vector
                    def _(vector, tc=tc, pp=pp, op=op, xp=xp, x_ld=x_ld):
                        sAC.wait(vector)
                        b = 8 * pp
                        sSp.wait(vector, o_free[op])
                        if tc == 0 and "snap" in DBG:
                            vector.tensor_copy(s_snap[:], s_stat[:, 0:8])
                            vector.tensor_copy(s_v1b[:], s_v1[:])
                        vector.tensor_scalar(s_o[op][:], s_v1[:], s_stat[:, b + 2:b + 3],
                                             s_stat[:, b + 6:b + 7], ALU.add, ALU.mult)
                        sLd.wait(vector, x_ld)
                        n = sDV.inc(vector.tensor_tensor(s_o[op][:], s_o[op][:],
                                                         s_x[xp][:], ALU.add))
                        x_free3[xp] = n
                        stat_free[pp] = n
                    o_dv = sDV.n

                    if tc == 0 and "o0" in DBG:
                        @block.gpsimd
                        def _(gpsimd, o_dv=o_dv):
                            sDV.wait(gpsimd, o_dv)
                            sW.inc(gpsimd.dma_start(out=DBG["o0"], in_=s_o[0][:]), 16)
                            if "st3" in DBG:
                                sW.inc(gpsimd.dma_start(out=DBG["st3"], in_=s_stat[:]), 16)
                            if "snap" in DBG:
                                sW.inc(gpsimd.dma_start(out=DBG["snap"], in_=s_snap[:]), 16)
                                sW.inc(gpsimd.dma_start(out=DBG["v1b"], in_=s_v1b[:]), 16)

                    @block.sync
                    def _(sync, tc=tc, op=op, o_dv=o_dv):
                        sDV.wait(sync, o_dv)
                        o_free[op] = sSp.inc(sync.dma_start(
                            out=xp_r[tc * 128:(tc + 1) * 128, :], in_=s_o[op][:]), 16)

            if "xp" in DBG:
                @block.gpsimd
                def _(gpsimd):
                    sSp.wait(gpsimd)
                    sW.inc(gpsimd.dma_start(out=DBG["xp"], in_=xp_r[:, :]), 16)
            xp_done = sSp.n
        p123.close()

        with ExitStack() as p4:
            s_x3 = sb(p4, "s_x3", [128, NKC, 512], BF16)
            s_scr4 = sb(p4, "s_scr4", [128, DM])
            s_o4 = sb(p4, "s_o4", [128, DM])
            s_wpool = sb(p4, "s_wpool", [128, 4 * NKC * 512], BF16)
            s_gw = [s_wpool[:, i * 8192:(i + 1) * 8192]
                    .rearrange("p (a b) -> p a b", a=NKC) for i in range(2)]
            s_uw = [s_wpool[:, (2 + i) * 8192:(3 + i) * 8192]
                    .rearrange("p (a b) -> p a b", a=NKC) for i in range(2)]
            s_dw = [s_wpool[:, i * 4096:(i + 1) * 4096]
                    .rearrange("p (a b) -> p a b", a=2) for i in range(2)]
            s_h = sb(p4, "s_h", [128, NIC, 512], BF16)
            s_t1 = sb(p4, "s_t1", [128, 512], BF16)
            s_t2 = sb(p4, "s_t2", [128, 512], BF16)
            s_dsb = sb(p4, "s_dsb", [128, DM])
            s_xsum = sb(p4, "s_xsum", [1, 512], BF16)
            s_hsum = sb(p4, "s_hsum", [1, 512])
            s_hsbf = sb(p4, "s_hsbf", [1, 1024], BF16)

            @block.sync
            def _(sync):
                sLd.inc(sync.dma_start(out=s_dsb[:], in_=dsb_d), 16)
            dsb_ld = sLd.n

            x_free4 = [0, 0]
            tr2_free = [0, 0]
            gw_free = [0, 0]
            uw_free = [0, 0]
            dw_free = [0, 0]
            pf_free = [[0, 0], [0, 0]]
            t1_free = [0]
            x3_free = [0]
            h_free = [0]
            scr4_dv_free = [0]
            o4_free = [0, 0]
            xsum_free = [0]
            down_evac_dv = [0]

            for ft in range(nft):
                with nc.psum_tensor(f"pTr2_{ft}", [128, 2048], BF16) as pTr2:
                    for cl in range(nfc):
                        tc = ft * nfc + cl
                        xp = tc % 2
                        pp = tc % 2

                        @block.sync
                        def _(sync, tc=tc, xp=xp):
                            sSp.wait(sync, xp_done)
                            sDV.wait(sync, x_free4[xp])
                            sLd.inc(sync.dma_start(
                                out=s_x[xp][:],
                                in_=xp_r[tc * 128:(tc + 1) * 128, :]), 16)
                        x_ld = sLd.n
                        ln_stats(s_x[xp], s_scr4, pp,
                                 pre_wait=[(sLd, x_ld), (sDV, sDV.n)])

                        @block.vector
                        def _(vector, xp=xp, pp=pp):
                            sAC.wait(vector)
                            b = 8 * pp
                            n = sDV.inc(vector.tensor_scalar(
                                s_xlntm[:], s_x[xp][:], s_stat[:, b + 2:b + 3],
                                s_stat[:, b + 6:b + 7], ALU.add, ALU.mult))
                            x_free4[xp] = n
                            stat_free[pp] = n

                        for fq in range(4):
                            tp = fq % 2

                            @block.tensor
                            def _(tensor, fq=fq, tp=tp):
                                sDV.wait(tensor)
                                sDV.wait(tensor, tr2_free[tp])
                                if fq == 0:
                                    sPE.wait(tensor, x3_free[0])
                                ins = None
                                for j in range(4):
                                    fc = fq * 4 + j
                                    ins = tensor.transpose(
                                        pTr2[:, tp * 1024 + j * 128:
                                             tp * 1024 + (j + 1) * 128],
                                        s_xlntm[:, fc * 128:(fc + 1) * 128], s_ident[:])
                                sPE.inc(ins)

                            @block.vector
                            def _(vector, cl=cl, fq=fq, tp=tp):
                                sPE.wait(vector)
                                tr2_free[tp] = sDV.inc(vector.tensor_copy(
                                    s_x3[:, fq * 4:(fq + 1) * 4, cl * 128:(cl + 1) * 128],
                                    pTr2[:, tp * 1024:tp * 1024 + 512]
                                        .rearrange("p (a b) -> p a b", a=4)))
                x3_dv = sDV.n
                if "x30" in DBG and ft == 0:
                    @block.gpsimd
                    def _(gpsimd):
                        sDV.wait(gpsimd)
                        sW.inc(gpsimd.dma_start(
                            out=DBG["x30"],
                            in_=s_x3[:, :, :].rearrange("p a b -> p (a b)")), 16)

                gw_free[0] = gw_free[1] = sPE.n
                uw_free[0] = uw_free[1] = sPE.n
                with nc.psum_tensor(f"pF0_{ft}", [128, 1024]) as pF0, \
                     nc.psum_tensor(f"pF1_{ft}", [128, 1024]) as pF1, \
                     nc.psum_tensor(f"pRH_{ft}", [1, 512]) as pRH:
                    pF = [pF0, pF1]

                    @block.tensor
                    def _(tensor, x3_dv=x3_dv):
                        sDV.wait(tensor, x3_dv)
                        sDV.wait(tensor, down_evac_dv[0])
                        sDV.wait(tensor, xsum_free[0])
                        ins = None
                        for kc in range(NKC):
                            ins = tensor.matmul(pRH[:], s_ones[:], s_x3[:, kc, :],
                                                start=(kc == 0), stop=(kc == NKC - 1))
                        sPE.inc(ins)
                    xs_pe = sPE.n

                    @block.vector
                    def _(vector, xs_pe=xs_pe):
                        sPE.wait(vector, xs_pe)
                        sDV.inc(vector.tensor_copy(s_xsum[:], pRH[:]))
                    xsum_free[0] = sDV.n
                    hs_started = [False]

                    for ib in range(FF // 512):
                        wb = ib % 2
                        for src_r, bufs, fl in ((gw_r, s_gw, gw_free),
                                                (uw_r, s_uw, uw_free)):
                            @block.gpsimd
                            def _(gpsimd, ib=ib, wb=wb, src_r=src_r, bufs=bufs, fl=fl):
                                sPE.wait(gpsimd, fl[wb])
                                for kq in range(4):
                                    sW.inc(gpsimd.dma_start(
                                        out=bufs[wb][:, kq * 4:(kq + 1) * 4, :],
                                        in_=src_r[:, kq * 4:(kq + 1) * 4,
                                                  ib * 512:(ib + 1) * 512]), 16)
                        w_ld = sW.n

                        for icl in range(4):
                            icg = ib * 4 + icl
                            par = icg % 2

                            @block.tensor
                            def _(tensor, wb=wb, icl=icl, par=par, w_ld=w_ld):
                                sW.wait(tensor, w_ld)
                                sAC.wait(tensor, pf_free[par][0])
                                sDV.wait(tensor, pf_free[par][1])
                                sDV.wait(tensor, xsum_free[0])
                                for kc in range(NKC):
                                    tensor.matmul(pF[par][:, 0:512],
                                                  s_gw[wb][:, kc, icl * 128:(icl + 1) * 128],
                                                  s_x3[:, kc, :],
                                                  start=(kc == 0), stop=False)
                                tensor.matmul(pF[par][:, 0:512], s_n128q[:], s_xsum[:],
                                              start=False, stop=True)
                                for kc in range(NKC):
                                    tensor.matmul(pF[par][:, 512:1024],
                                                  s_uw[wb][:, kc, icl * 128:(icl + 1) * 128],
                                                  s_x3[:, kc, :],
                                                  start=(kc == 0), stop=False)
                                ins = tensor.matmul(pF[par][:, 512:1024], s_n128q[:],
                                                    s_xsum[:], start=False, stop=True)
                                sPE.inc(ins)
                            mm_pe = sPE.n
                            if icl == 3:
                                gw_free[wb] = mm_pe
                                uw_free[wb] = mm_pe
                            x3_free[0] = mm_pe

                            @block.scalar
                            def _(scalar, icg=icg, par=par, mm=mm_pe):
                                sPE.wait(scalar, mm)
                                sDV.wait(scalar, t1_free[0])
                                sAC.inc(scalar.activation(
                                    s_t1[:], pF[par][:, 0:512], AF.Silu,
                                    bias=0.0, scale=s_gs[:, icg:icg + 1]))
                            ac_done = sAC.n
                            pf_free[par][0] = ac_done

                            @block.vector
                            def _(vector, icg=icg, par=par, mm=mm_pe, ac=ac_done):
                                sPE.wait(vector, mm)
                                vector.tensor_scalar(s_t2[:], pF[par][:, 512:1024],
                                                     s_us[:, icg:icg + 1], None, ALU.mult)
                                sAC.wait(vector, ac)
                                sPE.wait(vector, h_free[0])
                                sDV.inc(vector.tensor_tensor(
                                    s_h[:, icg, :], s_t1[:], s_t2[:], ALU.mult))
                            h_dv = sDV.n
                            pf_free[par][1] = h_dv
                            t1_free[0] = h_dv

                            @block.tensor
                            def _(tensor, icg=icg, h_dv=h_dv):
                                sDV.wait(tensor, h_dv)
                                sPE.inc(tensor.matmul(pRH[:], s_ones[:], s_h[:, icg, :],
                                                      start=not hs_started[0],
                                                      stop=(icg == NIC - 1)))
                                hs_started[0] = True

                    if "h0" in DBG and ft == 0:
                        @block.gpsimd
                        def _(gpsimd):
                            sDV.wait(gpsimd)
                            sW.inc(gpsimd.dma_start(out=DBG["h0"], in_=s_h[:, 0, :]), 16)

                    @block.vector
                    def _(vector):
                        sPE.wait(vector)
                        vector.tensor_copy(s_hsum[:], pRH[:])
                        vector.tensor_copy(s_junk[:], s_ones[:])
                        vector.tensor_copy(s_hsbf[:, 0:512], s_hsum[:])
                        vector.tensor_copy(s_junk[:], s_ones[:])
                        sDV.inc(vector.tensor_tensor(s_hsbf[:, 512:1024], s_hsum[:],
                                                     s_hsbf[:, 0:512], ALU.subtract))
                    hsum_dv = sDV.n

                dw_free[0] = dw_free[1] = sPE.n
                with nc.psum_tensor(f"pN0_{ft}", [128, 1024]) as pN0, \
                     nc.psum_tensor(f"pN1_{ft}", [128, 1024]) as pN1, \
                     nc.psum_tensor(f"pN2_{ft}", [128, 1024]) as pN2, \
                     nc.psum_tensor(f"pN3_{ft}", [128, 1024]) as pN3:
                    pN = [pN0, pN1, pN2, pN3]
                    pn_free = [0, 0, 0, 0]

                    for tp2 in range(nfc // 2):
                        tcl0 = tp2 * 2

                        @block.tensor
                        def _(tensor, hsum_dv=hsum_dv):
                            sDV.wait(tensor, hsum_dv)
                            sAC.wait(tensor)
                        for ic2 in range(NIC // 2):
                            dp = ic2 % 2

                            @block.gpsimd
                            def _(gpsimd, ic2=ic2, dp=dp):
                                sPE.wait(gpsimd, dw_free[dp])
                                sW.inc(gpsimd.dma_start(
                                    out=s_dw[dp][:],
                                    in_=dw_r[:, ic2 * 2:(ic2 + 1) * 2, :]), 16)
                            dw_ld = sW.n

                            @block.tensor
                            def _(tensor, ft=ft, tcl0=tcl0, ic2=ic2, dp=dp, dw_ld=dw_ld):
                                sW.wait(tensor, dw_ld)
                                if ic2 == 0:
                                    for q in range(4):
                                        sDV.wait(tensor, pn_free[q])
                                ins = None
                                for dj in range(2):
                                    ic = ic2 * 2 + dj
                                    for tl in range(2):
                                        tcl = tcl0 + tl
                                        for half in range(2):
                                            for ns in range(2):
                                                c0 = half * 1024 + ns * 512
                                                ins = tensor.matmul(
                                                    pN[tl * 2 + half][:, ns * 512:(ns + 1) * 512],
                                                    s_h[:, ic, tcl * 128:(tcl + 1) * 128],
                                                    s_dw[dp][:, dj, c0:c0 + 512],
                                                    start=(ic == 0), stop=False)
                                sPE.inc(ins)
                            dw_free[dp] = sPE.n
                            if ic2 == NIC // 2 - 1:
                                h_free[0] = sPE.n

                        @block.tensor
                        def _(tensor, ft=ft, tcl0=tcl0):
                            ins = None
                            for tl in range(2):
                                tcl = tcl0 + tl
                                for half in range(2):
                                    for ns in range(2):
                                        for hilo in range(2):
                                            ins = tensor.matmul(
                                                pN[tl * 2 + half][:, ns * 512:(ns + 1) * 512],
                                                s_hsbf[0:1, hilo * 512 + tcl * 128:
                                                       hilo * 512 + (tcl + 1) * 128],
                                                s_n128r[:],
                                                start=False,
                                                stop=(hilo == 1))
                            sPE.inc(ins)
                        dn_pe = sPE.n

                        for tl in range(2):
                            tcl = tcl0 + tl
                            tc = ft * nfc + tcl
                            xp = tc % 2
                            pp = tc % 2

                            @block.sync
                            def _(sync, tc=tc, xp=xp):
                                sDV.wait(sync, x_free4[xp])
                                sLd.inc(sync.dma_start(
                                    out=s_x[xp][:],
                                    in_=xp_r[tc * 128:(tc + 1) * 128, :]), 16)
                            x_ld = sLd.n

                            @block.vector
                            def _(vector, tl=tl, dn_pe=dn_pe):
                                sPE.wait(vector, dn_pe)
                                sLd.wait(vector, dsb_ld)
                                sAC.wait(vector)
                                for half in range(2):
                                    n = sDV.inc(vector.tensor_tensor(
                                        s_scr4[:, half * 1024:(half + 1) * 1024],
                                        pN[tl * 2 + half][:],
                                        s_dsb[:, half * 1024:(half + 1) * 1024],
                                        ALU.mult))
                                    pn_free[tl * 2 + half] = n
                            v_dv = sDV.n
                            down_evac_dv[0] = v_dv

                            b = 8 * pp

                            @block.scalar
                            def _(scalar, b=b, v_dv=v_dv, pp=pp):
                                sDV.wait(scalar, v_dv)
                                sDV.wait(scalar, o4_free[0])
                                sDV.wait(scalar, stat_free[pp])
                                scalar.activation(
                                    s_o4[:], s_scr4[:], AF.Square, bias=0.0, scale=1.0,
                                    accum_out=s_stat[:, b + 1:b + 2])
                                scalar.activation(
                                    s_o4[:], s_scr4[:], AF.Identity, bias=0.0, scale=1.0,
                                    accum_out=s_stat[:, b + 0:b + 1])
                                sAC.inc(scalar.copy(s_stat[:, b + 5:b + 7],
                                                    s_stat[:, b:b + 2]))

                            @block.vector
                            def _(vector, b=b):
                                sAC.wait(vector)
                                vector.tensor_scalar_mul(s_stat[:, b + 2:b + 3],
                                                         s_stat[:, b:b + 1], -1.0 / DM)
                                vector.tensor_scalar_mul(s_stat[:, b + 7:b + 8],
                                                         s_stat[:, b:b + 1], 1.0 / DM)
                                vector.tensor_scalar(s_stat[:, b + 3:b + 4],
                                                     s_stat[:, b + 1:b + 2],
                                                     1.0 / DM, EPS, ALU.mult, ALU.add)
                                vector.tensor_tensor(s_stat[:, b + 4:b + 5],
                                                     s_stat[:, b + 7:b + 8],
                                                     s_stat[:, b + 7:b + 8], ALU.mult)
                                vector.tensor_copy(s_junk[:], s_stat[:, b + 2:b + 3])
                                sDV.inc(vector.tensor_tensor(s_stat[:, b + 3:b + 4],
                                                             s_stat[:, b + 3:b + 4],
                                                             s_stat[:, b + 4:b + 5],
                                                             ALU.subtract))

                            @block.scalar
                            def _(scalar, b=b):
                                sDV.wait(scalar)
                                scalar.activation(s_stat[:, b + 5:b + 6],
                                                  s_stat[:, b + 3:b + 4],
                                                  AF.Ln, bias=0.0, scale=1.0)
                                sAC.inc(scalar.activation(s_stat[:, b + 6:b + 7],
                                                          s_stat[:, b + 5:b + 6],
                                                          AF.Exp, bias=0.0, scale=-0.5))

                            @block.vector
                            def _(vector, b=b, xp=xp, x_ld=x_ld, pp=pp):
                                sAC.wait(vector)
                                vector.tensor_scalar(s_o4[:], s_scr4[:],
                                                     s_stat[:, b + 2:b + 3],
                                                     s_stat[:, b + 6:b + 7],
                                                     ALU.add, ALU.mult)
                                sLd.wait(vector, x_ld)
                                sOut.wait(vector, o4_free[1])
                                n = sDV.inc(vector.tensor_tensor(
                                    s_o4[:], s_o4[:], s_x[xp][:], ALU.add))
                                x_free4[xp] = n
                                stat_free[pp] = n
                            o_dv = sDV.n
                            o4_free[0] = o_dv

                            @block.sync
                            def _(sync, tc=tc, o_dv=o_dv):
                                sDV.wait(sync, o_dv)
                                sOut.inc(sync.dma_start(
                                    out=out_d[tc * 128:(tc + 1) * 128, :],
                                    in_=s_o4[:]), 16)
                            o4_free[1] = sOut.n

            @block.sync
            def _(sync):
                sOut.wait(sync)
            if DBG:
                @block.gpsimd
                def _(gpsimd):
                    sW.wait(gpsimd)
    return nc, DBG



def _u8(a):
    a = np.ascontiguousarray(np.asarray(a))
    assert a.dtype == np.int32
    return np.ascontiguousarray(a.view("<u1")[..., 0::4])


def prep_inputs(L0_attn_x, L0_attn_w_qkv, L0_attn_w_out, L0_attn_decay_param,
                L0_ffn_gate_w, L0_ffn_gate_scale, L0_ffn_up_w, L0_ffn_up_scale,
                L0_ffn_down_w, L0_ffn_down_scale, tok=2048):
    x = np.ascontiguousarray(np.asarray(L0_attn_x), dtype=np.float32)
    B, T, _ = x.shape
    wq = np.ascontiguousarray(np.asarray(L0_attn_w_qkv), dtype=np.float32)
    wo = np.ascontiguousarray(np.asarray(L0_attn_w_out), dtype=np.float32)
    dec = np.ascontiguousarray(
        np.broadcast_to(np.asarray(L0_attn_decay_param, dtype=np.float32), (128, H)))
    gw = np.ascontiguousarray(_u8(L0_ffn_gate_w).T)
    uw = np.ascontiguousarray(_u8(L0_ffn_up_w).T)
    dw = np.ascontiguousarray(_u8(L0_ffn_down_w).T)
    gs = np.ascontiguousarray(
        np.asarray(L0_ffn_gate_scale, dtype=np.float32).reshape(NIC, 128).T)
    us = np.ascontiguousarray(
        np.asarray(L0_ffn_up_scale, dtype=np.float32).reshape(NIC, 128).T)
    dsb = np.ascontiguousarray(
        np.broadcast_to(np.asarray(L0_ffn_down_scale, dtype=np.float32), (128, DM)))
    iota = np.ascontiguousarray(
        np.broadcast_to(-np.arange(1, tok + 1, dtype=np.float32), (128, tok)))
    ident = np.eye(128, dtype=ml_dtypes.bfloat16)
    n128q = np.full((1, 128), -128.0, dtype=ml_dtypes.bfloat16)
    n128r = np.full((1, 512), -128.0, dtype=ml_dtypes.bfloat16)
    ones = np.ones((128, 1), dtype=ml_dtypes.bfloat16)
    onesr = np.ones((1, 128), dtype=ml_dtypes.bfloat16)

    halves = T // tok
    in_maps = []
    for c in range(NCORE):
        s, hf = c // halves, c % halves
        oh = np.zeros((NCORE, 1), np.float32)
        if hf > 0:
            oh[c - 1, 0] = 1.0
        in_maps.append({
            "x": np.ascontiguousarray(x[s, hf * tok:(hf + 1) * tok, :]),
            "wq": wq, "wo": wo, "dec": dec, "oh": oh,
            "gw": gw, "uw": uw, "dw": dw, "gs": gs, "us": us, "dsb": dsb,
            "iota": iota, "ident": ident, "n128q": n128q, "n128r": n128r,
            "ones": ones, "onesr": onesr,
        })
    return in_maps, (B, T)


def kernel(**inputs):
    tok = 2048
    in_maps, (B, T) = prep_inputs(tok=tok, **inputs)
    nc, _ = build_nc(tok=tok)
    res = run_bass_kernel_spmd(nc, in_maps, core_ids=list(range(NCORE)))
    halves = T // tok
    out = np.empty((B, T, DM), dtype=np.float32)
    for c in range(NCORE):
        s, hf = c // halves, c % halves
        out[s, hf * tok:(hf + 1) * tok, :] = res.results[c]["out"]
    return out


# version 03
# speedup vs baseline: 1.0795x; 1.0795x over previous
import numpy as np
import ml_dtypes
from contextlib import ExitStack

import concourse.bass as bass
import concourse.mybir as mybir
from concourse.bass_utils import run_bass_kernel_spmd

F32 = mybir.dt.float32
BF16 = mybir.dt.bfloat16
U8 = mybir.dt.uint8
AF = mybir.ActivationFunctionType
ALU = mybir.AluOpType

NCORE = 8
DM = 2048
NKC = DM // 128
H = 16
QKV = 3 * DM
FF = 8192
NIC = FF // 128
EPS = 1e-5
EPS_DEN = 1e-6


def build_nc(tok=2048, dbg=()):
    NTC = tok // 128
    nft = max(1, tok // 512)
    ftok = tok // nft
    nfc = ftok // 128
    ngrp = H // 2

    nc = bass.Bass("TRN2", target_bir_lowering=False, debug=False)

    x_d = nc.dram_tensor("x", [tok, DM], F32, kind="ExternalInput").ap()
    wq_d = nc.dram_tensor("wq", [DM, QKV], F32, kind="ExternalInput").ap()
    wo_d = nc.dram_tensor("wo", [DM, DM], F32, kind="ExternalInput").ap()
    dec_d = nc.dram_tensor("dec", [128, H], F32, kind="ExternalInput").ap()
    oh_d = nc.dram_tensor("oh", [NCORE, 1], F32, kind="ExternalInput").ap()
    gw_d = nc.dram_tensor("gw", [DM, FF], U8, kind="ExternalInput").ap()
    uw_d = nc.dram_tensor("uw", [DM, FF], U8, kind="ExternalInput").ap()
    dw_d = nc.dram_tensor("dw", [FF, DM], U8, kind="ExternalInput").ap()
    gs_d = nc.dram_tensor("gs", [128, NIC], F32, kind="ExternalInput").ap()
    us_d = nc.dram_tensor("us", [128, NIC], F32, kind="ExternalInput").ap()
    dsb_d = nc.dram_tensor("dsb", [128, DM], F32, kind="ExternalInput").ap()
    iota_d = nc.dram_tensor("iota", [128, tok], F32, kind="ExternalInput").ap()
    ident_d = nc.dram_tensor("ident", [128, 128], BF16, kind="ExternalInput").ap()
    n128q_d = nc.dram_tensor("n128q", [1, 128], BF16, kind="ExternalInput").ap()
    n128r_d = nc.dram_tensor("n128r", [1, 512], BF16, kind="ExternalInput").ap()
    ones_d = nc.dram_tensor("ones", [128, 1], BF16, kind="ExternalInput").ap()
    onesr_d = nc.dram_tensor("onesr", [1, 128], BF16, kind="ExternalInput").ap()
    out_d = nc.dram_tensor("out", [tok, DM], F32, kind="ExternalOutput").ap()

    attn_sp = nc.dram_tensor("attn_sp", [DM, tok], BF16)
    xp_sp = nc.dram_tensor("xp_sp", [tok, DM], F32)
    cc_in = [nc.dram_tensor(f"cc_in{g}", [128, 4], F32) for g in range(ngrp)]
    cc_out = [nc.dram_tensor(f"cc_out{g}", [NCORE * 128, 4], F32)
              for g in range(ngrp)]

    DBG = {}
    def dbg_t(name, shape, dt=F32):
        if name in dbg:
            DBG[name] = nc.dram_tensor("dbg_" + name, shape, dt,
                                       kind="ExternalOutput").ap()
    dbg_t("xln", [128, NKC * tok], BF16)
    dbg_t("q0", [128, tok], BF16); dbg_t("k0", [128, tok], BF16)
    dbg_t("kv0", [128, tok], BF16)
    dbg_t("y0", [128, tok], BF16); dbg_t("ys0", [128, tok], BF16)
    dbg_t("at0", [128, tok], BF16)
    dbg_t("den", [128, NTC]); dbg_t("xp", [tok, DM])
    dbg_t("qy", [128, tok], BF16)
    dbg_t("yl0", [128, tok], BF16); dbg_t("lamt", [128, tok])
    dbg_t("init", [128, 4]); dbg_t("gath", [NCORE, 512])
    dbg_t("ccin", [128, 4])
    dbg_t("wo0", [128, DM], BF16); dbg_t("att0", [128, NKC * 512], BF16)
    dbg_t("v1", [128, DM])
    dbg_t("o0", [128, DM]); dbg_t("st3", [128, 16])
    dbg_t("snap", [128, 8]); dbg_t("v1b", [128, DM])
    dbg_t("x30", [128, NKC * 512], BF16); dbg_t("h0", [128, 512], BF16)
    dbg_t("stat", [128, 16]); dbg_t("rden", [128, NTC])

    wq_r = wq_d.rearrange("(kc p) c -> p kc c", p=128)
    wo_r = wo_d.rearrange("(kc p) c -> p kc c", p=128)
    gw_r = gw_d.rearrange("(kc p) c -> p kc c", p=128)
    uw_r = uw_d.rearrange("(kc p) c -> p kc c", p=128)
    dw_r = dw_d.rearrange("(ic p) c -> p ic c", p=128)
    attn_r = attn_sp.ap().rearrange("(fc p) t -> p fc t", p=128)
    xp_r = xp_sp.ap()

    st = ExitStack()
    with st:
        def sb(stk, name, shape, dt=F32):
            return stk.enter_context(nc.sbuf_tensor(name, shape, dt))
        def psm(stk, name, shape, dt=F32):
            return stk.enter_context(nc.psum_tensor(name, shape, dt))

        class _S:
            def __init__(self, name):
                self.h = st.enter_context(nc.semaphore(name))
                self.n = 0
            def inc(self, instr, k=1):
                instr.then_inc(self.h, k); self.n += k; return self.n
            def wait(self, eng, val=None):
                v = self.n if val is None else val
                if v > 0:
                    eng.wait_ge(self.h, v)

        sLd = _S("sLd"); sW = _S("sW"); sPE = _S("sPE"); sDV = _S("sDV")
        sAC = _S("sAC"); sSp = _S("sSp"); sCc = _S("sCc"); sG = _S("sG")
        sOut = _S("sOut")

        s_x = [sb(st, f"s_x{i}", [128, DM]) for i in range(2)]
        s_xlntm = sb(st, "s_xlntm", [128, DM], BF16)
        s_stat = sb(st, "s_stat", [128, 16])
        s_dec = sb(st, "s_dec", [128, H]); s_lam = sb(st, "s_lam", [128, H])
        s_lam2 = sb(st, "s_lam2", [128, H]); s_junk = sb(st, "s_junk", [128, 1])
        s_lnl = sb(st, "s_lnl", [128, H])
        s_oh = sb(st, "s_oh", [NCORE, 1]); s_ohbf = sb(st, "s_ohbf", [NCORE, 1], BF16)
        s_ident = sb(st, "s_ident", [128, 128], BF16)
        s_ones = sb(st, "s_ones", [128, 1], BF16)
        s_onesr = sb(st, "s_onesr", [1, 128], BF16)
        s_n128q = sb(st, "s_n128q", [1, 128], BF16)
        s_n128r = sb(st, "s_n128r", [1, 512], BF16)
        s_gs = sb(st, "s_gs", [128, NIC]); s_us = sb(st, "s_us", [128, NIC])
        s_den = sb(st, "s_den", [128, NTC]); s_rden = sb(st, "s_rden", [128, NTC])

        block = st.enter_context(nc.Block())

        @block.sync
        def _(sync):
            for tile, src in ((s_dec, dec_d), (s_oh, oh_d), (s_ident, ident_d),
                              (s_ones, ones_d), (s_n128q, n128q_d),
                              (s_n128r, n128r_d), (s_gs, gs_d), (s_us, us_d),
                              (s_onesr, onesr_d)):
                sLd.inc(sync.dma_start(out=tile[:], in_=src), 16)

        @block.scalar
        def _(scalar):
            sLd.wait(scalar)
            sAC.inc(scalar.activation(s_lam[:], s_dec[:], AF.Exp, bias=0.0, scale=-1.0))

        @block.vector
        def _(vector):
            vector.memset(s_stat[:], 0.0)
            sAC.wait(vector)
            sDV.inc(vector.tensor_scalar_add(s_lam[:], s_lam[:], 1.0))
            sDV.inc(vector.tensor_copy(s_ohbf[:], s_oh[:]))

        @block.scalar
        def _(scalar):
            sDV.wait(scalar)
            scalar.activation(s_lnl[:], s_lam[:], AF.Ln, bias=0.0, scale=1.0)
            sAC.inc(scalar.activation(s_lam2[:], s_lnl[:], AF.Exp, bias=0.0, scale=-1.0))
        lam_ac = sAC.n
        lam_dv = sDV.n
        stat_free = {0: 0, 1: 0}

        def ln_stats(src_tile, scr_tile, pp, pre_wait=()):
            b = 8 * pp

            @block.scalar
            def _(scalar, src=src_tile, scr=scr_tile, b=b, pp=pp, pw=tuple(pre_wait)):
                for sem, val in pw:
                    sem.wait(scalar, val)
                sDV.wait(scalar, stat_free[pp])
                scalar.activation(scr[:], src[:], AF.Square, bias=0.0,
                                  scale=1.0, accum_out=s_stat[:, b + 1:b + 2])
                scalar.activation(scr[:], src[:], AF.Identity, bias=0.0,
                                  scale=1.0, accum_out=s_stat[:, b + 0:b + 1])
                sAC.inc(scalar.copy(s_stat[:, b + 5:b + 7], s_stat[:, b:b + 2]))

            @block.vector
            def _(vector, b=b):
                sAC.wait(vector)
                vector.tensor_scalar_mul(s_stat[:, b + 2:b + 3], s_stat[:, b:b + 1], -1.0 / DM)
                vector.tensor_scalar_mul(s_stat[:, b + 7:b + 8], s_stat[:, b:b + 1], 1.0 / DM)
                vector.tensor_scalar(s_stat[:, b + 3:b + 4], s_stat[:, b + 1:b + 2],
                                     1.0 / DM, EPS, ALU.mult, ALU.add)
                vector.tensor_tensor(s_stat[:, b + 4:b + 5], s_stat[:, b + 7:b + 8],
                                     s_stat[:, b + 7:b + 8], ALU.mult)
                vector.tensor_copy(s_junk[:], s_stat[:, b + 2:b + 3])
                sDV.inc(vector.tensor_tensor(s_stat[:, b + 3:b + 4], s_stat[:, b + 3:b + 4],
                                             s_stat[:, b + 4:b + 5], ALU.subtract))

            @block.scalar
            def _(scalar, b=b):
                sDV.wait(scalar)
                scalar.activation(s_stat[:, b + 5:b + 6], s_stat[:, b + 3:b + 4],
                                  AF.Ln, bias=0.0, scale=1.0)
                sAC.inc(scalar.activation(s_stat[:, b + 6:b + 7], s_stat[:, b + 5:b + 6],
                                          AF.Exp, bias=0.0, scale=-0.5))

        p123 = st.enter_context(ExitStack())
        s_big = sb(p123, "s_big", [128, NKC, max(tok, DM)], BF16)
        s_scr = sb(p123, "s_scr", [128, DM])

        with ExitStack() as p1:
            pTr = psm(p1, "pTr", [128, 2048], BF16)
            x_free = [0, 0]
            tr_free = [0, 0]
            xlntm_free = [0]
            for tc in range(NTC):
                par = tc % 2
                pp = tc % 2

                @block.sync
                def _(sync, tc=tc, par=par):
                    sDV.wait(sync, x_free[par])
                    sLd.inc(sync.dma_start(out=s_x[par][:],
                                           in_=x_d[tc * 128:(tc + 1) * 128, :]), 16)
                x_ld = sLd.n
                ln_stats(s_x[par], s_scr, pp, pre_wait=[(sLd, x_ld)])

                @block.vector
                def _(vector, par=par, pp=pp):
                    sAC.wait(vector)
                    sPE.wait(vector, xlntm_free[0])
                    b = 8 * pp
                    n = sDV.inc(vector.tensor_scalar(
                        s_xlntm[:], s_x[par][:], s_stat[:, b + 2:b + 3],
                        s_stat[:, b + 6:b + 7], ALU.add, ALU.mult))
                    x_free[par] = n
                    stat_free[pp] = n

                for fq in range(4):
                    tp = fq % 2

                    @block.tensor
                    def _(tensor, fq=fq, tp=tp):
                        sDV.wait(tensor)
                        sDV.wait(tensor, tr_free[tp])
                        ins = None
                        for j in range(4):
                            fc = fq * 4 + j
                            ins = tensor.transpose(
                                pTr[:, tp * 1024 + j * 128: tp * 1024 + (j + 1) * 128],
                                s_xlntm[:, fc * 128:(fc + 1) * 128], s_ident[:])
                        sPE.inc(ins)

                    @block.vector
                    def _(vector, tc=tc, fq=fq, tp=tp):
                        sPE.wait(vector)
                        n = sDV.inc(vector.tensor_copy(
                            s_big[:, fq * 4:(fq + 1) * 4, tc * 128:(tc + 1) * 128],
                            pTr[:, tp * 1024:tp * 1024 + 512]
                                .rearrange("p (a b) -> p a b", a=4)))
                        tr_free[tp] = n
                xlntm_free[0] = sPE.n
        xln_dv = sDV.n

        if "xln" in DBG:
            @block.gpsimd
            def _(gpsimd):
                sDV.wait(gpsimd)
                sW.inc(gpsimd.dma_start(
                    out=DBG["xln"].rearrange("p (a b) -> p a b", a=NKC),
                    in_=s_big[:, :, 0:tok]), 16)

        p2 = ExitStack()
        with p2:
            pQ = [psm(p2, f"pQ{i}", [128, 1024]) for i in range(2)]
            pD = psm(p2, "pD", [128, 1024])
            pR = psm(p2, "pR", [128, 1024])
            s_denr = sb(p2, "s_denr", [1, min(1024, tok)])
            s_denrb = sb(p2, "s_denrb", [1, min(1024, tok)], BF16)
            s_wqb = [sb(p2, f"s_wqb{i}", [128, NKC, 128], BF16) for i in range(4)]
            s_q = [sb(p2, f"s_q{i}", [128, tok], BF16) for i in range(4)]
            s_k = [sb(p2, f"s_k{i}", [128, tok], BF16) for i in range(2)]
            s_kv = [sb(p2, f"s_kv{i}", [128, tok], BF16) for i in range(2)]
            s_y = [sb(p2, f"s_y{i}", [128, tok], BF16) for i in range(6)]
            s_attn = sb(p2, "s_attn", [128, tok], BF16)
            s_qy = sb(p2, "s_qy", [128, tok], BF16)
            s_lamt = sb(p2, "s_lamt", [128, tok])
            s_iota = sb(p2, "s_iota", [128, tok])
            s_phr = sb(p2, "s_phr", [128, 1024], BF16)
            s_phe = sb(p2, "s_phe", [128, 1024], BF16)
            s_sloc = sb(p2, "s_sloc", [128, 4])
            s_gath = sb(p2, "s_gath", [NCORE, 512])
            s_gathbf = sb(p2, "s_gathbf", [NCORE, 512], BF16)
            s_init = [sb(p2, f"s_init{i}", [128, 4]) for i in range(2)]

            @block.sync
            def _(sync):
                sLd.inc(sync.dma_start(out=s_iota[:], in_=iota_d), 16)
            iota_ld = sLd.n

            @block.vector
            def _(vector):
                vector.memset(s_sloc[:], 0.0)

            wq_free = [0] * 4
            pq_free_dv = [0, 0]
            pq_free_ac = [0, 0]
            ph_free = [0]
            scan_dv = [0] * H
            grp_gath = [0] * ngrp
            grp_init = [0] * ngrp
            denr_free = [0]
            pr_free = [0]
            denr_ac_gate = [0]
            denrb_free = [0]
            gath_free = [0]
            sloc_free = [0]
            lamt_free = [0]
            qy_free = [0]
            attn_free = [0]
            pq_par = [0]

            def emit_strip_load(hi, tp):
                slot = (hi * 3 + tp) % 4
                cbase = tp * DM + hi * 128

                @block.gpsimd
                def _(gpsimd, slot=slot, cbase=cbase):
                    sPE.wait(gpsimd, wq_free[slot])
                    ins = None
                    for kq in range(4):
                        ins = gpsimd.dma_start(
                            out=s_wqb[slot][:, kq * 4:(kq + 1) * 4, :],
                            in_=wq_r[:, kq * 4:(kq + 1) * 4, cbase:cbase + 128])
                        sW.inc(ins, 16)
                return slot, sW.n

            nhalf = max(1, tok // 1024)
            hw = min(1024, tok)

            def emit_qkv(hi, tp, slot, wq_rdy):
                for half in range(nhalf):
                    par = pq_par[0]
                    pq = pQ[par]

                    @block.tensor
                    def _(tensor, slot=slot, wq_rdy=wq_rdy, half=half, pq=pq, par=par):
                        sW.wait(tensor, wq_rdy)
                        sDV.wait(tensor, xln_dv)
                        sAC.wait(tensor, pq_free_ac[par])
                        sDV.wait(tensor, pq_free_dv[par])
                        ins = None
                        for ns in range(hw // 512):
                            t0 = half * hw + ns * 512
                            for kc in range(NKC):
                                ins = tensor.matmul(
                                    pq[:, ns * 512:(ns + 1) * 512],
                                    s_wqb[slot][:, kc, :],
                                    s_big[:, kc, t0:t0 + 512],
                                    start=(kc == 0), stop=(kc == NKC - 1))
                        sPE.inc(ins)
                    pe_done = sPE.n

                    if tp < 2:
                        dst = s_q[hi % 4] if tp == 0 else s_k[hi % 2]

                        @block.scalar
                        def _(scalar, pq=pq, pe=pe_done):
                            sPE.wait(scalar, pe)
                            sDV.wait(scalar, ph_free[0])
                            scalar.activation(s_phr[:, 0:hw], pq[:, 0:hw], AF.Relu,
                                              bias=0.0, scale=-1.0)
                            sAC.inc(scalar.activation(s_phe[:, 0:hw], s_phr[:, 0:hw],
                                                      AF.Exp, bias=0.0, scale=-1.0))
                        ac_done = sAC.n

                        @block.vector
                        def _(vector, pq=pq, dst=dst, half=half, ac=ac_done, par=par):
                            sAC.wait(vector, ac)
                            vector.tensor_tensor(s_phr[:, 0:hw], pq[:, 0:hw],
                                                 s_phr[:, 0:hw], ALU.add)
                            n = sDV.inc(vector.tensor_tensor(
                                dst[:, half * hw:(half + 1) * hw],
                                s_phr[:, 0:hw], s_phe[:, 0:hw], ALU.add))
                            pq_free_dv[par] = n
                            ph_free[0] = n
                        pq_free_ac[par] = ac_done
                    else:
                        @block.vector
                        def _(vector, pq=pq, hi=hi, half=half, pe=pe_done, par=par):
                            sPE.wait(vector, pe)
                            n = sDV.inc(vector.tensor_tensor(
                                s_kv[hi % 2][:, half * hw:(half + 1) * hw],
                                s_k[hi % 2][:, half * hw:(half + 1) * hw],
                                pq[:, 0:hw], ALU.mult))
                            pq_free_dv[par] = n
                            pq_free_ac[par] = sAC.n
                    pq_par[0] = 1 - par

            def emit_scans(hi):
                ykv = s_y[(hi % 3) * 2]
                yks = s_y[(hi % 3) * 2 + 1]

                @block.vector
                def _(vector, hi=hi, ykv=ykv, yks=yks):
                    sAC.wait(vector, lam_ac)
                    sG.wait(vector, sloc_free[0])
                    if DBG:
                        sW.wait(vector)
                    lamc = s_lam2[:, hi:hi + 1].broadcast_to([128, tok])
                    vector.tensor_tensor_scan(ykv[:], lamc, s_kv[hi % 2][:],
                                              0.0, ALU.mult, ALU.add)
                    vector.tensor_tensor_scan(yks[:], lamc, s_k[hi % 2][:],
                                              0.0, ALU.mult, ALU.add)
                    hl = hi % 2
                    vector.tensor_copy(s_sloc[:, 2 * hl:2 * hl + 1], ykv[:, tok - 1:tok])
                    vector.tensor_copy(s_sloc[:, 2 * hl + 1:2 * hl + 2],
                                       yks[:, tok - 1:tok])
                    sDV.inc(vector.reduce_sum(s_junk[:], s_sloc[:],
                                              axis=mybir.AxisListType.X))
                scan_dv[hi] = sDV.n
                return ykv, yks

            def emit_collective(g):
                @block.gpsimd
                def _(gpsimd, g=g):
                    sDV.wait(gpsimd, scan_dv[2 * g + 1])
                    n = sG.inc(gpsimd.dma_start(out=cc_in[g].ap(), in_=s_sloc[:]), 16)
                    sloc_free[0] = n
                    sG.wait(gpsimd, n)
                    sCc.inc(gpsimd.collective_compute(
                        "AllGather", ALU.bypass,
                        replica_groups=[list(range(NCORE))],
                        ins=[cc_in[g].ap().opt()],
                        outs=[cc_out[g].ap().opt()]), 1)
                    sCc.wait(gpsimd)
                    sPE.wait(gpsimd, gath_free[0])
                    sG.inc(gpsimd.dma_start(
                        out=s_gath[:],
                        in_=cc_out[g].ap().rearrange("(r p) g -> r (p g)", p=128)), 16)
                grp_gath[g] = sG.n

            def emit_select(g):
                @block.vector
                def _(vector, g=g):
                    sG.wait(vector, grp_gath[g])
                    vector.tensor_copy(
                        s_gathbf[:].rearrange("r (g p) -> r g p", g=4),
                        s_gath[:, :].rearrange("r (p g) -> r g p", g=4))
                    vector.tensor_copy(s_junk[:], s_ones[:])
                    sDV.inc(vector.tensor_copy(s_junk[:], s_ones[:]))
                dv = sDV.n

                @block.tensor
                def _(tensor, dv=dv, g=g):
                    sDV.wait(tensor, dv)
                    if g > 0:
                        sDV.wait(tensor, grp_init[g - 1])
                    ins = None
                    for gc in range(4):
                        ins = tensor.matmul(pD[:, 512 + gc:513 + gc],
                                            s_gathbf[:, gc * 128:(gc + 1) * 128],
                                            s_ohbf[:],
                                            start=True, stop=True)
                    sPE.inc(ins)
                gath_free[0] = sPE.n

                @block.vector
                def _(vector, g=g):
                    sPE.wait(vector)
                    sDV.inc(vector.tensor_copy(s_init[g % 2][:], pD[:, 512:516]))
                grp_init[g] = sDV.n
                if g == 0 and "init" in DBG:
                    @block.gpsimd
                    def _(gpsimd):
                        sDV.wait(gpsimd)
                        sW.inc(gpsimd.dma_start(out=DBG["init"], in_=s_init[0][:]), 16)
                        sW.inc(gpsimd.dma_start(out=DBG["gath"], in_=s_gath[:]), 16)
                        if "ccin" in DBG:
                            sW.inc(gpsimd.dma_start(out=DBG["ccin"],
                                                    in_=cc_in[0].ap()), 16)

            def emit_epilogue(g, heads_y, last=False):
                for hl, (hi, ykv, yks) in enumerate(heads_y):
                    @block.scalar
                    def _(scalar, hi=hi):
                        sAC.wait(scalar, lam_ac)
                        sLd.wait(scalar, iota_ld)
                        sDV.wait(scalar, lamt_free[0])
                        sAC.inc(scalar.activation(s_lamt[:], s_iota[:], AF.Exp,
                                                  bias=0.0, scale=s_lnl[:, hi:hi + 1]))
                    ac_done = sAC.n

                    if hi == 0 and "lamt" in DBG:
                        @block.gpsimd
                        def _(gpsimd, ac=ac_done):
                            sAC.wait(gpsimd, ac)
                            sW.inc(gpsimd.dma_start(out=DBG["lamt"], in_=s_lamt[:]), 16)

                    @block.vector
                    def _(vector, g=g, hl=hl, hi=hi, ykv=ykv, yks=yks, ac=ac_done):
                        sAC.wait(vector, ac)
                        sDV.wait(vector, grp_init[g])
                        if DBG:
                            sW.wait(vector)
                        ini = s_init[g % 2]
                        vector.scalar_tensor_tensor(ykv[:], s_lamt[:],
                                                    ini[:, 2 * hl:2 * hl + 1],
                                                    ykv[:], ALU.mult, ALU.add)
                        n = sDV.inc(vector.scalar_tensor_tensor(
                            yks[:], s_lamt[:], ini[:, 2 * hl + 1:2 * hl + 2],
                            yks[:], ALU.mult, ALU.add))
                        lamt_free[0] = n
                        sPE.wait(vector, qy_free[0])
                        sDV.inc(vector.tensor_tensor(s_qy[:], s_q[hi % 4][:],
                                                     yks[:], ALU.mult))
                    qy_dv = sDV.n

                    @block.vector
                    def _(vector, hi=hi, ykv=ykv):
                        sSp.wait(vector, attn_free[0])
                        if DBG:
                            sW.wait(vector)
                        sDV.inc(vector.tensor_tensor(s_attn[:], s_q[hi % 4][:],
                                                     ykv[:], ALU.mult))
                    atm_dv = sDV.n

                    hw2 = min(1024, tok)
                    for half in range(max(1, tok // 1024)):
                        for qr in range(hw2 // 512):
                            @block.tensor
                            def _(tensor, half=half, qr=qr, qy_dv=qy_dv):
                                if half == 0 and qr == 0:
                                    sDV.wait(tensor, qy_dv)
                                sDV.wait(tensor, denr_free[0])
                                sAC.wait(tensor, denr_ac_gate[0])
                                t0 = half * hw2 + qr * 512
                                sPE.inc(tensor.matmul(pD[0:1, 0:512], s_ones[:],
                                                      s_qy[:, t0:t0 + 512],
                                                      start=True, stop=True))
                            pe_n = sPE.n

                            @block.vector
                            def _(vector, qr=qr, pe_n=pe_n):
                                sPE.wait(vector, pe_n)
                                denr_free[0] = sDV.inc(vector.tensor_copy(
                                    s_denr[0:1, qr * 512:(qr + 1) * 512],
                                    pD[0:1, 0:512]))
                        if half + 1 == max(1, tok // 1024):
                            qy_free[0] = sPE.n

                        @block.vector
                        def _(vector):
                            vector.tensor_copy(s_junk[:], s_ones[:])
                            sDV.inc(vector.tensor_scalar_max(s_denr[:], s_denr[:],
                                                             EPS_DEN))
                        dmax_dv = sDV.n

                        @block.scalar
                        def _(scalar, dmax_dv=dmax_dv):
                            sDV.wait(scalar, dmax_dv)
                            sPE.wait(scalar, denrb_free[0])
                            scalar.activation(s_denr[:], s_denr[:], AF.Ln,
                                              bias=0.0, scale=1.0)
                            scalar.activation(s_denrb[:], s_denr[:], AF.Exp,
                                              bias=0.0, scale=-1.0)
                            sAC.inc(scalar.copy(s_stat[:, 0:0 + 1] if False else
                                                s_denr[0:1, 0:1], s_denrb[0:1, 0:1]))
                        rdenr_ac = sAC.n
                        denr_ac_gate[0] = rdenr_ac

                        @block.tensor
                        def _(tensor, half=half, hw2=hw2, rdenr_ac=rdenr_ac, g=g):
                            sAC.wait(tensor, rdenr_ac)
                            sDV.wait(tensor, grp_init[g])
                            sDV.wait(tensor, pr_free[0])
                            ins = None
                            for ns in range(hw2 // 512):
                                ins = tensor.matmul(
                                    pR[:, ns * 512:(ns + 1) * 512], s_onesr[:],
                                    s_denrb[0:1, ns * 512:(ns + 1) * 512],
                                    start=True, stop=True)
                            sPE.inc(ins)
                        bc_pe = sPE.n
                        denrb_free[0] = bc_pe

                        @block.vector
                        def _(vector, half=half, hw2=hw2, bc_pe=bc_pe):
                            sPE.wait(vector, bc_pe)
                            pr_free[0] = sDV.inc(vector.tensor_tensor(
                                s_attn[:, half * hw2:(half + 1) * hw2],
                                s_attn[:, half * hw2:(half + 1) * hw2],
                                pR[:, 0:hw2], ALU.mult))
                    at_dv = sDV.n

                    if hi == 0 and "at0" in DBG:
                        @block.gpsimd
                        def _(gpsimd, at_dv=at_dv):
                            sDV.wait(gpsimd, at_dv)
                            sW.inc(gpsimd.dma_start(out=DBG["at0"], in_=s_attn[:]), 16)
                    if hi == 0 and "y0" in DBG:
                        @block.gpsimd
                        def _(gpsimd, at_dv=at_dv, ykv=ykv, yks=yks):
                            sDV.wait(gpsimd, at_dv)
                            sW.inc(gpsimd.dma_start(out=DBG["y0"], in_=ykv[:]), 16)
                            sW.inc(gpsimd.dma_start(out=DBG["ys0"], in_=yks[:]), 16)

                    @block.sync
                    def _(sync, hi=hi, at_dv=at_dv):
                        sDV.wait(sync, at_dv)
                        attn_free[0] = sSp.inc(sync.dma_start(
                            out=attn_r[:, hi, :], in_=s_attn[:]), 16)

            pend = []
            for g in range(ngrp):
                heads_y = []
                for hl in range(2):
                    hi = 2 * g + hl
                    strips = [emit_strip_load(hi, tp) for tp in range(3)]
                    for tp in range(3):
                        slot, rdy = strips[tp]
                        emit_qkv(hi, tp, slot, rdy)
                        wq_free[slot] = sPE.n
                    heads_y.append((hi,) + emit_scans(hi))

                    if hi == 0 and "yl0" in DBG:
                        @block.gpsimd
                        def _(gpsimd):
                            sDV.wait(gpsimd)
                            sW.inc(gpsimd.dma_start(out=DBG["yl0"],
                                                    in_=s_y[0][:]), 16)
                    if hi == 0:
                        for nm, tile in (("q0", s_q[0]), ("k0", s_k[0]), ("kv0", s_kv[0])):
                            if nm in DBG:
                                @block.gpsimd
                                def _(gpsimd, nm=nm, tile=tile):
                                    sDV.wait(gpsimd)
                                    sW.inc(gpsimd.dma_start(out=DBG[nm], in_=tile[:]), 16)

                    if pend and hl == 0:
                        emit_epilogue(*pend.pop(0))
                emit_collective(g)
                emit_select(g)
                pend.append((g, heads_y))
            while pend:
                pg, ph = pend.pop(0)
                emit_epilogue(pg, ph, last=not pend)

            rden_ac = sAC.n

            if "qy" in DBG:
                @block.gpsimd
                def _(gpsimd):
                    sDV.wait(gpsimd)
                    sW.inc(gpsimd.dma_start(out=DBG["qy"], in_=s_qy[:]), 16)

            attn_spilled = sSp.n
            last_qkv_pe = sPE.n

        with ExitStack() as p3:
            pW = [psm(p3, f"pW{i}", [128, 1024]) for i in range(2)]
            s_snap = sb(p3, "s_snap", [128, 8]) if "snap" in DBG else None
            s_v1b = sb(p3, "s_v1b", [128, DM]) if "snap" in DBG else None
            s_att = [sb(p3, f"s_att{i}", [128, NKC, 512], BF16) for i in range(2)]
            s_v1 = sb(p3, "s_v1", [128, DM])
            s_o = [sb(p3, f"s_o{i}", [128, DM]) for i in range(2)]

            @block.gpsimd
            def _(gpsimd):
                sPE.wait(gpsimd, last_qkv_pe)
                sW.wait(gpsimd)
                ins = None
                for kc in range(NKC):
                    ins = gpsimd.dma_start(out=s_big[:, kc:kc + 1, 0:DM],
                                           in_=wo_r[:, kc:kc + 1, :])
                    sW.inc(ins, 16)
            wo_ld = sW.n
            if "wo0" in DBG:
                @block.gpsimd
                def _(gpsimd):
                    sW.wait(gpsimd)
                    sW.inc(gpsimd.dma_start(out=DBG["wo0"], in_=s_big[:, 0, 0:DM]), 16)

            att_free = [0, 0]
            pw_free = [0, 0]
            v1_ac_free = [0]
            o_free = [0, 0]
            x_free3 = [0, 0]
            for tt in range((NTC + 3) // 4):
                ab = tt % 2

                @block.sync
                def _(sync, tt=tt, ab=ab):
                    sSp.wait(sync, attn_spilled)
                    sPE.wait(sync, att_free[ab])
                    for fq in range(4):
                        sLd.inc(sync.dma_start(
                            out=s_att[ab][:, fq * 4:(fq + 1) * 4, :],
                            in_=attn_r[:, fq * 4:(fq + 1) * 4,
                                       tt * 512:(tt + 1) * 512]), 16)
                att_ld = sLd.n
                if tt == 0 and "att0" in DBG:
                    @block.gpsimd
                    def _(gpsimd, ab=ab, att_ld=att_ld):
                        sLd.wait(gpsimd, att_ld)
                        sW.inc(gpsimd.dma_start(
                            out=DBG["att0"].rearrange("p (a b) -> p a b", a=NKC),
                            in_=s_att[ab][:, :, :]), 16)

                for tcl in range(min(4, NTC - tt * 4)):
                    tc = tt * 4 + tcl
                    xp = tc % 2

                    @block.sync
                    def _(sync, tc=tc, xp=xp):
                        sDV.wait(sync, x_free3[xp])
                        sLd.inc(sync.dma_start(out=s_x[xp][:],
                                               in_=x_d[tc * 128:(tc + 1) * 128, :]), 16)
                    x_ld = sLd.n

                    for half in range(2):
                        @block.tensor
                        def _(tensor, tcl=tcl, half=half, ab=ab, att_ld=att_ld):
                            sLd.wait(tensor, att_ld)
                            sW.wait(tensor, wo_ld)
                            sAC.wait(tensor, rden_ac)
                            sDV.wait(tensor, pw_free[half])
                            ins = None
                            for ns in range(2):
                                c0 = half * 1024 + ns * 512
                                for kc in range(NKC):
                                    ins = tensor.matmul(
                                        pW[half][:, ns * 512:(ns + 1) * 512],
                                        s_att[ab][:, kc, tcl * 128:(tcl + 1) * 128],
                                        s_big[:, kc, c0:c0 + 512],
                                        start=(kc == 0), stop=(kc == NKC - 1))
                            sPE.inc(ins)
                        pe_done = sPE.n
                        if half == 1:
                            att_free[ab] = pe_done

                        @block.vector
                        def _(vector, tc=tc, half=half, pe=pe_done):
                            sPE.wait(vector, pe)
                            sAC.wait(vector, v1_ac_free[0])
                            pw_free[half] = sDV.inc(vector.tensor_copy(
                                s_v1[:, half * 1024:(half + 1) * 1024], pW[half][:]))

                    if tc == 0 and "v1" in DBG:
                        @block.gpsimd
                        def _(gpsimd):
                            sDV.wait(gpsimd)
                            sW.inc(gpsimd.dma_start(out=DBG["v1"], in_=s_v1[:]), 16)
                    pp = tc % 2
                    ln_stats(s_v1, s_scr, pp,
                             pre_wait=[(sDV, sDV.n)] + ([(sW, sW.n)] if DBG else []))
                    v1_ac_free[0] = sAC.n
                    op = tc % 2

                    @block.vector
                    def _(vector, tc=tc, pp=pp, op=op, xp=xp, x_ld=x_ld):
                        sAC.wait(vector)
                        b = 8 * pp
                        sSp.wait(vector, o_free[op])
                        if tc == 0 and "snap" in DBG:
                            vector.tensor_copy(s_snap[:], s_stat[:, 0:8])
                            vector.tensor_copy(s_v1b[:], s_v1[:])
                        vector.tensor_scalar(s_o[op][:], s_v1[:], s_stat[:, b + 2:b + 3],
                                             s_stat[:, b + 6:b + 7], ALU.add, ALU.mult)
                        sLd.wait(vector, x_ld)
                        n = sDV.inc(vector.tensor_tensor(s_o[op][:], s_o[op][:],
                                                         s_x[xp][:], ALU.add))
                        x_free3[xp] = n
                        stat_free[pp] = n
                    o_dv = sDV.n

                    if tc == 0 and "o0" in DBG:
                        @block.gpsimd
                        def _(gpsimd, o_dv=o_dv):
                            sDV.wait(gpsimd, o_dv)
                            sW.inc(gpsimd.dma_start(out=DBG["o0"], in_=s_o[0][:]), 16)
                            if "st3" in DBG:
                                sW.inc(gpsimd.dma_start(out=DBG["st3"], in_=s_stat[:]), 16)
                            if "snap" in DBG:
                                sW.inc(gpsimd.dma_start(out=DBG["snap"], in_=s_snap[:]), 16)
                                sW.inc(gpsimd.dma_start(out=DBG["v1b"], in_=s_v1b[:]), 16)

                    @block.sync
                    def _(sync, tc=tc, op=op, o_dv=o_dv):
                        sDV.wait(sync, o_dv)
                        o_free[op] = sSp.inc(sync.dma_start(
                            out=xp_r[tc * 128:(tc + 1) * 128, :], in_=s_o[op][:]), 16)

            if "xp" in DBG:
                @block.gpsimd
                def _(gpsimd):
                    sSp.wait(gpsimd)
                    sW.inc(gpsimd.dma_start(out=DBG["xp"], in_=xp_r[:, :]), 16)
            xp_done = sSp.n
        p123.close()

        with ExitStack() as p4:
            s_x3 = sb(p4, "s_x3", [128, NKC, 512], BF16)
            s_scr4 = sb(p4, "s_scr4", [128, DM])
            s_o4 = sb(p4, "s_o4", [128, DM])
            s_wpool = sb(p4, "s_wpool", [128, 4 * NKC * 512], BF16)
            s_gw = [s_wpool[:, i * 8192:(i + 1) * 8192]
                    .rearrange("p (a b) -> p a b", a=NKC) for i in range(2)]
            s_uw = [s_wpool[:, (2 + i) * 8192:(3 + i) * 8192]
                    .rearrange("p (a b) -> p a b", a=NKC) for i in range(2)]
            s_dw = [s_wpool[:, i * 4096:(i + 1) * 4096]
                    .rearrange("p (a b) -> p a b", a=2) for i in range(2)]
            s_h = sb(p4, "s_h", [128, NIC, 512], BF16)
            s_t1 = sb(p4, "s_t1", [128, 512], BF16)
            s_t2 = sb(p4, "s_t2", [128, 512], BF16)
            s_dsb = sb(p4, "s_dsb", [128, DM])
            s_xsum = sb(p4, "s_xsum", [1, 512], BF16)
            s_hsum = sb(p4, "s_hsum", [1, 512])
            s_hsbf = sb(p4, "s_hsbf", [1, 1024], BF16)

            @block.sync
            def _(sync):
                sLd.inc(sync.dma_start(out=s_dsb[:], in_=dsb_d), 16)
            dsb_ld = sLd.n

            x_free4 = [0, 0]
            tr2_free = [0, 0]
            gw_free = [0, 0]
            uw_free = [0, 0]
            dw_free = [0, 0]
            pf_free = [[0, 0], [0, 0]]
            t1_free = [0]
            x3_free = [0]
            h_free = [0]
            scr4_dv_free = [0]
            o4_free = [0, 0]
            xsum_free = [0]
            down_evac_dv = [0]

            for ft in range(nft):
                with nc.psum_tensor(f"pTr2_{ft}", [128, 2048], BF16) as pTr2:
                    for cl in range(nfc):
                        tc = ft * nfc + cl
                        xp = tc % 2
                        pp = tc % 2

                        @block.sync
                        def _(sync, tc=tc, xp=xp):
                            sSp.wait(sync, xp_done)
                            sDV.wait(sync, x_free4[xp])
                            sLd.inc(sync.dma_start(
                                out=s_x[xp][:],
                                in_=xp_r[tc * 128:(tc + 1) * 128, :]), 16)
                        x_ld = sLd.n
                        ln_stats(s_x[xp], s_scr4, pp,
                                 pre_wait=[(sLd, x_ld), (sDV, sDV.n)])

                        @block.vector
                        def _(vector, xp=xp, pp=pp):
                            sAC.wait(vector)
                            b = 8 * pp
                            n = sDV.inc(vector.tensor_scalar(
                                s_xlntm[:], s_x[xp][:], s_stat[:, b + 2:b + 3],
                                s_stat[:, b + 6:b + 7], ALU.add, ALU.mult))
                            x_free4[xp] = n
                            stat_free[pp] = n

                        for fq in range(4):
                            tp = fq % 2

                            @block.tensor
                            def _(tensor, fq=fq, tp=tp):
                                sDV.wait(tensor)
                                sDV.wait(tensor, tr2_free[tp])
                                if fq == 0:
                                    sPE.wait(tensor, x3_free[0])
                                ins = None
                                for j in range(4):
                                    fc = fq * 4 + j
                                    ins = tensor.transpose(
                                        pTr2[:, tp * 1024 + j * 128:
                                             tp * 1024 + (j + 1) * 128],
                                        s_xlntm[:, fc * 128:(fc + 1) * 128], s_ident[:])
                                sPE.inc(ins)

                            @block.vector
                            def _(vector, cl=cl, fq=fq, tp=tp):
                                sPE.wait(vector)
                                tr2_free[tp] = sDV.inc(vector.tensor_copy(
                                    s_x3[:, fq * 4:(fq + 1) * 4, cl * 128:(cl + 1) * 128],
                                    pTr2[:, tp * 1024:tp * 1024 + 512]
                                        .rearrange("p (a b) -> p a b", a=4)))
                x3_dv = sDV.n
                if "x30" in DBG and ft == 0:
                    @block.gpsimd
                    def _(gpsimd):
                        sDV.wait(gpsimd)
                        sW.inc(gpsimd.dma_start(
                            out=DBG["x30"],
                            in_=s_x3[:, :, :].rearrange("p a b -> p (a b)")), 16)

                gw_free[0] = gw_free[1] = sPE.n
                uw_free[0] = uw_free[1] = sPE.n
                with nc.psum_tensor(f"pF0_{ft}", [128, 1024]) as pF0, \
                     nc.psum_tensor(f"pF1_{ft}", [128, 1024]) as pF1, \
                     nc.psum_tensor(f"pRH_{ft}", [1, 512]) as pRH:
                    pF = [pF0, pF1]

                    @block.tensor
                    def _(tensor, x3_dv=x3_dv):
                        sDV.wait(tensor, x3_dv)
                        sDV.wait(tensor, down_evac_dv[0])
                        sDV.wait(tensor, xsum_free[0])
                        ins = None
                        for kc in range(NKC):
                            ins = tensor.matmul(pRH[:], s_ones[:], s_x3[:, kc, :],
                                                start=(kc == 0), stop=(kc == NKC - 1))
                        sPE.inc(ins)
                    xs_pe = sPE.n

                    @block.vector
                    def _(vector, xs_pe=xs_pe):
                        sPE.wait(vector, xs_pe)
                        vector.tensor_copy(s_xsum[:], pRH[:])
                        vector.tensor_copy(s_junk[:], s_ones[:])
                        sDV.inc(vector.tensor_copy(s_junk[:], s_ones[:]))
                    xsum_free[0] = sDV.n
                    hs_started = [False]

                    for ib in range(FF // 512):
                        wb = ib % 2
                        for src_r, bufs, fl in ((gw_r, s_gw, gw_free),
                                                (uw_r, s_uw, uw_free)):
                            @block.gpsimd
                            def _(gpsimd, ib=ib, wb=wb, src_r=src_r, bufs=bufs, fl=fl):
                                sPE.wait(gpsimd, fl[wb])
                                for kq in range(4):
                                    sW.inc(gpsimd.dma_start(
                                        out=bufs[wb][:, kq * 4:(kq + 1) * 4, :],
                                        in_=src_r[:, kq * 4:(kq + 1) * 4,
                                                  ib * 512:(ib + 1) * 512]), 16)
                        w_ld = sW.n

                        for icl in range(4):
                            icg = ib * 4 + icl
                            par = icg % 2

                            @block.tensor
                            def _(tensor, wb=wb, icl=icl, par=par, w_ld=w_ld):
                                sW.wait(tensor, w_ld)
                                sAC.wait(tensor, pf_free[par][0])
                                sDV.wait(tensor, pf_free[par][1])
                                sDV.wait(tensor, xsum_free[0])
                                for kc in range(NKC):
                                    tensor.matmul(pF[par][:, 0:512],
                                                  s_gw[wb][:, kc, icl * 128:(icl + 1) * 128],
                                                  s_x3[:, kc, :],
                                                  start=(kc == 0), stop=False)
                                tensor.matmul(pF[par][:, 0:512], s_n128q[:], s_xsum[:],
                                              start=False, stop=True)
                                for kc in range(NKC):
                                    tensor.matmul(pF[par][:, 512:1024],
                                                  s_uw[wb][:, kc, icl * 128:(icl + 1) * 128],
                                                  s_x3[:, kc, :],
                                                  start=(kc == 0), stop=False)
                                ins = tensor.matmul(pF[par][:, 512:1024], s_n128q[:],
                                                    s_xsum[:], start=False, stop=True)
                                sPE.inc(ins)
                            mm_pe = sPE.n
                            if icl == 3:
                                gw_free[wb] = mm_pe
                                uw_free[wb] = mm_pe
                            x3_free[0] = mm_pe

                            @block.scalar
                            def _(scalar, icg=icg, par=par, mm=mm_pe):
                                sPE.wait(scalar, mm)
                                sDV.wait(scalar, t1_free[0])
                                sAC.inc(scalar.activation(
                                    s_t1[:], pF[par][:, 0:512], AF.Silu,
                                    bias=0.0, scale=s_gs[:, icg:icg + 1]))
                            ac_done = sAC.n
                            pf_free[par][0] = ac_done

                            @block.vector
                            def _(vector, icg=icg, par=par, mm=mm_pe, ac=ac_done):
                                sPE.wait(vector, mm)
                                vector.tensor_scalar(s_t2[:], pF[par][:, 512:1024],
                                                     s_us[:, icg:icg + 1], None, ALU.mult)
                                sAC.wait(vector, ac)
                                sPE.wait(vector, h_free[0])
                                sDV.inc(vector.tensor_tensor(
                                    s_h[:, icg, :], s_t1[:], s_t2[:], ALU.mult))
                            h_dv = sDV.n
                            pf_free[par][1] = h_dv
                            t1_free[0] = h_dv

                            @block.tensor
                            def _(tensor, icg=icg, h_dv=h_dv):
                                sDV.wait(tensor, h_dv)
                                sPE.inc(tensor.matmul(pRH[:], s_ones[:], s_h[:, icg, :],
                                                      start=not hs_started[0],
                                                      stop=(icg == NIC - 1)))
                                hs_started[0] = True

                    if "h0" in DBG and ft == 0:
                        @block.gpsimd
                        def _(gpsimd):
                            sDV.wait(gpsimd)
                            sW.inc(gpsimd.dma_start(out=DBG["h0"], in_=s_h[:, 0, :]), 16)

                    @block.vector
                    def _(vector):
                        sPE.wait(vector)
                        vector.tensor_copy(s_hsum[:], pRH[:])
                        vector.tensor_copy(s_junk[:], s_ones[:])
                        vector.tensor_copy(s_hsbf[:, 0:512], s_hsum[:])
                        vector.tensor_copy(s_junk[:], s_ones[:])
                        vector.tensor_tensor(s_hsbf[:, 512:1024], s_hsum[:],
                                             s_hsbf[:, 0:512], ALU.subtract)
                        vector.tensor_copy(s_junk[:], s_ones[:])
                        sDV.inc(vector.tensor_copy(s_junk[:], s_ones[:]))
                    hsum_dv = sDV.n

                dw_free[0] = dw_free[1] = sPE.n
                with nc.psum_tensor(f"pN0_{ft}", [128, 1024]) as pN0, \
                     nc.psum_tensor(f"pN1_{ft}", [128, 1024]) as pN1, \
                     nc.psum_tensor(f"pN2_{ft}", [128, 1024]) as pN2, \
                     nc.psum_tensor(f"pN3_{ft}", [128, 1024]) as pN3:
                    pN = [pN0, pN1, pN2, pN3]
                    pn_free = [0, 0, 0, 0]

                    for tp2 in range(nfc // 2):
                        tcl0 = tp2 * 2

                        @block.tensor
                        def _(tensor, hsum_dv=hsum_dv):
                            sDV.wait(tensor, hsum_dv)
                            sAC.wait(tensor)
                        for ic2 in range(NIC // 2):
                            dp = ic2 % 2

                            @block.gpsimd
                            def _(gpsimd, ic2=ic2, dp=dp):
                                sPE.wait(gpsimd, dw_free[dp])
                                sW.inc(gpsimd.dma_start(
                                    out=s_dw[dp][:],
                                    in_=dw_r[:, ic2 * 2:(ic2 + 1) * 2, :]), 16)
                            dw_ld = sW.n

                            @block.tensor
                            def _(tensor, ft=ft, tcl0=tcl0, ic2=ic2, dp=dp, dw_ld=dw_ld):
                                sW.wait(tensor, dw_ld)
                                if ic2 == 0:
                                    for q in range(4):
                                        sDV.wait(tensor, pn_free[q])
                                ins = None
                                for dj in range(2):
                                    ic = ic2 * 2 + dj
                                    for tl in range(2):
                                        tcl = tcl0 + tl
                                        for half in range(2):
                                            for ns in range(2):
                                                c0 = half * 1024 + ns * 512
                                                ins = tensor.matmul(
                                                    pN[tl * 2 + half][:, ns * 512:(ns + 1) * 512],
                                                    s_h[:, ic, tcl * 128:(tcl + 1) * 128],
                                                    s_dw[dp][:, dj, c0:c0 + 512],
                                                    start=(ic == 0), stop=False)
                                sPE.inc(ins)
                            dw_free[dp] = sPE.n
                            if ic2 == NIC // 2 - 1:
                                h_free[0] = sPE.n

                        @block.tensor
                        def _(tensor, ft=ft, tcl0=tcl0):
                            ins = None
                            for tl in range(2):
                                tcl = tcl0 + tl
                                for half in range(2):
                                    for ns in range(2):
                                        for hilo in range(2):
                                            ins = tensor.matmul(
                                                pN[tl * 2 + half][:, ns * 512:(ns + 1) * 512],
                                                s_hsbf[0:1, hilo * 512 + tcl * 128:
                                                       hilo * 512 + (tcl + 1) * 128],
                                                s_n128r[:],
                                                start=False,
                                                stop=(hilo == 1))
                            sPE.inc(ins)
                        dn_pe = sPE.n

                        for tl in range(2):
                            tcl = tcl0 + tl
                            tc = ft * nfc + tcl
                            xp = tc % 2
                            pp = tc % 2

                            @block.sync
                            def _(sync, tc=tc, xp=xp):
                                sDV.wait(sync, x_free4[xp])
                                sLd.inc(sync.dma_start(
                                    out=s_x[xp][:],
                                    in_=xp_r[tc * 128:(tc + 1) * 128, :]), 16)
                            x_ld = sLd.n

                            @block.vector
                            def _(vector, tl=tl, dn_pe=dn_pe):
                                sPE.wait(vector, dn_pe)
                                sLd.wait(vector, dsb_ld)
                                sAC.wait(vector)
                                for half in range(2):
                                    n = sDV.inc(vector.tensor_tensor(
                                        s_scr4[:, half * 1024:(half + 1) * 1024],
                                        pN[tl * 2 + half][:],
                                        s_dsb[:, half * 1024:(half + 1) * 1024],
                                        ALU.mult))
                                    pn_free[tl * 2 + half] = n
                            v_dv = sDV.n
                            down_evac_dv[0] = v_dv

                            b = 8 * pp

                            @block.scalar
                            def _(scalar, b=b, v_dv=v_dv, pp=pp):
                                sDV.wait(scalar, v_dv)
                                sDV.wait(scalar, o4_free[0])
                                sDV.wait(scalar, stat_free[pp])
                                scalar.activation(
                                    s_o4[:], s_scr4[:], AF.Square, bias=0.0, scale=1.0,
                                    accum_out=s_stat[:, b + 1:b + 2])
                                scalar.activation(
                                    s_o4[:], s_scr4[:], AF.Identity, bias=0.0, scale=1.0,
                                    accum_out=s_stat[:, b + 0:b + 1])
                                sAC.inc(scalar.copy(s_stat[:, b + 5:b + 7],
                                                    s_stat[:, b:b + 2]))

                            @block.vector
                            def _(vector, b=b):
                                sAC.wait(vector)
                                vector.tensor_scalar_mul(s_stat[:, b + 2:b + 3],
                                                         s_stat[:, b:b + 1], -1.0 / DM)
                                vector.tensor_scalar_mul(s_stat[:, b + 7:b + 8],
                                                         s_stat[:, b:b + 1], 1.0 / DM)
                                vector.tensor_scalar(s_stat[:, b + 3:b + 4],
                                                     s_stat[:, b + 1:b + 2],
                                                     1.0 / DM, EPS, ALU.mult, ALU.add)
                                vector.tensor_tensor(s_stat[:, b + 4:b + 5],
                                                     s_stat[:, b + 7:b + 8],
                                                     s_stat[:, b + 7:b + 8], ALU.mult)
                                vector.tensor_copy(s_junk[:], s_stat[:, b + 2:b + 3])
                                sDV.inc(vector.tensor_tensor(s_stat[:, b + 3:b + 4],
                                                             s_stat[:, b + 3:b + 4],
                                                             s_stat[:, b + 4:b + 5],
                                                             ALU.subtract))

                            @block.scalar
                            def _(scalar, b=b):
                                sDV.wait(scalar)
                                scalar.activation(s_stat[:, b + 5:b + 6],
                                                  s_stat[:, b + 3:b + 4],
                                                  AF.Ln, bias=0.0, scale=1.0)
                                sAC.inc(scalar.activation(s_stat[:, b + 6:b + 7],
                                                          s_stat[:, b + 5:b + 6],
                                                          AF.Exp, bias=0.0, scale=-0.5))

                            @block.vector
                            def _(vector, b=b, xp=xp, x_ld=x_ld, pp=pp):
                                sAC.wait(vector)
                                vector.tensor_scalar(s_o4[:], s_scr4[:],
                                                     s_stat[:, b + 2:b + 3],
                                                     s_stat[:, b + 6:b + 7],
                                                     ALU.add, ALU.mult)
                                sLd.wait(vector, x_ld)
                                sOut.wait(vector, o4_free[1])
                                n = sDV.inc(vector.tensor_tensor(
                                    s_o4[:], s_o4[:], s_x[xp][:], ALU.add))
                                x_free4[xp] = n
                                stat_free[pp] = n
                            o_dv = sDV.n
                            o4_free[0] = o_dv

                            @block.sync
                            def _(sync, tc=tc, o_dv=o_dv):
                                sDV.wait(sync, o_dv)
                                sOut.inc(sync.dma_start(
                                    out=out_d[tc * 128:(tc + 1) * 128, :],
                                    in_=s_o4[:]), 16)
                            o4_free[1] = sOut.n

            @block.sync
            def _(sync):
                sOut.wait(sync)
            if DBG:
                @block.gpsimd
                def _(gpsimd):
                    sW.wait(gpsimd)
    return nc, DBG



def _u8(a):
    a = np.ascontiguousarray(np.asarray(a))
    assert a.dtype == np.int32
    return np.ascontiguousarray(a.view("<u1")[..., 0::4])


def prep_inputs(L0_attn_x, L0_attn_w_qkv, L0_attn_w_out, L0_attn_decay_param,
                L0_ffn_gate_w, L0_ffn_gate_scale, L0_ffn_up_w, L0_ffn_up_scale,
                L0_ffn_down_w, L0_ffn_down_scale, tok=2048):
    x = np.ascontiguousarray(np.asarray(L0_attn_x), dtype=np.float32)
    B, T, _ = x.shape
    wq = np.ascontiguousarray(np.asarray(L0_attn_w_qkv), dtype=np.float32)
    wo = np.ascontiguousarray(np.asarray(L0_attn_w_out), dtype=np.float32)
    dec = np.ascontiguousarray(
        np.broadcast_to(np.asarray(L0_attn_decay_param, dtype=np.float32), (128, H)))
    gw = np.ascontiguousarray(_u8(L0_ffn_gate_w).T)
    uw = np.ascontiguousarray(_u8(L0_ffn_up_w).T)
    dw = np.ascontiguousarray(_u8(L0_ffn_down_w).T)
    gs = np.ascontiguousarray(
        np.asarray(L0_ffn_gate_scale, dtype=np.float32).reshape(NIC, 128).T)
    us = np.ascontiguousarray(
        np.asarray(L0_ffn_up_scale, dtype=np.float32).reshape(NIC, 128).T)
    dsb = np.ascontiguousarray(
        np.broadcast_to(np.asarray(L0_ffn_down_scale, dtype=np.float32), (128, DM)))
    iota = np.ascontiguousarray(
        np.broadcast_to(-np.arange(1, tok + 1, dtype=np.float32), (128, tok)))
    ident = np.eye(128, dtype=ml_dtypes.bfloat16)
    n128q = np.full((1, 128), -128.0, dtype=ml_dtypes.bfloat16)
    n128r = np.full((1, 512), -128.0, dtype=ml_dtypes.bfloat16)
    ones = np.ones((128, 1), dtype=ml_dtypes.bfloat16)
    onesr = np.ones((1, 128), dtype=ml_dtypes.bfloat16)

    halves = T // tok
    in_maps = []
    for c in range(NCORE):
        s, hf = c // halves, c % halves
        oh = np.zeros((NCORE, 1), np.float32)
        if hf > 0:
            oh[c - 1, 0] = 1.0
        in_maps.append({
            "x": np.ascontiguousarray(x[s, hf * tok:(hf + 1) * tok, :]),
            "wq": wq, "wo": wo, "dec": dec, "oh": oh,
            "gw": gw, "uw": uw, "dw": dw, "gs": gs, "us": us, "dsb": dsb,
            "iota": iota, "ident": ident, "n128q": n128q, "n128r": n128r,
            "ones": ones, "onesr": onesr,
        })
    return in_maps, (B, T)


def kernel(**inputs):
    tok = 2048
    in_maps, (B, T) = prep_inputs(tok=tok, **inputs)
    nc, _ = build_nc(tok=tok)
    res = run_bass_kernel_spmd(nc, in_maps, core_ids=list(range(NCORE)))
    halves = T // tok
    out = np.empty((B, T, DM), dtype=np.float32)
    for c in range(NCORE):
        s, hf = c // halves, c % halves
        out[s, hf * tok:(hf + 1) * tok, :] = res.results[c]["out"]
    return out


# version 05
# speedup vs baseline: 1.0912x; 1.0108x over previous
import numpy as np
import ml_dtypes
from contextlib import ExitStack

import concourse.bass as bass
import concourse.mybir as mybir
from concourse.bass_utils import run_bass_kernel_spmd

F32 = mybir.dt.float32
BF16 = mybir.dt.bfloat16
U8 = mybir.dt.uint8
AF = mybir.ActivationFunctionType
ALU = mybir.AluOpType

NCORE = 8
DM = 2048
NKC = DM // 128
H = 16
QKV = 3 * DM
FF = 8192
NIC = FF // 128
EPS = 1e-5
EPS_DEN = 1e-6


def build_nc(tok=2048, dbg=()):
    NTC = tok // 128
    nft = max(1, tok // 512)
    ftok = tok // nft
    nfc = ftok // 128
    ngrp = H // 2

    nc = bass.Bass("TRN2", target_bir_lowering=False, debug=False)

    x_d = nc.dram_tensor("x", [tok, DM], F32, kind="ExternalInput").ap()
    wq_d = nc.dram_tensor("wq", [DM, QKV], F32, kind="ExternalInput").ap()
    wo_d = nc.dram_tensor("wo", [DM, DM], F32, kind="ExternalInput").ap()
    dec_d = nc.dram_tensor("dec", [128, H], F32, kind="ExternalInput").ap()
    oh_d = nc.dram_tensor("oh", [NCORE, 1], F32, kind="ExternalInput").ap()
    gw_d = nc.dram_tensor("gw", [DM, FF], U8, kind="ExternalInput").ap()
    uw_d = nc.dram_tensor("uw", [DM, FF], U8, kind="ExternalInput").ap()
    dw_d = nc.dram_tensor("dw", [FF, DM], U8, kind="ExternalInput").ap()
    gs_d = nc.dram_tensor("gs", [128, NIC], F32, kind="ExternalInput").ap()
    us_d = nc.dram_tensor("us", [128, NIC], F32, kind="ExternalInput").ap()
    dsb_d = nc.dram_tensor("dsb", [128, DM], F32, kind="ExternalInput").ap()
    iota_d = nc.dram_tensor("iota", [128, tok], mybir.dt.float16, kind="ExternalInput").ap()
    ident_d = nc.dram_tensor("ident", [128, 128], BF16, kind="ExternalInput").ap()
    n128q_d = nc.dram_tensor("n128q", [1, 128], BF16, kind="ExternalInput").ap()
    n128r_d = nc.dram_tensor("n128r", [1, 512], BF16, kind="ExternalInput").ap()
    ones_d = nc.dram_tensor("ones", [128, 1], BF16, kind="ExternalInput").ap()
    onesr_d = nc.dram_tensor("onesr", [1, 128], BF16, kind="ExternalInput").ap()
    out_d = nc.dram_tensor("out", [tok, DM], F32, kind="ExternalOutput").ap()

    attn_sp = nc.dram_tensor("attn_sp", [DM, tok], BF16)
    xp_sp = nc.dram_tensor("xp_sp", [tok, DM], F32)
    cc_in = [nc.dram_tensor(f"cc_in{g}", [128, 4], F32) for g in range(ngrp)]
    cc_out = [nc.dram_tensor(f"cc_out{g}", [NCORE * 128, 4], F32)
              for g in range(ngrp)]

    DBG = {}
    def dbg_t(name, shape, dt=F32):
        if name in dbg:
            DBG[name] = nc.dram_tensor("dbg_" + name, shape, dt,
                                       kind="ExternalOutput").ap()
    dbg_t("xln", [128, NKC * tok], BF16)
    dbg_t("q0", [128, tok], BF16); dbg_t("k0", [128, tok], BF16)
    dbg_t("kv0", [128, tok], BF16)
    dbg_t("y0", [128, tok], BF16); dbg_t("ys0", [128, tok], BF16)
    dbg_t("at0", [128, tok], BF16)
    dbg_t("den", [128, NTC]); dbg_t("xp", [tok, DM])
    dbg_t("qy", [128, tok], BF16)
    dbg_t("yl0", [128, tok], BF16); dbg_t("lamt", [128, tok])
    dbg_t("init", [128, 4]); dbg_t("gath", [NCORE, 512])
    dbg_t("ccin", [128, 4])
    dbg_t("wo0", [128, DM], BF16); dbg_t("att0", [128, NKC * 512], BF16)
    dbg_t("v1", [128, DM])
    dbg_t("o0", [128, DM]); dbg_t("st3", [128, 16])
    dbg_t("snap", [128, 8]); dbg_t("v1b", [128, DM])
    dbg_t("x30", [128, NKC * 512], BF16); dbg_t("h0", [128, 512], BF16)
    dbg_t("stat", [128, 16]); dbg_t("rden", [128, NTC])

    wq_r = wq_d.rearrange("(kc p) c -> p kc c", p=128)
    wo_r = wo_d.rearrange("(kc p) c -> p kc c", p=128)
    gw_r = gw_d.rearrange("(kc p) c -> p kc c", p=128)
    uw_r = uw_d.rearrange("(kc p) c -> p kc c", p=128)
    dw_r = dw_d.rearrange("(ic p) c -> p ic c", p=128)
    attn_r = attn_sp.ap().rearrange("(fc p) t -> p fc t", p=128)
    xp_r = xp_sp.ap()

    st = ExitStack()
    with st:
        def sb(stk, name, shape, dt=F32):
            return stk.enter_context(nc.sbuf_tensor(name, shape, dt))
        def psm(stk, name, shape, dt=F32):
            return stk.enter_context(nc.psum_tensor(name, shape, dt))

        class _S:
            def __init__(self, name):
                self.h = st.enter_context(nc.semaphore(name))
                self.n = 0
            def inc(self, instr, k=1):
                instr.then_inc(self.h, k); self.n += k; return self.n
            def wait(self, eng, val=None):
                v = self.n if val is None else val
                if v > 0:
                    eng.wait_ge(self.h, v)

        sLd = _S("sLd"); sW = _S("sW"); sPE = _S("sPE"); sDV = _S("sDV")
        sAC = _S("sAC"); sSp = _S("sSp"); sCc = _S("sCc"); sG = _S("sG")
        sOut = _S("sOut")

        s_x = [sb(st, f"s_x{i}", [128, DM]) for i in range(2)]
        s_xlntm = sb(st, "s_xlntm", [128, DM], BF16)
        s_stat = sb(st, "s_stat", [128, 16])
        s_dec = sb(st, "s_dec", [128, H]); s_lam = sb(st, "s_lam", [128, H])
        s_lam2 = sb(st, "s_lam2", [128, H]); s_junk = sb(st, "s_junk", [128, 1])
        s_lnl = sb(st, "s_lnl", [128, H])
        s_oh = sb(st, "s_oh", [NCORE, 1]); s_ohbf = sb(st, "s_ohbf", [NCORE, 1], BF16)
        s_ident = sb(st, "s_ident", [128, 128], BF16)
        s_ones = sb(st, "s_ones", [128, 1], BF16)
        s_onesr = sb(st, "s_onesr", [1, 128], BF16)
        s_n128q = sb(st, "s_n128q", [1, 128], BF16)
        s_n128r = sb(st, "s_n128r", [1, 512], BF16)
        s_gs = sb(st, "s_gs", [128, NIC]); s_us = sb(st, "s_us", [128, NIC])
        s_den = sb(st, "s_den", [128, NTC]); s_rden = sb(st, "s_rden", [128, NTC])

        block = st.enter_context(nc.Block())

        @block.sync
        def _(sync):
            for tile, src in ((s_dec, dec_d), (s_oh, oh_d), (s_ident, ident_d),
                              (s_ones, ones_d), (s_n128q, n128q_d),
                              (s_n128r, n128r_d), (s_gs, gs_d), (s_us, us_d),
                              (s_onesr, onesr_d)):
                sLd.inc(sync.dma_start(out=tile[:], in_=src), 16)

        @block.scalar
        def _(scalar):
            sLd.wait(scalar)
            sAC.inc(scalar.activation(s_lam[:], s_dec[:], AF.Exp, bias=0.0, scale=-1.0))

        @block.vector
        def _(vector):
            vector.memset(s_stat[:], 0.0)
            sAC.wait(vector)
            sDV.inc(vector.tensor_scalar_add(s_lam[:], s_lam[:], 1.0))
            sDV.inc(vector.tensor_copy(s_ohbf[:], s_oh[:]))

        @block.scalar
        def _(scalar):
            sDV.wait(scalar)
            scalar.activation(s_lnl[:], s_lam[:], AF.Ln, bias=0.0, scale=1.0)
            sAC.inc(scalar.activation(s_lam2[:], s_lnl[:], AF.Exp, bias=0.0, scale=-1.0))
        lam_ac = sAC.n
        lam_dv = sDV.n
        stat_free = {0: 0, 1: 0}

        def ln_stats(src_tile, scr_tile, pp, pre_wait=()):
            b = 8 * pp

            @block.scalar
            def _(scalar, src=src_tile, scr=scr_tile, b=b, pp=pp, pw=tuple(pre_wait)):
                for sem, val in pw:
                    sem.wait(scalar, val)
                sDV.wait(scalar, stat_free[pp])
                scalar.activation(scr[:], src[:], AF.Square, bias=0.0,
                                  scale=1.0, accum_out=s_stat[:, b + 1:b + 2])
                scalar.activation(scr[:], src[:], AF.Identity, bias=0.0,
                                  scale=1.0, accum_out=s_stat[:, b + 0:b + 1])
                sAC.inc(scalar.copy(s_stat[:, b + 5:b + 7], s_stat[:, b:b + 2]))

            @block.vector
            def _(vector, b=b):
                sAC.wait(vector)
                vector.tensor_scalar_mul(s_stat[:, b + 2:b + 3], s_stat[:, b:b + 1], -1.0 / DM)
                vector.tensor_scalar_mul(s_stat[:, b + 7:b + 8], s_stat[:, b:b + 1], 1.0 / DM)
                vector.tensor_scalar(s_stat[:, b + 3:b + 4], s_stat[:, b + 1:b + 2],
                                     1.0 / DM, EPS, ALU.mult, ALU.add)
                vector.tensor_tensor(s_stat[:, b + 4:b + 5], s_stat[:, b + 7:b + 8],
                                     s_stat[:, b + 7:b + 8], ALU.mult)
                vector.tensor_copy(s_junk[:], s_stat[:, b + 2:b + 3])
                sDV.inc(vector.tensor_tensor(s_stat[:, b + 3:b + 4], s_stat[:, b + 3:b + 4],
                                             s_stat[:, b + 4:b + 5], ALU.subtract))

            @block.scalar
            def _(scalar, b=b):
                sDV.wait(scalar)
                scalar.activation(s_stat[:, b + 5:b + 6], s_stat[:, b + 3:b + 4],
                                  AF.Ln, bias=0.0, scale=1.0)
                sAC.inc(scalar.activation(s_stat[:, b + 6:b + 7], s_stat[:, b + 5:b + 6],
                                          AF.Exp, bias=0.0, scale=-0.5))

        p123 = st.enter_context(ExitStack())
        s_big = sb(p123, "s_big", [128, NKC, max(tok, DM)], BF16)
        s_scr = sb(p123, "s_scr", [128, DM])

        with ExitStack() as p1:
            pTr = psm(p1, "pTr", [128, 2048], BF16)
            x_free = [0, 0]
            tr_free = [0, 0]
            xlntm_free = [0]
            for tc in range(NTC):
                par = tc % 2
                pp = tc % 2

                @block.sync
                def _(sync, tc=tc, par=par):
                    sDV.wait(sync, x_free[par])
                    sLd.inc(sync.dma_start(out=s_x[par][:],
                                           in_=x_d[tc * 128:(tc + 1) * 128, :]), 16)
                x_ld = sLd.n
                ln_stats(s_x[par], s_scr, pp, pre_wait=[(sLd, x_ld)])

                @block.vector
                def _(vector, par=par, pp=pp):
                    sAC.wait(vector)
                    sPE.wait(vector, xlntm_free[0])
                    b = 8 * pp
                    n = sDV.inc(vector.tensor_scalar(
                        s_xlntm[:], s_x[par][:], s_stat[:, b + 2:b + 3],
                        s_stat[:, b + 6:b + 7], ALU.add, ALU.mult))
                    x_free[par] = n
                    stat_free[pp] = n

                for fq in range(4):
                    tp = fq % 2

                    @block.tensor
                    def _(tensor, fq=fq, tp=tp):
                        sDV.wait(tensor)
                        sDV.wait(tensor, tr_free[tp])
                        ins = None
                        for j in range(4):
                            fc = fq * 4 + j
                            ins = tensor.transpose(
                                pTr[:, tp * 1024 + j * 128: tp * 1024 + (j + 1) * 128],
                                s_xlntm[:, fc * 128:(fc + 1) * 128], s_ident[:])
                        sPE.inc(ins)

                    @block.vector
                    def _(vector, tc=tc, fq=fq, tp=tp):
                        sPE.wait(vector)
                        n = sDV.inc(vector.tensor_copy(
                            s_big[:, fq * 4:(fq + 1) * 4, tc * 128:(tc + 1) * 128],
                            pTr[:, tp * 1024:tp * 1024 + 512]
                                .rearrange("p (a b) -> p a b", a=4)))
                        tr_free[tp] = n
                xlntm_free[0] = sPE.n
        xln_dv = sDV.n

        if "xln" in DBG:
            @block.gpsimd
            def _(gpsimd):
                sDV.wait(gpsimd)
                sW.inc(gpsimd.dma_start(
                    out=DBG["xln"].rearrange("p (a b) -> p a b", a=NKC),
                    in_=s_big[:, :, 0:tok]), 16)

        p2 = ExitStack()
        with p2:
            pQ = [psm(p2, f"pQ{i}", [128, 1024]) for i in range(2)]
            pD = psm(p2, "pD", [128, 1024])
            pR = psm(p2, "pR", [128, 1024])
            s_denr2 = sb(p2, "s_denr2", [1, min(1024, tok)])
            s_denrb2 = sb(p2, "s_denrb2", [1, tok], BF16)
            s_wqb = [sb(p2, f"s_wqb{i}", [128, NKC, 128], BF16) for i in range(4)]
            s_q = [sb(p2, f"s_q{i}", [128, tok], BF16) for i in range(4)]
            s_k = [sb(p2, f"s_k{i}", [128, tok], BF16) for i in range(2)]
            s_kv = [sb(p2, f"s_kv{i}", [128, tok], BF16) for i in range(2)]
            s_y = [sb(p2, f"s_y{i}", [128, tok], BF16) for i in range(6)]
            s_attn2 = [sb(p2, f"s_attn{i}", [128, tok], BF16) for i in range(2)]
            s_qy = sb(p2, "s_qy", [128, tok], BF16)
            s_lamt = sb(p2, "s_lamt", [128, tok])
            s_iota = sb(p2, "s_iota", [128, tok], mybir.dt.float16)
            s_phr = sb(p2, "s_phr", [128, 512], BF16)
            s_phe = sb(p2, "s_phe", [128, 512], BF16)
            s_sloc = sb(p2, "s_sloc", [128, 4])
            s_gath = sb(p2, "s_gath", [NCORE, 512])
            s_gathbf = sb(p2, "s_gathbf", [NCORE, 512], BF16)
            s_init = [sb(p2, f"s_init{i}", [128, 4]) for i in range(2)]

            @block.sync
            def _(sync):
                sLd.inc(sync.dma_start(out=s_iota[:], in_=iota_d), 16)
            iota_ld = sLd.n

            @block.vector
            def _(vector):
                vector.memset(s_sloc[:], 0.0)

            wq_free = [0] * 4
            pq_free_dv = [0, 0]
            pq_free_ac = [0, 0]
            ph_free = [0]
            scan_dv = [0] * H
            grp_gath = [0] * ngrp
            grp_init = [0] * ngrp
            denr_free = [0, 0]
            pr_free = [0]
            denr_ac_gate = [0]
            denrb_free = [0]
            gath_free = [0]
            sloc_free = [0]
            lamt_free = [0]
            qy_free = [0]
            attn_free = [0, 0]
            stage_ds = []
            pq_par = [0]

            def emit_strip_load(hi, tp):
                slot = (hi * 3 + tp) % 4
                cbase = tp * DM + hi * 128

                @block.gpsimd
                def _(gpsimd, slot=slot, cbase=cbase):
                    sPE.wait(gpsimd, wq_free[slot])
                    ins = None
                    for kq in range(4):
                        ins = gpsimd.dma_start(
                            out=s_wqb[slot][:, kq * 4:(kq + 1) * 4, :],
                            in_=wq_r[:, kq * 4:(kq + 1) * 4, cbase:cbase + 128])
                        sW.inc(ins, 16)
                return slot, sW.n

            nhalf = max(1, tok // 1024)
            hw = min(1024, tok)

            def emit_qkv(hi, tp, slot, wq_rdy):
                for half in range(nhalf):
                    par = pq_par[0]
                    pq = pQ[par]

                    @block.tensor
                    def _(tensor, slot=slot, wq_rdy=wq_rdy, half=half, pq=pq, par=par):
                        sW.wait(tensor, wq_rdy)
                        sDV.wait(tensor, xln_dv)
                        sAC.wait(tensor, pq_free_ac[par])
                        sDV.wait(tensor, pq_free_dv[par])
                        ins = None
                        for ns in range(hw // 512):
                            t0 = half * hw + ns * 512
                            for kc in range(NKC):
                                ins = tensor.matmul(
                                    pq[:, ns * 512:(ns + 1) * 512],
                                    s_wqb[slot][:, kc, :],
                                    s_big[:, kc, t0:t0 + 512],
                                    start=(kc == 0), stop=(kc == NKC - 1))
                        sPE.inc(ins)
                    pe_done = sPE.n

                    if tp < 2:
                        dst = s_q[hi % 4] if tp == 0 else s_k[hi % 2]

                        for sub in range(hw // 512):
                            s0 = sub * 512

                            @block.scalar
                            def _(scalar, pq=pq, pe=pe_done, s0=s0):
                                sPE.wait(scalar, pe)
                                sDV.wait(scalar, ph_free[0])
                                scalar.activation(s_phr[:], pq[:, s0:s0 + 512],
                                                  AF.Relu, bias=0.0, scale=-1.0)
                                sAC.inc(scalar.activation(s_phe[:], s_phr[:],
                                                          AF.Exp, bias=0.0, scale=-1.0))
                            ac_done = sAC.n

                            @block.vector
                            def _(vector, pq=pq, dst=dst, half=half, ac=ac_done,
                                  par=par, s0=s0):
                                sAC.wait(vector, ac)
                                vector.tensor_tensor(s_phr[:], pq[:, s0:s0 + 512],
                                                     s_phr[:], ALU.add)
                                n = sDV.inc(vector.tensor_tensor(
                                    dst[:, half * hw + s0:half * hw + s0 + 512],
                                    s_phr[:], s_phe[:], ALU.add))
                                pq_free_dv[par] = n
                                ph_free[0] = n
                            pq_free_ac[par] = ac_done
                    else:
                        @block.vector
                        def _(vector, pq=pq, hi=hi, half=half, pe=pe_done, par=par):
                            sPE.wait(vector, pe)
                            n = sDV.inc(vector.tensor_tensor(
                                s_kv[hi % 2][:, half * hw:(half + 1) * hw],
                                s_k[hi % 2][:, half * hw:(half + 1) * hw],
                                pq[:, 0:hw], ALU.mult))
                            pq_free_dv[par] = n
                            pq_free_ac[par] = sAC.n
                    pq_par[0] = 1 - par

            def emit_scans(hi):
                ykv = s_y[(hi % 3) * 2]
                yks = s_y[(hi % 3) * 2 + 1]

                @block.vector
                def _(vector, hi=hi, ykv=ykv, yks=yks):
                    sAC.wait(vector, lam_ac)
                    sG.wait(vector, sloc_free[0])
                    if DBG:
                        sW.wait(vector)
                    lamc = s_lam2[:, hi:hi + 1].broadcast_to([128, tok])
                    vector.tensor_tensor_scan(ykv[:], lamc, s_kv[hi % 2][:],
                                              0.0, ALU.mult, ALU.add)
                    vector.tensor_tensor_scan(yks[:], lamc, s_k[hi % 2][:],
                                              0.0, ALU.mult, ALU.add)
                    hl = hi % 2
                    vector.tensor_copy(s_sloc[:, 2 * hl:2 * hl + 1], ykv[:, tok - 1:tok])
                    vector.tensor_copy(s_sloc[:, 2 * hl + 1:2 * hl + 2],
                                       yks[:, tok - 1:tok])
                    sDV.inc(vector.reduce_sum(s_junk[:], s_sloc[:],
                                              axis=mybir.AxisListType.X))
                scan_dv[hi] = sDV.n
                return ykv, yks

            def emit_collective(g):
                @block.gpsimd
                def _(gpsimd, g=g):
                    sDV.wait(gpsimd, scan_dv[2 * g + 1])
                    n = sG.inc(gpsimd.dma_start(out=cc_in[g].ap(), in_=s_sloc[:]), 16)
                    sloc_free[0] = n
                    sG.wait(gpsimd, n)
                    sCc.inc(gpsimd.collective_compute(
                        "AllGather", ALU.bypass,
                        replica_groups=[list(range(NCORE))],
                        ins=[cc_in[g].ap().opt()],
                        outs=[cc_out[g].ap().opt()]), 1)
                    sCc.wait(gpsimd)
                    sPE.wait(gpsimd, gath_free[0])
                    sG.inc(gpsimd.dma_start(
                        out=s_gath[:],
                        in_=cc_out[g].ap().rearrange("(r p) g -> r (p g)", p=128)), 16)
                grp_gath[g] = sG.n

            def emit_select(g):
                @block.vector
                def _(vector, g=g):
                    sG.wait(vector, grp_gath[g])
                    vector.tensor_copy(
                        s_gathbf[:].rearrange("r (g p) -> r g p", g=4),
                        s_gath[:, :].rearrange("r (p g) -> r g p", g=4))
                    vector.tensor_copy(s_junk[:], s_ones[:])
                    sDV.inc(vector.tensor_copy(s_junk[:], s_ones[:]))
                dv = sDV.n

                @block.tensor
                def _(tensor, dv=dv, g=g):
                    sDV.wait(tensor, dv)
                    if g > 0:
                        sDV.wait(tensor, grp_init[g - 1])
                    ins = None
                    for gc in range(4):
                        ins = tensor.matmul(pD[:, 512 + gc:513 + gc],
                                            s_gathbf[:, gc * 128:(gc + 1) * 128],
                                            s_ohbf[:],
                                            start=True, stop=True)
                    sPE.inc(ins)
                gath_free[0] = sPE.n

                @block.vector
                def _(vector, g=g):
                    sPE.wait(vector)
                    sDV.inc(vector.tensor_copy(s_init[g % 2][:], pD[:, 512:516]))
                grp_init[g] = sDV.n
                if g == 0 and "init" in DBG:
                    @block.gpsimd
                    def _(gpsimd):
                        sDV.wait(gpsimd)
                        sW.inc(gpsimd.dma_start(out=DBG["init"], in_=s_init[0][:]), 16)
                        if "ccin" in DBG:
                            sW.inc(gpsimd.dma_start(out=DBG["ccin"],
                                                    in_=cc_in[0].ap()), 16)

            def emit_epilogue(g, heads_y, last=False):
                stage_ds.clear()
                for hl, (hi, ykv, yks) in enumerate(heads_y):
                    @block.scalar
                    def _(scalar, hi=hi):
                        sAC.wait(scalar, lam_ac)
                        sLd.wait(scalar, iota_ld)
                        sDV.wait(scalar, lamt_free[0])
                        sAC.inc(scalar.activation(s_lamt[:], s_iota[:], AF.Exp,
                                                  bias=0.0, scale=s_lnl[:, hi:hi + 1]))
                    ac_done = sAC.n

                    if hi == 0 and "lamt" in DBG:
                        @block.gpsimd
                        def _(gpsimd, ac=ac_done):
                            sAC.wait(gpsimd, ac)
                            sW.inc(gpsimd.dma_start(out=DBG["lamt"], in_=s_lamt[:]), 16)

                    @block.vector
                    def _(vector, g=g, hl=hl, hi=hi, ykv=ykv, yks=yks, ac=ac_done):
                        sAC.wait(vector, ac)
                        sDV.wait(vector, grp_init[g])
                        if DBG:
                            sW.wait(vector)
                        ini = s_init[g % 2]
                        vector.scalar_tensor_tensor(ykv[:], s_lamt[:],
                                                    ini[:, 2 * hl:2 * hl + 1],
                                                    ykv[:], ALU.mult, ALU.add)
                        n = sDV.inc(vector.scalar_tensor_tensor(
                            yks[:], s_lamt[:], ini[:, 2 * hl + 1:2 * hl + 2],
                            yks[:], ALU.mult, ALU.add))
                        lamt_free[0] = n
                        sPE.wait(vector, qy_free[0])
                        sDV.inc(vector.tensor_tensor(s_qy[:], s_q[hi % 4][:],
                                                     yks[:], ALU.mult))
                    qy_dv = sDV.n

                    @block.vector
                    def _(vector, hi=hi, ykv=ykv):
                        sSp.wait(vector, attn_free[hi % 2])
                        if DBG:
                            sW.wait(vector)
                        sDV.inc(vector.tensor_tensor(s_attn2[hi % 2][:], s_q[hi % 4][:],
                                                     ykv[:], ALU.mult))
                    atm_dv = sDV.n

                    if stage_ds:
                        for f in stage_ds:
                            f()
                        stage_ds.clear()

                    hw2 = min(1024, tok)
                    for dh in range(max(1, tok // 1024)):
                        for ql in range(hw2 // 512):
                            qr = dh * 2 + ql
                            dbank = pD if ql % 2 == 0 else pR

                            @block.tensor
                            def _(tensor, qr=qr, ql=ql, qy_dv=qy_dv, dbank=dbank):
                                if qr == 0:
                                    sDV.wait(tensor, qy_dv)
                                sDV.wait(tensor, denr_free[ql % 2])
                                if ql % 2 == 1:
                                    sDV.wait(tensor, pr_free[0])
                                sPE.inc(tensor.matmul(dbank[0:1, 0:512], s_ones[:],
                                                      s_qy[:, qr * 512:(qr + 1) * 512],
                                                      start=True, stop=True))
                            pe_n = sPE.n

                            @block.vector
                            def _(vector, ql=ql, pe_n=pe_n, dbank=dbank):
                                sPE.wait(vector, pe_n)
                                sAC.wait(vector, denr_ac_gate[0])
                                denr_free[ql % 2] = sDV.inc(vector.tensor_copy(
                                    s_denr2[0:1, ql * 512:(ql + 1) * 512],
                                    dbank[0:1, 0:512]))
                        if dh + 1 == max(1, tok // 1024):
                            qy_free[0] = sPE.n

                        @block.vector
                        def _(vector):
                            vector.tensor_copy(s_junk[:], s_ones[:])
                            sDV.inc(vector.tensor_scalar_max(s_denr2[:], s_denr2[:],
                                                             EPS_DEN))
                        dmax_dv = sDV.n

                        @block.scalar
                        def _(scalar, dh=dh, dmax_dv=dmax_dv):
                            sDV.wait(scalar, dmax_dv)
                            sPE.wait(scalar, denrb_free[0])
                            scalar.activation(s_denr2[:], s_denr2[:], AF.Ln,
                                              bias=0.0, scale=1.0)
                            scalar.activation(s_denrb2[0:1, dh * hw2:(dh + 1) * hw2],
                                              s_denr2[:], AF.Exp,
                                              bias=0.0, scale=-1.0)
                            sAC.inc(scalar.copy(s_denr2[0:1, 0:1],
                                                s_denrb2[0:1, dh * hw2:dh * hw2 + 1]))
                        rdenr_ac = sAC.n
                        denr_ac_gate[0] = rdenr_ac

                    def stage_d(g=g, hi=hi, rdenr_ac=rdenr_ac, at_base=None):
                        for half in range(max(1, tok // 1024)):
                            @block.tensor
                            def _(tensor, half=half, rdenr_ac=rdenr_ac, g=g):
                                sAC.wait(tensor, rdenr_ac)
                                sDV.wait(tensor, grp_init[g])
                                sDV.wait(tensor, pr_free[0])
                                ins = None
                                for ns in range(hw2 // 512):
                                    ins = tensor.matmul(
                                        pR[:, ns * 512:(ns + 1) * 512], s_onesr[:],
                                        s_denrb2[0:1, half * hw2 + ns * 512:
                                                 half * hw2 + (ns + 1) * 512],
                                        start=True, stop=True)
                                sPE.inc(ins)
                            bc_pe = sPE.n
                            denrb_free[0] = bc_pe

                            @block.vector
                            def _(vector, half=half, bc_pe=bc_pe, hi=hi):
                                sPE.wait(vector, bc_pe)
                                pr_free[0] = sDV.inc(vector.tensor_tensor(
                                    s_attn2[hi % 2][:, half * hw2:(half + 1) * hw2],
                                    s_attn2[hi % 2][:, half * hw2:(half + 1) * hw2],
                                    pR[:, 0:hw2], ALU.mult))
                        at_dv = sDV.n

                        if hi == 0 and "at0" in DBG:
                            @block.gpsimd
                            def _(gpsimd, at_dv=at_dv, hi=hi):
                                sDV.wait(gpsimd, at_dv)
                                sW.inc(gpsimd.dma_start(out=DBG["at0"],
                                                        in_=s_attn2[hi % 2][:]), 16)
                        if hi == 0 and "y0" in DBG:
                            @block.gpsimd
                            def _(gpsimd, at_dv=at_dv):
                                sDV.wait(gpsimd, at_dv)
                                pass

                        @block.sync
                        def _(sync, hi=hi, at_dv=at_dv):
                            sDV.wait(sync, at_dv)
                            attn_free[hi % 2] = sSp.inc(sync.dma_start(
                                out=attn_r[:, hi, :], in_=s_attn2[hi % 2][:]), 16)
                    stage_ds.append(stage_d)
                for f in stage_ds:
                    f()
                stage_ds.clear()

            pend = []
            for g in range(ngrp):
                heads_y = []
                for hl in range(2):
                    hi = 2 * g + hl
                    strips = [emit_strip_load(hi, tp) for tp in range(3)]
                    for tp in range(3):
                        slot, rdy = strips[tp]
                        emit_qkv(hi, tp, slot, rdy)
                        wq_free[slot] = sPE.n
                    heads_y.append((hi,) + emit_scans(hi))

                    if hi == 0 and "yl0" in DBG:
                        @block.gpsimd
                        def _(gpsimd):
                            sDV.wait(gpsimd)
                            sW.inc(gpsimd.dma_start(out=DBG["yl0"],
                                                    in_=s_y[0][:]), 16)
                    if hi == 0:
                        for nm, tile in (("q0", s_q[0]), ("k0", s_k[0]), ("kv0", s_kv[0])):
                            if nm in DBG:
                                @block.gpsimd
                                def _(gpsimd, nm=nm, tile=tile):
                                    sDV.wait(gpsimd)
                                    sW.inc(gpsimd.dma_start(out=DBG[nm], in_=tile[:]), 16)

                    if pend and hl == 0:
                        emit_epilogue(*pend.pop(0))
                emit_collective(g)
                emit_select(g)
                pend.append((g, heads_y))
            while pend:
                pg, ph = pend.pop(0)
                emit_epilogue(pg, ph, last=not pend)

            rden_ac = sAC.n

            if "qy" in DBG:
                @block.gpsimd
                def _(gpsimd):
                    sDV.wait(gpsimd)
                    sW.inc(gpsimd.dma_start(out=DBG["qy"], in_=s_qy[:]), 16)

            attn_spilled = sSp.n
            last_qkv_pe = sPE.n

        with ExitStack() as p3:
            pW = [psm(p3, f"pW{i}", [128, 1024]) for i in range(2)]
            s_snap = sb(p3, "s_snap", [128, 8]) if "snap" in DBG else None
            s_v1b = sb(p3, "s_v1b", [128, DM]) if "snap" in DBG else None
            s_att = [sb(p3, f"s_att{i}", [128, NKC, 512], BF16) for i in range(2)]
            s_v1p = [sb(p3, f"s_v1{i}", [128, DM]) for i in range(2)]
            s_o = [sb(p3, f"s_o{i}", [128, DM]) for i in range(2)]

            @block.gpsimd
            def _(gpsimd):
                sPE.wait(gpsimd, last_qkv_pe)
                sW.wait(gpsimd)
                ins = None
                for kc in range(NKC):
                    ins = gpsimd.dma_start(out=s_big[:, kc:kc + 1, 0:DM],
                                           in_=wo_r[:, kc:kc + 1, :])
                    sW.inc(ins, 16)
            wo_ld = sW.n
            if "wo0" in DBG:
                @block.gpsimd
                def _(gpsimd):
                    sW.wait(gpsimd)
                    sW.inc(gpsimd.dma_start(out=DBG["wo0"], in_=s_big[:, 0, 0:DM]), 16)

            att_free = [0, 0]
            pw_free = [0, 0]
            v1_ac_free = [0, 0]
            o_free = [0, 0]
            x_free3 = [0, 0]
            for tt in range((NTC + 3) // 4):
                ab = tt % 2

                @block.sync
                def _(sync, tt=tt, ab=ab):
                    sSp.wait(sync, attn_spilled)
                    sPE.wait(sync, att_free[ab])
                    for fq in range(4):
                        sLd.inc(sync.dma_start(
                            out=s_att[ab][:, fq * 4:(fq + 1) * 4, :],
                            in_=attn_r[:, fq * 4:(fq + 1) * 4,
                                       tt * 512:(tt + 1) * 512]), 16)
                att_ld = sLd.n
                if tt == 0 and "att0" in DBG:
                    @block.gpsimd
                    def _(gpsimd, ab=ab, att_ld=att_ld):
                        sLd.wait(gpsimd, att_ld)
                        sW.inc(gpsimd.dma_start(
                            out=DBG["att0"].rearrange("p (a b) -> p a b", a=NKC),
                            in_=s_att[ab][:, :, :]), 16)

                for tcl in range(min(4, NTC - tt * 4)):
                    tc = tt * 4 + tcl
                    xp = tc % 2

                    @block.sync
                    def _(sync, tc=tc, xp=xp):
                        sDV.wait(sync, x_free3[xp])
                        sLd.inc(sync.dma_start(out=s_x[xp][:],
                                               in_=x_d[tc * 128:(tc + 1) * 128, :]), 16)
                    x_ld = sLd.n

                    for half in range(2):
                        @block.tensor
                        def _(tensor, tcl=tcl, half=half, ab=ab, att_ld=att_ld):
                            sLd.wait(tensor, att_ld)
                            sW.wait(tensor, wo_ld)
                            sAC.wait(tensor, rden_ac)
                            sDV.wait(tensor, pw_free[half])
                            ins = None
                            for ns in range(2):
                                c0 = half * 1024 + ns * 512
                                for kc in range(NKC):
                                    ins = tensor.matmul(
                                        pW[half][:, ns * 512:(ns + 1) * 512],
                                        s_att[ab][:, kc, tcl * 128:(tcl + 1) * 128],
                                        s_big[:, kc, c0:c0 + 512],
                                        start=(kc == 0), stop=(kc == NKC - 1))
                            sPE.inc(ins)
                        pe_done = sPE.n
                        if half == 1:
                            att_free[ab] = pe_done

                        @block.vector
                        def _(vector, tc=tc, half=half, pe=pe_done):
                            sPE.wait(vector, pe)
                            sAC.wait(vector, v1_ac_free[tc % 2])
                            pw_free[half] = sDV.inc(vector.tensor_copy(
                                s_v1p[tc % 2][:, half * 1024:(half + 1) * 1024],
                                pW[half][:]))

                    if tc == 0 and "v1" in DBG:
                        @block.gpsimd
                        def _(gpsimd):
                            sDV.wait(gpsimd)
                            sW.inc(gpsimd.dma_start(out=DBG["v1"], in_=s_v1p[0][:]), 16)
                    pp = tc % 2
                    ln_stats(s_v1p[tc % 2], s_scr, pp,
                             pre_wait=[(sDV, sDV.n)] + ([(sW, sW.n)] if DBG else []))
                    v1_ac_free[tc % 2] = sAC.n
                    op = tc % 2

                    @block.vector
                    def _(vector, tc=tc, pp=pp, op=op, xp=xp, x_ld=x_ld):
                        sAC.wait(vector)
                        b = 8 * pp
                        sSp.wait(vector, o_free[op])
                        if tc == 0 and "snap" in DBG:
                            vector.tensor_copy(s_snap[:], s_stat[:, 0:8])
                            vector.tensor_copy(s_v1b[:], s_v1p[0][:])
                        vector.tensor_scalar(s_o[op][:], s_v1p[tc % 2][:],
                                             s_stat[:, b + 2:b + 3],
                                             s_stat[:, b + 6:b + 7], ALU.add, ALU.mult)
                        sLd.wait(vector, x_ld)
                        n = sDV.inc(vector.tensor_tensor(s_o[op][:], s_o[op][:],
                                                         s_x[xp][:], ALU.add))
                        x_free3[xp] = n
                        stat_free[pp] = n
                    o_dv = sDV.n

                    if tc == 0 and "o0" in DBG:
                        @block.gpsimd
                        def _(gpsimd, o_dv=o_dv):
                            sDV.wait(gpsimd, o_dv)
                            sW.inc(gpsimd.dma_start(out=DBG["o0"], in_=s_o[0][:]), 16)
                            if "st3" in DBG:
                                sW.inc(gpsimd.dma_start(out=DBG["st3"], in_=s_stat[:]), 16)
                            if "snap" in DBG:
                                sW.inc(gpsimd.dma_start(out=DBG["snap"], in_=s_snap[:]), 16)
                                sW.inc(gpsimd.dma_start(out=DBG["v1b"], in_=s_v1b[:]), 16)

                    @block.sync
                    def _(sync, tc=tc, op=op, o_dv=o_dv):
                        sDV.wait(sync, o_dv)
                        o_free[op] = sSp.inc(sync.dma_start(
                            out=xp_r[tc * 128:(tc + 1) * 128, :], in_=s_o[op][:]), 16)

            if "xp" in DBG:
                @block.gpsimd
                def _(gpsimd):
                    sSp.wait(gpsimd)
                    sW.inc(gpsimd.dma_start(out=DBG["xp"], in_=xp_r[:, :]), 16)
            xp_done = sSp.n
        p123.close()

        with ExitStack() as p4:
            s_x3 = sb(p4, "s_x3", [128, NKC, 512], BF16)
            s_scr4 = sb(p4, "s_scr4", [128, DM])
            s_v14 = sb(p4, "s_v14", [128, DM])
            s_o4 = sb(p4, "s_o4", [128, DM])
            s_wpool = sb(p4, "s_wpool", [128, 4 * NKC * 512], BF16)
            s_gw = [s_wpool[:, i * 8192:(i + 1) * 8192]
                    .rearrange("p (a b) -> p a b", a=NKC) for i in range(2)]
            s_uw = [s_wpool[:, (2 + i) * 8192:(3 + i) * 8192]
                    .rearrange("p (a b) -> p a b", a=NKC) for i in range(2)]
            s_dw = [s_wpool[:, i * 4096:(i + 1) * 4096]
                    .rearrange("p (a b) -> p a b", a=2) for i in range(2)]
            s_h = sb(p4, "s_h", [128, NIC, 512], BF16)
            s_t1 = sb(p4, "s_t1", [128, 512], BF16)
            s_t2 = sb(p4, "s_t2", [128, 512], BF16)
            s_dsb = sb(p4, "s_dsb", [128, DM])
            s_xsum = sb(p4, "s_xsum", [1, 512], BF16)
            s_hsum = sb(p4, "s_hsum", [1, 512])
            s_hsbf = sb(p4, "s_hsbf", [1, 1024], BF16)

            @block.sync
            def _(sync):
                sLd.inc(sync.dma_start(out=s_dsb[:], in_=dsb_d), 16)
            dsb_ld = sLd.n

            x_free4 = [0, 0]
            tr2_free = [0, 0]
            gw_free = [0, 0]
            uw_free = [0, 0]
            dw_free = [0, 0]
            pf_free = [[0, 0], [0, 0]]
            t1_free = [0]
            x3_free = [0]
            h_free = [0]
            scr4_dv_free = [0]
            o4_free = [0, 0]
            xsum_free = [0]
            down_evac_dv = [0]
            v14_ac_free = [0]

            for ft in range(nft):
                with nc.psum_tensor(f"pTr2_{ft}", [128, 2048], BF16) as pTr2:
                    for cl in range(nfc):
                        tc = ft * nfc + cl
                        xp = tc % 2
                        pp = tc % 2

                        @block.sync
                        def _(sync, tc=tc, xp=xp):
                            sSp.wait(sync, xp_done)
                            sDV.wait(sync, x_free4[xp])
                            sLd.inc(sync.dma_start(
                                out=s_x[xp][:],
                                in_=xp_r[tc * 128:(tc + 1) * 128, :]), 16)
                        x_ld = sLd.n
                        ln_stats(s_x[xp], s_scr4, pp,
                                 pre_wait=[(sLd, x_ld), (sDV, sDV.n)])

                        @block.vector
                        def _(vector, xp=xp, pp=pp):
                            sAC.wait(vector)
                            b = 8 * pp
                            n = sDV.inc(vector.tensor_scalar(
                                s_xlntm[:], s_x[xp][:], s_stat[:, b + 2:b + 3],
                                s_stat[:, b + 6:b + 7], ALU.add, ALU.mult))
                            x_free4[xp] = n
                            stat_free[pp] = n

                        for fq in range(4):
                            tp = fq % 2

                            @block.tensor
                            def _(tensor, fq=fq, tp=tp):
                                sDV.wait(tensor)
                                sDV.wait(tensor, tr2_free[tp])
                                if fq == 0:
                                    sPE.wait(tensor, x3_free[0])
                                ins = None
                                for j in range(4):
                                    fc = fq * 4 + j
                                    ins = tensor.transpose(
                                        pTr2[:, tp * 1024 + j * 128:
                                             tp * 1024 + (j + 1) * 128],
                                        s_xlntm[:, fc * 128:(fc + 1) * 128], s_ident[:])
                                sPE.inc(ins)

                            @block.vector
                            def _(vector, cl=cl, fq=fq, tp=tp):
                                sPE.wait(vector)
                                tr2_free[tp] = sDV.inc(vector.tensor_copy(
                                    s_x3[:, fq * 4:(fq + 1) * 4, cl * 128:(cl + 1) * 128],
                                    pTr2[:, tp * 1024:tp * 1024 + 512]
                                        .rearrange("p (a b) -> p a b", a=4)))
                x3_dv = sDV.n
                if "x30" in DBG and ft == 0:
                    @block.gpsimd
                    def _(gpsimd):
                        sDV.wait(gpsimd)
                        sW.inc(gpsimd.dma_start(
                            out=DBG["x30"],
                            in_=s_x3[:, :, :].rearrange("p a b -> p (a b)")), 16)

                gw_free[0] = gw_free[1] = sPE.n
                uw_free[0] = uw_free[1] = sPE.n
                with nc.psum_tensor(f"pF0_{ft}", [128, 1024]) as pF0, \
                     nc.psum_tensor(f"pF1_{ft}", [128, 1024]) as pF1, \
                     nc.psum_tensor(f"pRH_{ft}", [1, 512]) as pRH:
                    pF = [pF0, pF1]

                    @block.tensor
                    def _(tensor, x3_dv=x3_dv):
                        sDV.wait(tensor, x3_dv)
                        sDV.wait(tensor, down_evac_dv[0])
                        sDV.wait(tensor, xsum_free[0])
                        ins = None
                        for kc in range(NKC):
                            ins = tensor.matmul(pRH[:], s_ones[:], s_x3[:, kc, :],
                                                start=(kc == 0), stop=(kc == NKC - 1))
                        sPE.inc(ins)
                    xs_pe = sPE.n

                    @block.vector
                    def _(vector, xs_pe=xs_pe):
                        sPE.wait(vector, xs_pe)
                        vector.tensor_copy(s_xsum[:], pRH[:])
                        vector.tensor_copy(s_junk[:], s_ones[:])
                        sDV.inc(vector.tensor_copy(s_junk[:], s_ones[:]))
                    xsum_free[0] = sDV.n
                    hs_started = [False]

                    for ib in range(FF // 512):
                        wb = ib % 2
                        for src_r, bufs, fl in ((gw_r, s_gw, gw_free),
                                                (uw_r, s_uw, uw_free)):
                            @block.gpsimd
                            def _(gpsimd, ib=ib, wb=wb, src_r=src_r, bufs=bufs, fl=fl):
                                sPE.wait(gpsimd, fl[wb])
                                for kq in range(4):
                                    sW.inc(gpsimd.dma_start(
                                        out=bufs[wb][:, kq * 4:(kq + 1) * 4, :],
                                        in_=src_r[:, kq * 4:(kq + 1) * 4,
                                                  ib * 512:(ib + 1) * 512]), 16)
                        w_ld = sW.n

                        for icl in range(4):
                            icg = ib * 4 + icl
                            par = icg % 2

                            @block.tensor
                            def _(tensor, wb=wb, icl=icl, par=par, w_ld=w_ld):
                                sW.wait(tensor, w_ld)
                                sAC.wait(tensor, pf_free[par][0])
                                sDV.wait(tensor, pf_free[par][1])
                                sDV.wait(tensor, xsum_free[0])
                                for kc in range(NKC):
                                    tensor.matmul(pF[par][:, 0:512],
                                                  s_gw[wb][:, kc, icl * 128:(icl + 1) * 128],
                                                  s_x3[:, kc, :],
                                                  start=(kc == 0), stop=False)
                                tensor.matmul(pF[par][:, 0:512], s_n128q[:], s_xsum[:],
                                              start=False, stop=True)
                                for kc in range(NKC):
                                    tensor.matmul(pF[par][:, 512:1024],
                                                  s_uw[wb][:, kc, icl * 128:(icl + 1) * 128],
                                                  s_x3[:, kc, :],
                                                  start=(kc == 0), stop=False)
                                ins = tensor.matmul(pF[par][:, 512:1024], s_n128q[:],
                                                    s_xsum[:], start=False, stop=True)
                                sPE.inc(ins)
                            mm_pe = sPE.n
                            if icl == 3:
                                gw_free[wb] = mm_pe
                                uw_free[wb] = mm_pe
                            x3_free[0] = mm_pe

                            @block.scalar
                            def _(scalar, icg=icg, par=par, mm=mm_pe):
                                sPE.wait(scalar, mm)
                                sDV.wait(scalar, t1_free[0])
                                sAC.inc(scalar.activation(
                                    s_t1[:], pF[par][:, 0:512], AF.Silu,
                                    bias=0.0, scale=s_gs[:, icg:icg + 1]))
                            ac_done = sAC.n
                            pf_free[par][0] = ac_done

                            @block.vector
                            def _(vector, icg=icg, par=par, mm=mm_pe, ac=ac_done):
                                sPE.wait(vector, mm)
                                vector.tensor_scalar(s_t2[:], pF[par][:, 512:1024],
                                                     s_us[:, icg:icg + 1], None, ALU.mult)
                                sAC.wait(vector, ac)
                                sPE.wait(vector, h_free[0])
                                sDV.inc(vector.tensor_tensor(
                                    s_h[:, icg, :], s_t1[:], s_t2[:], ALU.mult))
                            h_dv = sDV.n
                            pf_free[par][1] = h_dv
                            t1_free[0] = h_dv

                            @block.tensor
                            def _(tensor, icg=icg, h_dv=h_dv):
                                sDV.wait(tensor, h_dv)
                                sPE.inc(tensor.matmul(pRH[:], s_ones[:], s_h[:, icg, :],
                                                      start=not hs_started[0],
                                                      stop=(icg == NIC - 1)))
                                hs_started[0] = True

                    if "h0" in DBG and ft == 0:
                        @block.gpsimd
                        def _(gpsimd):
                            sDV.wait(gpsimd)
                            sW.inc(gpsimd.dma_start(out=DBG["h0"], in_=s_h[:, 0, :]), 16)

                    @block.vector
                    def _(vector):
                        sPE.wait(vector)
                        vector.tensor_copy(s_hsum[:], pRH[:])
                        vector.tensor_copy(s_junk[:], s_ones[:])
                        vector.tensor_copy(s_hsbf[:, 0:512], s_hsum[:])
                        vector.tensor_copy(s_junk[:], s_ones[:])
                        vector.tensor_tensor(s_hsbf[:, 512:1024], s_hsum[:],
                                             s_hsbf[:, 0:512], ALU.subtract)
                        vector.tensor_copy(s_junk[:], s_ones[:])
                        sDV.inc(vector.tensor_copy(s_junk[:], s_ones[:]))
                    hsum_dv = sDV.n

                dw_free[0] = dw_free[1] = sPE.n
                with nc.psum_tensor(f"pN0_{ft}", [128, 1024]) as pN0, \
                     nc.psum_tensor(f"pN1_{ft}", [128, 1024]) as pN1, \
                     nc.psum_tensor(f"pN2_{ft}", [128, 1024]) as pN2, \
                     nc.psum_tensor(f"pN3_{ft}", [128, 1024]) as pN3:
                    pN = [pN0, pN1, pN2, pN3]
                    pn_free = [0, 0, 0, 0]

                    for tp2 in range(nfc // 2):
                        tcl0 = tp2 * 2

                        @block.tensor
                        def _(tensor, hsum_dv=hsum_dv):
                            sDV.wait(tensor, hsum_dv)
                            sAC.wait(tensor)
                        for ic2 in range(NIC // 2):
                            dp = ic2 % 2

                            @block.gpsimd
                            def _(gpsimd, ic2=ic2, dp=dp):
                                sPE.wait(gpsimd, dw_free[dp])
                                sW.inc(gpsimd.dma_start(
                                    out=s_dw[dp][:],
                                    in_=dw_r[:, ic2 * 2:(ic2 + 1) * 2, :]), 16)
                            dw_ld = sW.n

                            @block.tensor
                            def _(tensor, ft=ft, tcl0=tcl0, ic2=ic2, dp=dp, dw_ld=dw_ld):
                                sW.wait(tensor, dw_ld)
                                if ic2 == 0:
                                    for q in range(4):
                                        sDV.wait(tensor, pn_free[q])
                                ins = None
                                for dj in range(2):
                                    ic = ic2 * 2 + dj
                                    for tl in range(2):
                                        tcl = tcl0 + tl
                                        for half in range(2):
                                            for ns in range(2):
                                                c0 = half * 1024 + ns * 512
                                                ins = tensor.matmul(
                                                    pN[tl * 2 + half][:, ns * 512:(ns + 1) * 512],
                                                    s_h[:, ic, tcl * 128:(tcl + 1) * 128],
                                                    s_dw[dp][:, dj, c0:c0 + 512],
                                                    start=(ic == 0), stop=False)
                                sPE.inc(ins)
                            dw_free[dp] = sPE.n
                            if ic2 == NIC // 2 - 1:
                                h_free[0] = sPE.n

                        @block.tensor
                        def _(tensor, ft=ft, tcl0=tcl0):
                            ins = None
                            for tl in range(2):
                                tcl = tcl0 + tl
                                for half in range(2):
                                    for ns in range(2):
                                        for hilo in range(2):
                                            ins = tensor.matmul(
                                                pN[tl * 2 + half][:, ns * 512:(ns + 1) * 512],
                                                s_hsbf[0:1, hilo * 512 + tcl * 128:
                                                       hilo * 512 + (tcl + 1) * 128],
                                                s_n128r[:],
                                                start=False,
                                                stop=(hilo == 1))
                            sPE.inc(ins)
                        dn_pe = sPE.n

                        for tl in range(2):
                            tcl = tcl0 + tl
                            tc = ft * nfc + tcl
                            xp = tc % 2
                            pp = tc % 2

                            @block.sync
                            def _(sync, tc=tc, xp=xp):
                                sDV.wait(sync, x_free4[xp])
                                sLd.inc(sync.dma_start(
                                    out=s_x[xp][:],
                                    in_=xp_r[tc * 128:(tc + 1) * 128, :]), 16)
                            x_ld = sLd.n

                            @block.vector
                            def _(vector, tl=tl, dn_pe=dn_pe):
                                sPE.wait(vector, dn_pe)
                                sLd.wait(vector, dsb_ld)
                                sAC.wait(vector, v14_ac_free[0])
                                for half in range(2):
                                    n = sDV.inc(vector.tensor_tensor(
                                        s_v14[:, half * 1024:(half + 1) * 1024],
                                        pN[tl * 2 + half][:],
                                        s_dsb[:, half * 1024:(half + 1) * 1024],
                                        ALU.mult))
                                    pn_free[tl * 2 + half] = n
                            v_dv = sDV.n
                            down_evac_dv[0] = v_dv

                            b = 8 * pp

                            @block.scalar
                            def _(scalar, b=b, v_dv=v_dv, pp=pp):
                                sDV.wait(scalar, v_dv)
                                sDV.wait(scalar, o4_free[0])
                                sOut.wait(scalar, o4_free[1])
                                sDV.wait(scalar, stat_free[pp])
                                scalar.activation(
                                    s_o4[:], s_v14[:], AF.Square, bias=0.0, scale=1.0,
                                    accum_out=s_stat[:, b + 1:b + 2])
                                scalar.activation(
                                    s_o4[:], s_v14[:], AF.Identity, bias=0.0, scale=1.0,
                                    accum_out=s_stat[:, b + 0:b + 1])
                                sAC.inc(scalar.copy(s_stat[:, b + 5:b + 7],
                                                    s_stat[:, b:b + 2]))
                                v14_ac_free[0] = sAC.n

                            @block.vector
                            def _(vector, b=b):
                                sAC.wait(vector)
                                vector.tensor_scalar_mul(s_stat[:, b + 2:b + 3],
                                                         s_stat[:, b:b + 1], -1.0 / DM)
                                vector.tensor_scalar_mul(s_stat[:, b + 7:b + 8],
                                                         s_stat[:, b:b + 1], 1.0 / DM)
                                vector.tensor_scalar(s_stat[:, b + 3:b + 4],
                                                     s_stat[:, b + 1:b + 2],
                                                     1.0 / DM, EPS, ALU.mult, ALU.add)
                                vector.tensor_tensor(s_stat[:, b + 4:b + 5],
                                                     s_stat[:, b + 7:b + 8],
                                                     s_stat[:, b + 7:b + 8], ALU.mult)
                                vector.tensor_copy(s_junk[:], s_stat[:, b + 2:b + 3])
                                sDV.inc(vector.tensor_tensor(s_stat[:, b + 3:b + 4],
                                                             s_stat[:, b + 3:b + 4],
                                                             s_stat[:, b + 4:b + 5],
                                                             ALU.subtract))

                            @block.scalar
                            def _(scalar, b=b):
                                sDV.wait(scalar)
                                scalar.activation(s_stat[:, b + 5:b + 6],
                                                  s_stat[:, b + 3:b + 4],
                                                  AF.Ln, bias=0.0, scale=1.0)
                                sAC.inc(scalar.activation(s_stat[:, b + 6:b + 7],
                                                          s_stat[:, b + 5:b + 6],
                                                          AF.Exp, bias=0.0, scale=-0.5))

                            @block.vector
                            def _(vector, b=b, xp=xp, x_ld=x_ld, pp=pp):
                                sAC.wait(vector)
                                vector.tensor_scalar(s_o4[:], s_v14[:],
                                                     s_stat[:, b + 2:b + 3],
                                                     s_stat[:, b + 6:b + 7],
                                                     ALU.add, ALU.mult)
                                sLd.wait(vector, x_ld)
                                sOut.wait(vector, o4_free[1])
                                n = sDV.inc(vector.tensor_tensor(
                                    s_o4[:], s_o4[:], s_x[xp][:], ALU.add))
                                x_free4[xp] = n
                                stat_free[pp] = n
                            o_dv = sDV.n
                            o4_free[0] = o_dv

                            @block.sync
                            def _(sync, tc=tc, o_dv=o_dv):
                                sDV.wait(sync, o_dv)
                                sOut.inc(sync.dma_start(
                                    out=out_d[tc * 128:(tc + 1) * 128, :],
                                    in_=s_o4[:]), 16)
                            o4_free[1] = sOut.n

            @block.sync
            def _(sync):
                sOut.wait(sync)
            if DBG:
                @block.gpsimd
                def _(gpsimd):
                    sW.wait(gpsimd)
    return nc, DBG



def _u8(a):
    a = np.ascontiguousarray(np.asarray(a))
    assert a.dtype == np.int32
    return np.ascontiguousarray(a.view("<u1")[..., 0::4])


def prep_inputs(L0_attn_x, L0_attn_w_qkv, L0_attn_w_out, L0_attn_decay_param,
                L0_ffn_gate_w, L0_ffn_gate_scale, L0_ffn_up_w, L0_ffn_up_scale,
                L0_ffn_down_w, L0_ffn_down_scale, tok=2048):
    x = np.ascontiguousarray(np.asarray(L0_attn_x), dtype=np.float32)
    B, T, _ = x.shape
    wq = np.ascontiguousarray(np.asarray(L0_attn_w_qkv), dtype=np.float32)
    wo = np.ascontiguousarray(np.asarray(L0_attn_w_out), dtype=np.float32)
    dec = np.ascontiguousarray(
        np.broadcast_to(np.asarray(L0_attn_decay_param, dtype=np.float32), (128, H)))
    gw = np.ascontiguousarray(_u8(L0_ffn_gate_w).T)
    uw = np.ascontiguousarray(_u8(L0_ffn_up_w).T)
    dw = np.ascontiguousarray(_u8(L0_ffn_down_w).T)
    gs = np.ascontiguousarray(
        np.asarray(L0_ffn_gate_scale, dtype=np.float32).reshape(NIC, 128).T)
    us = np.ascontiguousarray(
        np.asarray(L0_ffn_up_scale, dtype=np.float32).reshape(NIC, 128).T)
    dsb = np.ascontiguousarray(
        np.broadcast_to(np.asarray(L0_ffn_down_scale, dtype=np.float32), (128, DM)))
    iota = np.ascontiguousarray(
        np.broadcast_to(-np.arange(1, tok + 1, dtype=np.float16), (128, tok)))
    ident = np.eye(128, dtype=ml_dtypes.bfloat16)
    n128q = np.full((1, 128), -128.0, dtype=ml_dtypes.bfloat16)
    n128r = np.full((1, 512), -128.0, dtype=ml_dtypes.bfloat16)
    ones = np.ones((128, 1), dtype=ml_dtypes.bfloat16)
    onesr = np.ones((1, 128), dtype=ml_dtypes.bfloat16)

    halves = T // tok
    in_maps = []
    for c in range(NCORE):
        s, hf = c // halves, c % halves
        oh = np.zeros((NCORE, 1), np.float32)
        if hf > 0:
            oh[c - 1, 0] = 1.0
        in_maps.append({
            "x": np.ascontiguousarray(x[s, hf * tok:(hf + 1) * tok, :]),
            "wq": wq, "wo": wo, "dec": dec, "oh": oh,
            "gw": gw, "uw": uw, "dw": dw, "gs": gs, "us": us, "dsb": dsb,
            "iota": iota, "ident": ident, "n128q": n128q, "n128r": n128r,
            "ones": ones, "onesr": onesr,
        })
    return in_maps, (B, T)


def kernel(**inputs):
    tok = 2048
    in_maps, (B, T) = prep_inputs(tok=tok, **inputs)
    nc, _ = build_nc(tok=tok)
    res = run_bass_kernel_spmd(nc, in_maps, core_ids=list(range(NCORE)))
    halves = T // tok
    out = np.empty((B, T, DM), dtype=np.float32)
    for c in range(NCORE):
        s, hf = c // halves, c % halves
        out[s, hf * tok:(hf + 1) * tok, :] = res.results[c]["out"]
    return out


# version 07
# speedup vs baseline: 1.2034x; 1.1029x over previous
import numpy as np
import ml_dtypes
from contextlib import ExitStack

import concourse.bass as bass
import concourse.mybir as mybir
from concourse.bass_utils import run_bass_kernel_spmd

F32 = mybir.dt.float32
BF16 = mybir.dt.bfloat16
U8 = mybir.dt.uint8
AF = mybir.ActivationFunctionType
ALU = mybir.AluOpType

NCORE = 8
DM = 2048
NKC = DM // 128
H = 16
QKV = 3 * DM
FF = 8192
NIC = FF // 128
EPS = 1e-5
EPS_DEN = 1e-6


def build_nc(tok=2048, dbg=()):
    NTC = tok // 128
    nft = max(1, tok // 512)
    ftok = tok // nft
    nfc = ftok // 128
    ngrp = H // 2

    nc = bass.Bass("TRN2", target_bir_lowering=False, debug=False)

    x_d = nc.dram_tensor("x", [tok, DM], F32, kind="ExternalInput").ap()
    wq_d = nc.dram_tensor("wq", [DM, QKV], F32, kind="ExternalInput").ap()
    wo_d = nc.dram_tensor("wo", [DM, DM], F32, kind="ExternalInput").ap()
    dec_d = nc.dram_tensor("dec", [128, H], F32, kind="ExternalInput").ap()
    oh_d = nc.dram_tensor("oh", [NCORE, 1], F32, kind="ExternalInput").ap()
    gw_d = nc.dram_tensor("gw", [DM, FF], U8, kind="ExternalInput").ap()
    uw_d = nc.dram_tensor("uw", [DM, FF], U8, kind="ExternalInput").ap()
    dw_d = nc.dram_tensor("dw", [FF, DM], U8, kind="ExternalInput").ap()
    gs_d = nc.dram_tensor("gs", [128, NIC], F32, kind="ExternalInput").ap()
    us_d = nc.dram_tensor("us", [128, NIC], F32, kind="ExternalInput").ap()
    dsb_d = nc.dram_tensor("dsb", [128, DM], F32, kind="ExternalInput").ap()
    iota_d = nc.dram_tensor("iota", [128, tok], mybir.dt.float16, kind="ExternalInput").ap()
    ident_d = nc.dram_tensor("ident", [128, 128], BF16, kind="ExternalInput").ap()
    n128q_d = nc.dram_tensor("n128q", [1, 128], BF16, kind="ExternalInput").ap()
    n128r_d = nc.dram_tensor("n128r", [1, 512], BF16, kind="ExternalInput").ap()
    ones_d = nc.dram_tensor("ones", [128, 1], BF16, kind="ExternalInput").ap()
    onesr_d = nc.dram_tensor("onesr", [1, 128], BF16, kind="ExternalInput").ap()
    out_d = nc.dram_tensor("out", [tok, DM], F32, kind="ExternalOutput").ap()

    attn_sp = nc.dram_tensor("attn_sp", [DM, tok], BF16)
    xp_sp = nc.dram_tensor("xp_sp", [tok, DM], F32)
    cc_in = [nc.dram_tensor(f"cc_in{g}", [128, 4], F32) for g in range(ngrp)]
    cc_out = [nc.dram_tensor(f"cc_out{g}", [NCORE * 128, 4], F32)
              for g in range(ngrp)]

    DBG = {}
    def dbg_t(name, shape, dt=F32):
        if name in dbg:
            DBG[name] = nc.dram_tensor("dbg_" + name, shape, dt,
                                       kind="ExternalOutput").ap()
    dbg_t("xln", [128, NKC * tok], BF16)
    dbg_t("q0", [128, tok], BF16); dbg_t("k0", [128, tok], BF16)
    dbg_t("kv0", [128, tok], BF16)
    dbg_t("y0", [128, tok], BF16); dbg_t("ys0", [128, tok], BF16)
    dbg_t("at0", [128, tok], BF16)
    dbg_t("den", [128, NTC]); dbg_t("xp", [tok, DM])
    dbg_t("qy", [128, tok], BF16)
    dbg_t("yl0", [128, tok], BF16); dbg_t("lamt", [128, tok])
    dbg_t("init", [128, 4]); dbg_t("gath", [NCORE, 512])
    dbg_t("ccin", [128, 4])
    dbg_t("wo0", [128, DM], BF16); dbg_t("att0", [128, NKC * 512], BF16)
    dbg_t("v1", [128, DM])
    dbg_t("o0", [128, DM]); dbg_t("st3", [128, 16])
    dbg_t("snap", [128, 8]); dbg_t("v1b", [128, DM])
    dbg_t("x30", [128, NKC * 512], BF16); dbg_t("h0", [128, 512], BF16)
    dbg_t("stat", [128, 16]); dbg_t("rden", [128, NTC])

    wq_r = wq_d.rearrange("(kc p) c -> p kc c", p=128)
    wo_r = wo_d.rearrange("(kc p) c -> p kc c", p=128)
    gw_r = gw_d.rearrange("(kc p) c -> p kc c", p=128)
    uw_r = uw_d.rearrange("(kc p) c -> p kc c", p=128)
    dw_r = dw_d.rearrange("(ic p) c -> p ic c", p=128)
    attn_r = attn_sp.ap().rearrange("(fc p) t -> p fc t", p=128)
    xp_r = xp_sp.ap()

    st = ExitStack()
    with st:
        def sb(stk, name, shape, dt=F32):
            return stk.enter_context(nc.sbuf_tensor(name, shape, dt))
        def psm(stk, name, shape, dt=F32):
            return stk.enter_context(nc.psum_tensor(name, shape, dt))

        class _S:
            def __init__(self, name):
                self.h = st.enter_context(nc.semaphore(name))
                self.n = 0
            def inc(self, instr, k=1):
                instr.then_inc(self.h, k); self.n += k; return self.n
            def wait(self, eng, val=None):
                v = self.n if val is None else val
                if v > 0:
                    eng.wait_ge(self.h, v)

        sLd = _S("sLd"); sW = _S("sW"); sPE = _S("sPE"); sDV = _S("sDV")
        sAC = _S("sAC"); sSp = _S("sSp"); sCc = _S("sCc"); sG = _S("sG")
        sOut = _S("sOut")

        s_x = [sb(st, f"s_x{i}", [128, DM]) for i in range(2)]
        s_xlntm = sb(st, "s_xlntm", [128, DM], BF16)
        s_stat = sb(st, "s_stat", [128, 16])
        s_dec = sb(st, "s_dec", [128, H]); s_lam = sb(st, "s_lam", [128, H])
        s_lam2 = sb(st, "s_lam2", [128, H]); s_junk = sb(st, "s_junk", [128, 1])
        s_lnl = sb(st, "s_lnl", [128, H])
        s_oh = sb(st, "s_oh", [NCORE, 1]); s_ohbf = sb(st, "s_ohbf", [NCORE, 1], BF16)
        s_ident = sb(st, "s_ident", [128, 128], BF16)
        s_ones = sb(st, "s_ones", [128, 1], BF16)
        s_onesr = sb(st, "s_onesr", [1, 128], BF16)
        s_n128q = sb(st, "s_n128q", [1, 128], BF16)
        s_n128r = sb(st, "s_n128r", [1, 512], BF16)
        s_gs = sb(st, "s_gs", [128, NIC]); s_us = sb(st, "s_us", [128, NIC])
        s_den = sb(st, "s_den", [128, NTC]); s_rden = sb(st, "s_rden", [128, NTC])

        block = st.enter_context(nc.Block())

        @block.sync
        def _(sync):
            for tile, src in ((s_dec, dec_d), (s_oh, oh_d), (s_ident, ident_d),
                              (s_ones, ones_d), (s_n128q, n128q_d),
                              (s_n128r, n128r_d), (s_gs, gs_d), (s_us, us_d),
                              (s_onesr, onesr_d)):
                sLd.inc(sync.dma_start(out=tile[:], in_=src), 16)

        def act_loadset(scalar, idx):
            scalar.add_instruction(mybir.InstLoadActFuncSet(
                name=nc.get_next_instruction_name(),
                act_func_set_id=idx, ins=[], outs=[]))

        @block.scalar
        def _(scalar):
            act_loadset(scalar, 6)
            sLd.wait(scalar)
            sAC.inc(scalar.activation(s_lam[:], s_dec[:], AF.Exp, bias=0.0, scale=-1.0))

        @block.vector
        def _(vector):
            vector.memset(s_stat[:], 0.0)
            sAC.wait(vector)
            sDV.inc(vector.tensor_scalar_add(s_lam[:], s_lam[:], 1.0))
            sDV.inc(vector.tensor_copy(s_ohbf[:], s_oh[:]))

        @block.scalar
        def _(scalar):
            sDV.wait(scalar)
            scalar.activation(s_lnl[:], s_lam[:], AF.Ln, bias=0.0, scale=1.0)
            sAC.inc(scalar.activation(s_lam2[:], s_lnl[:], AF.Exp, bias=0.0, scale=-1.0))
        lam_ac = sAC.n
        lam_dv = sDV.n
        stat_free = {0: 0, 1: 0}

        def ln_stats(src_tile, scr_tile, pp, pre_wait=()):
            b = 8 * pp

            @block.scalar
            def _(scalar, src=src_tile, scr=scr_tile, b=b, pp=pp, pw=tuple(pre_wait)):
                for sem, val in pw:
                    sem.wait(scalar, val)
                sDV.wait(scalar, stat_free[pp])
                scalar.activation(scr[:], src[:], AF.Square, bias=0.0,
                                  scale=1.0, accum_out=s_stat[:, b + 1:b + 2])
                scalar.activation(scr[:], src[:], AF.Identity, bias=0.0,
                                  scale=1.0, accum_out=s_stat[:, b + 0:b + 1])
                sAC.inc(scalar.copy(s_stat[:, b + 5:b + 7], s_stat[:, b:b + 2]))

            @block.vector
            def _(vector, b=b):
                sAC.wait(vector)
                vector.tensor_scalar_mul(s_stat[:, b + 2:b + 3], s_stat[:, b:b + 1], -1.0 / DM)
                vector.tensor_scalar_mul(s_stat[:, b + 7:b + 8], s_stat[:, b:b + 1], 1.0 / DM)
                vector.tensor_scalar(s_stat[:, b + 3:b + 4], s_stat[:, b + 1:b + 2],
                                     1.0 / DM, EPS, ALU.mult, ALU.add)
                vector.tensor_tensor(s_stat[:, b + 4:b + 5], s_stat[:, b + 7:b + 8],
                                     s_stat[:, b + 7:b + 8], ALU.mult)
                vector.tensor_copy(s_junk[:], s_stat[:, b + 2:b + 3])
                sDV.inc(vector.tensor_tensor(s_stat[:, b + 3:b + 4], s_stat[:, b + 3:b + 4],
                                             s_stat[:, b + 4:b + 5], ALU.subtract))

            @block.scalar
            def _(scalar, b=b):
                sDV.wait(scalar)
                scalar.activation(s_stat[:, b + 5:b + 6], s_stat[:, b + 3:b + 4],
                                  AF.Ln, bias=0.0, scale=1.0)
                sAC.inc(scalar.activation(s_stat[:, b + 6:b + 7], s_stat[:, b + 5:b + 6],
                                          AF.Exp, bias=0.0, scale=-0.5))

        p123 = st.enter_context(ExitStack())
        s_big = sb(p123, "s_big", [128, NKC, max(tok, DM)], BF16)
        s_scr = sb(p123, "s_scr", [128, DM])

        with ExitStack() as p1:
            pTr = psm(p1, "pTr", [128, 2048], BF16)
            x_free = [0, 0]
            tr_free = [0, 0]
            xlntm_free = [0]
            for tc in range(NTC):
                par = tc % 2
                pp = tc % 2

                @block.sync
                def _(sync, tc=tc, par=par):
                    sDV.wait(sync, x_free[par])
                    sLd.inc(sync.dma_start(out=s_x[par][:],
                                           in_=x_d[tc * 128:(tc + 1) * 128, :]), 16)
                x_ld = sLd.n
                ln_stats(s_x[par], s_scr, pp, pre_wait=[(sLd, x_ld)])

                @block.vector
                def _(vector, par=par, pp=pp):
                    sAC.wait(vector)
                    sPE.wait(vector, xlntm_free[0])
                    b = 8 * pp
                    n = sDV.inc(vector.tensor_scalar(
                        s_xlntm[:], s_x[par][:], s_stat[:, b + 2:b + 3],
                        s_stat[:, b + 6:b + 7], ALU.add, ALU.mult))
                    x_free[par] = n
                    stat_free[pp] = n

                for fq in range(4):
                    tp = fq % 2

                    @block.tensor
                    def _(tensor, fq=fq, tp=tp):
                        sDV.wait(tensor)
                        sDV.wait(tensor, tr_free[tp])
                        ins = None
                        for j in range(4):
                            fc = fq * 4 + j
                            ins = tensor.transpose(
                                pTr[:, tp * 1024 + j * 128: tp * 1024 + (j + 1) * 128],
                                s_xlntm[:, fc * 128:(fc + 1) * 128], s_ident[:])
                        sPE.inc(ins)

                    @block.vector
                    def _(vector, tc=tc, fq=fq, tp=tp):
                        sPE.wait(vector)
                        n = sDV.inc(vector.tensor_copy(
                            s_big[:, fq * 4:(fq + 1) * 4, tc * 128:(tc + 1) * 128],
                            pTr[:, tp * 1024:tp * 1024 + 512]
                                .rearrange("p (a b) -> p a b", a=4)))
                        tr_free[tp] = n
                xlntm_free[0] = sPE.n
        xln_dv = sDV.n

        if "xln" in DBG:
            @block.gpsimd
            def _(gpsimd):
                sDV.wait(gpsimd)
                sW.inc(gpsimd.dma_start(
                    out=DBG["xln"].rearrange("p (a b) -> p a b", a=NKC),
                    in_=s_big[:, :, 0:tok]), 16)

        p2 = ExitStack()
        with p2:
            pQ = [psm(p2, f"pQ{i}", [128, 1024]) for i in range(2)]
            pD = psm(p2, "pD", [128, 1024])
            pR = psm(p2, "pR", [128, 1024])
            s_denr2 = sb(p2, "s_denr2", [1, min(1024, tok)])
            s_denrb2 = sb(p2, "s_denrb2", [1, tok], BF16)
            s_wqb = [sb(p2, f"s_wqb{i}", [128, NKC, 128], BF16) for i in range(4)]
            s_q = [sb(p2, f"s_q{i}", [128, tok], BF16) for i in range(4)]
            s_k = [sb(p2, f"s_k{i}", [128, tok], BF16) for i in range(2)]
            s_kv = [sb(p2, f"s_kv{i}", [128, tok], BF16) for i in range(2)]
            s_y = [sb(p2, f"s_y{i}", [128, tok], BF16) for i in range(6)]
            s_attn2 = [sb(p2, f"s_attn{i}", [128, tok], BF16) for i in range(2)]
            s_qy = sb(p2, "s_qy", [128, tok], BF16)
            s_lamt = sb(p2, "s_lamt", [128, tok])
            s_iota = sb(p2, "s_iota", [128, tok], mybir.dt.float16)
            s_phr = sb(p2, "s_phr", [128, 512], BF16)
            s_phe = sb(p2, "s_phe", [128, 512], BF16)
            s_sloc = sb(p2, "s_sloc", [128, 4])
            s_gath = sb(p2, "s_gath", [NCORE, 512])
            s_gathbf = sb(p2, "s_gathbf", [NCORE, 512], BF16)
            s_init = [sb(p2, f"s_init{i}", [128, 4]) for i in range(2)]

            @block.sync
            def _(sync):
                sLd.inc(sync.dma_start(out=s_iota[:], in_=iota_d), 16)
            iota_ld = sLd.n

            @block.vector
            def _(vector):
                vector.memset(s_sloc[:], 0.0)

            wq_free = [0] * 4
            pq_free_dv = [0, 0]
            pq_free_ac = [0, 0]
            ph_free = [0]
            scan_dv = [0] * H
            grp_gath = [0] * ngrp
            grp_init = [0] * ngrp
            denr_free = [0, 0]
            pr_free = [0]
            denr_ac_gate = [0]
            denrb_free = [0]
            gath_free = [0]
            sloc_free = [0]
            lamt_free = [0]
            qy_free = [0]
            attn_free = [0, 0]
            stage_ds = []
            pq_par = [0]

            def emit_strip_load(hi, tp):
                slot = (hi * 3 + tp) % 4
                cbase = tp * DM + hi * 128

                @block.gpsimd
                def _(gpsimd, slot=slot, cbase=cbase):
                    sPE.wait(gpsimd, wq_free[slot])
                    ins = None
                    for kq in range(4):
                        ins = gpsimd.dma_start(
                            out=s_wqb[slot][:, kq * 4:(kq + 1) * 4, :],
                            in_=wq_r[:, kq * 4:(kq + 1) * 4, cbase:cbase + 128])
                        sW.inc(ins, 16)
                return slot, sW.n

            nhalf = max(1, tok // 1024)
            hw = min(1024, tok)

            def emit_qkv(hi, tp, slot, wq_rdy):
                for half in range(nhalf):
                    par = pq_par[0]
                    pq = pQ[par]

                    @block.tensor
                    def _(tensor, slot=slot, wq_rdy=wq_rdy, half=half, pq=pq, par=par):
                        sW.wait(tensor, wq_rdy)
                        sDV.wait(tensor, xln_dv)
                        sAC.wait(tensor, pq_free_ac[par])
                        sDV.wait(tensor, pq_free_dv[par])
                        ins = None
                        for ns in range(hw // 512):
                            t0 = half * hw + ns * 512
                            for kc in range(NKC):
                                ins = tensor.matmul(
                                    pq[:, ns * 512:(ns + 1) * 512],
                                    s_wqb[slot][:, kc, :],
                                    s_big[:, kc, t0:t0 + 512],
                                    start=(kc == 0), stop=(kc == NKC - 1))
                        sPE.inc(ins)
                    pe_done = sPE.n

                    if tp < 2:
                        dst = s_q[hi % 4] if tp == 0 else s_k[hi % 2]

                        for sub in range(hw // 512):
                            s0 = sub * 512

                            @block.scalar
                            def _(scalar, pq=pq, pe=pe_done, s0=s0):
                                sPE.wait(scalar, pe)
                                sDV.wait(scalar, ph_free[0])
                                scalar.activation(s_phr[:], pq[:, s0:s0 + 512],
                                                  AF.Relu, bias=0.0, scale=-1.0)
                                sAC.inc(scalar.activation(s_phe[:], s_phr[:],
                                                          AF.Exp, bias=0.0, scale=-1.0))
                            ac_done = sAC.n

                            @block.vector
                            def _(vector, pq=pq, dst=dst, half=half, ac=ac_done,
                                  par=par, s0=s0):
                                sAC.wait(vector, ac)
                                vector.tensor_tensor(s_phr[:], pq[:, s0:s0 + 512],
                                                     s_phr[:], ALU.add)
                                n = sDV.inc(vector.tensor_tensor(
                                    dst[:, half * hw + s0:half * hw + s0 + 512],
                                    s_phr[:], s_phe[:], ALU.add))
                                pq_free_dv[par] = n
                                ph_free[0] = n
                            pq_free_ac[par] = ac_done
                    else:
                        @block.vector
                        def _(vector, pq=pq, hi=hi, half=half, pe=pe_done, par=par):
                            sPE.wait(vector, pe)
                            n = sDV.inc(vector.tensor_tensor(
                                s_kv[hi % 2][:, half * hw:(half + 1) * hw],
                                s_k[hi % 2][:, half * hw:(half + 1) * hw],
                                pq[:, 0:hw], ALU.mult))
                            pq_free_dv[par] = n
                            pq_free_ac[par] = sAC.n
                    pq_par[0] = 1 - par

            def emit_scans(hi):
                ykv = s_y[(hi % 3) * 2]
                yks = s_y[(hi % 3) * 2 + 1]

                @block.vector
                def _(vector, hi=hi, ykv=ykv, yks=yks):
                    sAC.wait(vector, lam_ac)
                    sG.wait(vector, sloc_free[0])
                    if DBG:
                        sW.wait(vector)
                    lamc = s_lam2[:, hi:hi + 1].broadcast_to([128, tok])
                    vector.tensor_tensor_scan(ykv[:], lamc, s_kv[hi % 2][:],
                                              0.0, ALU.mult, ALU.add)
                    vector.tensor_tensor_scan(yks[:], lamc, s_k[hi % 2][:],
                                              0.0, ALU.mult, ALU.add)
                    hl = hi % 2
                    vector.tensor_copy(s_sloc[:, 2 * hl:2 * hl + 1], ykv[:, tok - 1:tok])
                    vector.tensor_copy(s_sloc[:, 2 * hl + 1:2 * hl + 2],
                                       yks[:, tok - 1:tok])
                    sDV.inc(vector.reduce_sum(s_junk[:], s_sloc[:],
                                              axis=mybir.AxisListType.X))
                scan_dv[hi] = sDV.n
                return ykv, yks

            def emit_collective(g):
                @block.gpsimd
                def _(gpsimd, g=g):
                    sDV.wait(gpsimd, scan_dv[2 * g + 1])
                    n = sG.inc(gpsimd.dma_start(out=cc_in[g].ap(), in_=s_sloc[:]), 16)
                    sloc_free[0] = n
                    sG.wait(gpsimd, n)
                    sCc.inc(gpsimd.collective_compute(
                        "AllGather", ALU.bypass,
                        replica_groups=[list(range(NCORE))],
                        ins=[cc_in[g].ap().opt()],
                        outs=[cc_out[g].ap().opt()]), 1)
                    sCc.wait(gpsimd)
                    sPE.wait(gpsimd, gath_free[0])
                    sG.inc(gpsimd.dma_start(
                        out=s_gath[:],
                        in_=cc_out[g].ap().rearrange("(r p) g -> r (p g)", p=128)), 16)
                grp_gath[g] = sG.n

            def emit_select(g):
                @block.vector
                def _(vector, g=g):
                    sG.wait(vector, grp_gath[g])
                    vector.tensor_copy(
                        s_gathbf[:].rearrange("r (g p) -> r g p", g=4),
                        s_gath[:, :].rearrange("r (p g) -> r g p", g=4))
                    vector.tensor_copy(s_junk[:], s_ones[:])
                    sDV.inc(vector.tensor_copy(s_junk[:], s_ones[:]))
                dv = sDV.n

                @block.tensor
                def _(tensor, dv=dv, g=g):
                    sDV.wait(tensor, dv)
                    if g > 0:
                        sDV.wait(tensor, grp_init[g - 1])
                    ins = None
                    for gc in range(4):
                        ins = tensor.matmul(pD[:, 512 + gc:513 + gc],
                                            s_gathbf[:, gc * 128:(gc + 1) * 128],
                                            s_ohbf[:],
                                            start=True, stop=True)
                    sPE.inc(ins)
                gath_free[0] = sPE.n

                @block.vector
                def _(vector, g=g):
                    sPE.wait(vector)
                    sDV.inc(vector.tensor_copy(s_init[g % 2][:], pD[:, 512:516]))
                grp_init[g] = sDV.n
                if g == 0 and "init" in DBG:
                    @block.gpsimd
                    def _(gpsimd):
                        sDV.wait(gpsimd)
                        sW.inc(gpsimd.dma_start(out=DBG["init"], in_=s_init[0][:]), 16)
                        if "ccin" in DBG:
                            sW.inc(gpsimd.dma_start(out=DBG["ccin"],
                                                    in_=cc_in[0].ap()), 16)

            def emit_epilogue(g, heads_y, last=False):
                stage_ds.clear()
                for hl, (hi, ykv, yks) in enumerate(heads_y):
                    @block.scalar
                    def _(scalar, hi=hi):
                        sAC.wait(scalar, lam_ac)
                        sLd.wait(scalar, iota_ld)
                        sDV.wait(scalar, lamt_free[0])
                        sAC.inc(scalar.activation(s_lamt[:], s_iota[:], AF.Exp,
                                                  bias=0.0, scale=s_lnl[:, hi:hi + 1]))
                    ac_done = sAC.n

                    if hi == 0 and "lamt" in DBG:
                        @block.gpsimd
                        def _(gpsimd, ac=ac_done):
                            sAC.wait(gpsimd, ac)
                            sW.inc(gpsimd.dma_start(out=DBG["lamt"], in_=s_lamt[:]), 16)

                    @block.vector
                    def _(vector, g=g, hl=hl, hi=hi, ykv=ykv, yks=yks, ac=ac_done):
                        sAC.wait(vector, ac)
                        sDV.wait(vector, grp_init[g])
                        if DBG:
                            sW.wait(vector)
                        ini = s_init[g % 2]
                        vector.scalar_tensor_tensor(ykv[:], s_lamt[:],
                                                    ini[:, 2 * hl:2 * hl + 1],
                                                    ykv[:], ALU.mult, ALU.add)
                        n = sDV.inc(vector.scalar_tensor_tensor(
                            yks[:], s_lamt[:], ini[:, 2 * hl + 1:2 * hl + 2],
                            yks[:], ALU.mult, ALU.add))
                        lamt_free[0] = n
                        sPE.wait(vector, qy_free[0])
                        sDV.inc(vector.tensor_tensor(s_qy[:], s_q[hi % 4][:],
                                                     yks[:], ALU.mult))
                    qy_dv = sDV.n

                    @block.vector
                    def _(vector, hi=hi, ykv=ykv):
                        sSp.wait(vector, attn_free[hi % 2])
                        if DBG:
                            sW.wait(vector)
                        sDV.inc(vector.tensor_tensor(s_attn2[hi % 2][:], s_q[hi % 4][:],
                                                     ykv[:], ALU.mult))
                    atm_dv = sDV.n

                    if stage_ds:
                        for f in stage_ds:
                            f()
                        stage_ds.clear()

                    hw2 = min(1024, tok)
                    for dh in range(max(1, tok // 1024)):
                        for ql in range(hw2 // 512):
                            qr = dh * 2 + ql
                            dbank = pD if ql % 2 == 0 else pR

                            @block.tensor
                            def _(tensor, qr=qr, ql=ql, qy_dv=qy_dv, dbank=dbank):
                                if qr == 0:
                                    sDV.wait(tensor, qy_dv)
                                sDV.wait(tensor, denr_free[ql % 2])
                                if ql % 2 == 1:
                                    sDV.wait(tensor, pr_free[0])
                                sPE.inc(tensor.matmul(dbank[0:1, 0:512], s_ones[:],
                                                      s_qy[:, qr * 512:(qr + 1) * 512],
                                                      start=True, stop=True))
                            pe_n = sPE.n

                            @block.vector
                            def _(vector, ql=ql, pe_n=pe_n, dbank=dbank):
                                sPE.wait(vector, pe_n)
                                sAC.wait(vector, denr_ac_gate[0])
                                denr_free[ql % 2] = sDV.inc(vector.tensor_copy(
                                    s_denr2[0:1, ql * 512:(ql + 1) * 512],
                                    dbank[0:1, 0:512]))
                        if dh + 1 == max(1, tok // 1024):
                            qy_free[0] = sPE.n

                        @block.vector
                        def _(vector):
                            vector.tensor_copy(s_junk[:], s_ones[:])
                            sDV.inc(vector.tensor_scalar_max(s_denr2[:], s_denr2[:],
                                                             EPS_DEN))
                        dmax_dv = sDV.n

                        @block.scalar
                        def _(scalar, dh=dh, dmax_dv=dmax_dv):
                            sDV.wait(scalar, dmax_dv)
                            sPE.wait(scalar, denrb_free[0])
                            scalar.activation(s_denr2[:], s_denr2[:], AF.Ln,
                                              bias=0.0, scale=1.0)
                            scalar.activation(s_denrb2[0:1, dh * hw2:(dh + 1) * hw2],
                                              s_denr2[:], AF.Exp,
                                              bias=0.0, scale=-1.0)
                            sAC.inc(scalar.copy(s_denr2[0:1, 0:1],
                                                s_denrb2[0:1, dh * hw2:dh * hw2 + 1]))
                        rdenr_ac = sAC.n
                        denr_ac_gate[0] = rdenr_ac

                    def stage_d(g=g, hi=hi, rdenr_ac=rdenr_ac, at_base=None):
                        for half in range(max(1, tok // 1024)):
                            @block.tensor
                            def _(tensor, half=half, rdenr_ac=rdenr_ac, g=g):
                                sAC.wait(tensor, rdenr_ac)
                                sDV.wait(tensor, grp_init[g])
                                sDV.wait(tensor, pr_free[0])
                                ins = None
                                for ns in range(hw2 // 512):
                                    ins = tensor.matmul(
                                        pR[:, ns * 512:(ns + 1) * 512], s_onesr[:],
                                        s_denrb2[0:1, half * hw2 + ns * 512:
                                                 half * hw2 + (ns + 1) * 512],
                                        start=True, stop=True)
                                sPE.inc(ins)
                            bc_pe = sPE.n
                            denrb_free[0] = bc_pe

                            @block.vector
                            def _(vector, half=half, bc_pe=bc_pe, hi=hi):
                                sPE.wait(vector, bc_pe)
                                pr_free[0] = sDV.inc(vector.tensor_tensor(
                                    s_attn2[hi % 2][:, half * hw2:(half + 1) * hw2],
                                    s_attn2[hi % 2][:, half * hw2:(half + 1) * hw2],
                                    pR[:, 0:hw2], ALU.mult))
                        at_dv = sDV.n

                        if hi == 0 and "at0" in DBG:
                            @block.gpsimd
                            def _(gpsimd, at_dv=at_dv, hi=hi):
                                sDV.wait(gpsimd, at_dv)
                                sW.inc(gpsimd.dma_start(out=DBG["at0"],
                                                        in_=s_attn2[hi % 2][:]), 16)
                        if hi == 0 and "y0" in DBG:
                            @block.gpsimd
                            def _(gpsimd, at_dv=at_dv):
                                sDV.wait(gpsimd, at_dv)
                                pass

                        @block.sync
                        def _(sync, hi=hi, at_dv=at_dv):
                            sDV.wait(sync, at_dv)
                            attn_free[hi % 2] = sSp.inc(sync.dma_start(
                                out=attn_r[:, hi, :], in_=s_attn2[hi % 2][:]), 16)
                    stage_ds.append(stage_d)
                for f in stage_ds:
                    f()
                stage_ds.clear()

            pend = []
            for g in range(ngrp):
                heads_y = []
                for hl in range(2):
                    hi = 2 * g + hl
                    strips = [emit_strip_load(hi, tp) for tp in range(3)]
                    for tp in range(3):
                        slot, rdy = strips[tp]
                        emit_qkv(hi, tp, slot, rdy)
                        wq_free[slot] = sPE.n
                    heads_y.append((hi,) + emit_scans(hi))

                    if hi == 0 and "yl0" in DBG:
                        @block.gpsimd
                        def _(gpsimd):
                            sDV.wait(gpsimd)
                            sW.inc(gpsimd.dma_start(out=DBG["yl0"],
                                                    in_=s_y[0][:]), 16)
                    if hi == 0:
                        for nm, tile in (("q0", s_q[0]), ("k0", s_k[0]), ("kv0", s_kv[0])):
                            if nm in DBG:
                                @block.gpsimd
                                def _(gpsimd, nm=nm, tile=tile):
                                    sDV.wait(gpsimd)
                                    sW.inc(gpsimd.dma_start(out=DBG[nm], in_=tile[:]), 16)

                    if pend and hl == 0:
                        emit_epilogue(*pend.pop(0))
                emit_collective(g)
                emit_select(g)
                pend.append((g, heads_y))
            while pend:
                pg, ph = pend.pop(0)
                emit_epilogue(pg, ph, last=not pend)

            rden_ac = sAC.n

            if "qy" in DBG:
                @block.gpsimd
                def _(gpsimd):
                    sDV.wait(gpsimd)
                    sW.inc(gpsimd.dma_start(out=DBG["qy"], in_=s_qy[:]), 16)

            attn_spilled = sSp.n
            last_qkv_pe = sPE.n

        with ExitStack() as p3:
            pW = [psm(p3, f"pW{i}", [128, 1024]) for i in range(2)]
            s_snap = sb(p3, "s_snap", [128, 8]) if "snap" in DBG else None
            s_v1b = sb(p3, "s_v1b", [128, DM]) if "snap" in DBG else None
            s_att = [sb(p3, f"s_att{i}", [128, NKC, 512], BF16) for i in range(2)]
            s_v1p = [sb(p3, f"s_v1{i}", [128, DM]) for i in range(2)]
            s_o = [sb(p3, f"s_o{i}", [128, DM]) for i in range(2)]

            @block.gpsimd
            def _(gpsimd):
                sPE.wait(gpsimd, last_qkv_pe)
                sW.wait(gpsimd)
                ins = None
                for kc in range(NKC):
                    ins = gpsimd.dma_start(out=s_big[:, kc:kc + 1, 0:DM],
                                           in_=wo_r[:, kc:kc + 1, :])
                    sW.inc(ins, 16)
            wo_ld = sW.n
            if "wo0" in DBG:
                @block.gpsimd
                def _(gpsimd):
                    sW.wait(gpsimd)
                    sW.inc(gpsimd.dma_start(out=DBG["wo0"], in_=s_big[:, 0, 0:DM]), 16)

            att_free = [0, 0]
            pw_free = [0, 0]
            xp_chunk_sp = {}
            v1_ac_free = [0, 0]
            o_free = [0, 0]
            x_free3 = [0, 0]
            for tt in range((NTC + 3) // 4):
                ab = tt % 2

                @block.sync
                def _(sync, tt=tt, ab=ab):
                    sSp.wait(sync, attn_spilled)
                    sPE.wait(sync, att_free[ab])
                    for fq in range(4):
                        sLd.inc(sync.dma_start(
                            out=s_att[ab][:, fq * 4:(fq + 1) * 4, :],
                            in_=attn_r[:, fq * 4:(fq + 1) * 4,
                                       tt * 512:(tt + 1) * 512]), 16)
                att_ld = sLd.n
                if tt == 0 and "att0" in DBG:
                    @block.gpsimd
                    def _(gpsimd, ab=ab, att_ld=att_ld):
                        sLd.wait(gpsimd, att_ld)
                        sW.inc(gpsimd.dma_start(
                            out=DBG["att0"].rearrange("p (a b) -> p a b", a=NKC),
                            in_=s_att[ab][:, :, :]), 16)

                for tcl in range(min(4, NTC - tt * 4)):
                    tc = tt * 4 + tcl
                    xp = tc % 2

                    @block.sync
                    def _(sync, tc=tc, xp=xp):
                        sDV.wait(sync, x_free3[xp])
                        sLd.inc(sync.dma_start(out=s_x[xp][:],
                                               in_=x_d[tc * 128:(tc + 1) * 128, :]), 16)
                    x_ld = sLd.n

                    for half in range(2):
                        @block.tensor
                        def _(tensor, tcl=tcl, half=half, ab=ab, att_ld=att_ld):
                            sLd.wait(tensor, att_ld)
                            sW.wait(tensor, wo_ld)
                            sAC.wait(tensor, rden_ac)
                            sDV.wait(tensor, pw_free[half])
                            ins = None
                            for ns in range(2):
                                c0 = half * 1024 + ns * 512
                                for kc in range(NKC):
                                    ins = tensor.matmul(
                                        pW[half][:, ns * 512:(ns + 1) * 512],
                                        s_att[ab][:, kc, tcl * 128:(tcl + 1) * 128],
                                        s_big[:, kc, c0:c0 + 512],
                                        start=(kc == 0), stop=(kc == NKC - 1))
                            sPE.inc(ins)
                        pe_done = sPE.n
                        if half == 1:
                            att_free[ab] = pe_done

                        @block.vector
                        def _(vector, tc=tc, half=half, pe=pe_done):
                            sPE.wait(vector, pe)
                            sAC.wait(vector, v1_ac_free[tc % 2])
                            pw_free[half] = sDV.inc(vector.tensor_copy(
                                s_v1p[tc % 2][:, half * 1024:(half + 1) * 1024],
                                pW[half][:]))

                    if tc == 0 and "v1" in DBG:
                        @block.gpsimd
                        def _(gpsimd):
                            sDV.wait(gpsimd)
                            sW.inc(gpsimd.dma_start(out=DBG["v1"], in_=s_v1p[0][:]), 16)
                    pp = tc % 2
                    ln_stats(s_v1p[tc % 2], s_scr, pp,
                             pre_wait=[(sDV, sDV.n)] + ([(sW, sW.n)] if DBG else []))
                    v1_ac_free[tc % 2] = sAC.n
                    op = tc % 2

                    @block.vector
                    def _(vector, tc=tc, pp=pp, op=op, xp=xp, x_ld=x_ld):
                        sAC.wait(vector)
                        b = 8 * pp
                        sSp.wait(vector, o_free[op])
                        if tc == 0 and "snap" in DBG:
                            vector.tensor_copy(s_snap[:], s_stat[:, 0:8])
                            vector.tensor_copy(s_v1b[:], s_v1p[0][:])
                        vector.tensor_scalar(s_o[op][:], s_v1p[tc % 2][:],
                                             s_stat[:, b + 2:b + 3],
                                             s_stat[:, b + 6:b + 7], ALU.add, ALU.mult)
                        sLd.wait(vector, x_ld)
                        n = sDV.inc(vector.tensor_tensor(s_o[op][:], s_o[op][:],
                                                         s_x[xp][:], ALU.add))
                        x_free3[xp] = n
                        stat_free[pp] = n
                    o_dv = sDV.n

                    if tc == 0 and "o0" in DBG:
                        @block.gpsimd
                        def _(gpsimd, o_dv=o_dv):
                            sDV.wait(gpsimd, o_dv)
                            sW.inc(gpsimd.dma_start(out=DBG["o0"], in_=s_o[0][:]), 16)
                            if "st3" in DBG:
                                sW.inc(gpsimd.dma_start(out=DBG["st3"], in_=s_stat[:]), 16)
                            if "snap" in DBG:
                                sW.inc(gpsimd.dma_start(out=DBG["snap"], in_=s_snap[:]), 16)
                                sW.inc(gpsimd.dma_start(out=DBG["v1b"], in_=s_v1b[:]), 16)

                    @block.sync
                    def _(sync, tc=tc, op=op, o_dv=o_dv):
                        sDV.wait(sync, o_dv)
                        o_free[op] = sSp.inc(sync.dma_start(
                            out=xp_r[tc * 128:(tc + 1) * 128, :], in_=s_o[op][:]), 16)
                    xp_chunk_sp[tc] = sSp.n

            if "xp" in DBG:
                @block.gpsimd
                def _(gpsimd):
                    sSp.wait(gpsimd)
                    sW.inc(gpsimd.dma_start(out=DBG["xp"], in_=xp_r[:, :]), 16)
            xp_done = sSp.n
        p3_dv = sDV.n
        p123.close()

        with ExitStack() as p4:
            s_x3 = sb(p4, "s_x3", [128, NKC, 512], BF16)
            s_v14 = sb(p4, "s_v14", [128, DM])
            s_xd = sb(p4, "s_xd", [128, DM])
            s_o4 = sb(p4, "s_o4", [128, DM])
            s_wpool = sb(p4, "s_wpool", [128, 4 * NKC * 512], BF16)
            s_gw = [s_wpool[:, i * 8192:(i + 1) * 8192]
                    .rearrange("p (a b) -> p a b", a=NKC) for i in range(2)]
            s_uw = [s_wpool[:, (2 + i) * 8192:(3 + i) * 8192]
                    .rearrange("p (a b) -> p a b", a=NKC) for i in range(2)]
            s_dw = [s_wpool[:, i * 4096:(i + 1) * 4096]
                    .rearrange("p (a b) -> p a b", a=2) for i in range(2)]
            s_h = sb(p4, "s_h", [128, NIC, 512], BF16)
            s_t1 = sb(p4, "s_t1", [128, 512], BF16)
            s_t2 = sb(p4, "s_t2", [128, 512], BF16)
            s_dsb = sb(p4, "s_dsb", [128, DM])
            s_xsum = sb(p4, "s_xsum", [1, 512], BF16)
            s_hsum = sb(p4, "s_hsum", [1, 512])
            s_hsbf = sb(p4, "s_hsbf", [1, 1024], BF16)

            @block.sync
            def _(sync):
                sLd.inc(sync.dma_start(out=s_dsb[:], in_=dsb_d), 16)
            dsb_ld = sLd.n

            x_free4 = [0, 0]
            xd_free = [0]
            tr2_free = [0, 0]
            gw_free = [0, 0]
            uw_free = [0, 0]
            dw_free = [0, 0]
            pf_free = [[0, 0], [0, 0]]
            t1_free = [0]
            x3_free = [0]
            h_free = [0]
            scr4_dv_free = [0]
            o4_free = [0, 0]
            xsum_free = [0]
            down_evac_dv = [0]
            v14_ac_free = [0]

            for ft in range(nft):
                with nc.psum_tensor(f"pTr2_{ft}", [128, 2048], BF16) as pTr2:
                    for cl in range(nfc):
                        tc = ft * nfc + cl
                        xp = tc % 2
                        pp = tc % 2

                        @block.sync
                        def _(sync, tc=tc, xp=xp):
                            sSp.wait(sync, xp_chunk_sp[tc])
                            sDV.wait(sync, x_free4[xp])
                            sLd.inc(sync.dma_start(
                                out=s_x[xp][:],
                                in_=xp_r[tc * 128:(tc + 1) * 128, :]), 16)
                        x_ld = sLd.n
                        ln_stats(s_x[xp], s_xlntm, pp,
                                 pre_wait=[(sLd, x_ld), (sDV, sDV.n)])

                        @block.vector
                        def _(vector, xp=xp, pp=pp):
                            sAC.wait(vector)
                            b = 8 * pp
                            n = sDV.inc(vector.tensor_scalar(
                                s_xlntm[:], s_x[xp][:], s_stat[:, b + 2:b + 3],
                                s_stat[:, b + 6:b + 7], ALU.add, ALU.mult))
                            x_free4[xp] = n
                            stat_free[pp] = n

                        for fq in range(4):
                            tp = fq % 2

                            @block.tensor
                            def _(tensor, fq=fq, tp=tp, ft=ft):
                                if ft == 0:
                                    sDV.wait(tensor, p3_dv)
                                sDV.wait(tensor)
                                sDV.wait(tensor, tr2_free[tp])
                                if fq == 0:
                                    sPE.wait(tensor, x3_free[0])
                                ins = None
                                for j in range(4):
                                    fc = fq * 4 + j
                                    ins = tensor.transpose(
                                        pTr2[:, tp * 1024 + j * 128:
                                             tp * 1024 + (j + 1) * 128],
                                        s_xlntm[:, fc * 128:(fc + 1) * 128], s_ident[:])
                                sPE.inc(ins)

                            @block.vector
                            def _(vector, cl=cl, fq=fq, tp=tp):
                                sPE.wait(vector)
                                tr2_free[tp] = sDV.inc(vector.tensor_copy(
                                    s_x3[:, fq * 4:(fq + 1) * 4, cl * 128:(cl + 1) * 128],
                                    pTr2[:, tp * 1024:tp * 1024 + 512]
                                        .rearrange("p (a b) -> p a b", a=4)))
                x3_dv = sDV.n
                if "x30" in DBG and ft == 0:
                    @block.gpsimd
                    def _(gpsimd):
                        sDV.wait(gpsimd)
                        sW.inc(gpsimd.dma_start(
                            out=DBG["x30"],
                            in_=s_x3[:, :, :].rearrange("p a b -> p (a b)")), 16)

                gw_free[0] = gw_free[1] = sPE.n
                uw_free[0] = uw_free[1] = sPE.n
                with nc.psum_tensor(f"pF0_{ft}", [128, 1024]) as pF0, \
                     nc.psum_tensor(f"pF1_{ft}", [128, 1024]) as pF1, \
                     nc.psum_tensor(f"pRH_{ft}", [1, 512]) as pRH:
                    pF = [pF0, pF1]

                    @block.tensor
                    def _(tensor, x3_dv=x3_dv):
                        sDV.wait(tensor, x3_dv)
                        sDV.wait(tensor, down_evac_dv[0])
                        sDV.wait(tensor, xsum_free[0])
                        ins = None
                        for kc in range(NKC):
                            ins = tensor.matmul(pRH[:], s_ones[:], s_x3[:, kc, :],
                                                start=(kc == 0), stop=(kc == NKC - 1))
                        sPE.inc(ins)
                    xs_pe = sPE.n

                    @block.vector
                    def _(vector, xs_pe=xs_pe):
                        sPE.wait(vector, xs_pe)
                        vector.tensor_copy(s_xsum[:], pRH[:])
                        vector.tensor_copy(s_junk[:], s_ones[:])
                        sDV.inc(vector.tensor_copy(s_junk[:], s_ones[:]))
                    xsum_free[0] = sDV.n
                    hs_started = [False]

                    @block.scalar
                    def _(scalar):
                        act_loadset(scalar, 18)

                    for ib in range(FF // 512):
                        wb = ib % 2
                        for src_r, bufs, fl in ((gw_r, s_gw, gw_free),
                                                (uw_r, s_uw, uw_free)):
                            @block.gpsimd
                            def _(gpsimd, ib=ib, wb=wb, src_r=src_r, bufs=bufs, fl=fl):
                                sPE.wait(gpsimd, fl[wb])
                                for kq in range(4):
                                    sW.inc(gpsimd.dma_start(
                                        out=bufs[wb][:, kq * 4:(kq + 1) * 4, :],
                                        in_=src_r[:, kq * 4:(kq + 1) * 4,
                                                  ib * 512:(ib + 1) * 512]), 16)
                        w_ld = sW.n

                        for icl in range(4):
                            icg = ib * 4 + icl
                            par = icg % 2

                            @block.tensor
                            def _(tensor, wb=wb, icl=icl, par=par, w_ld=w_ld):
                                sW.wait(tensor, w_ld)
                                sAC.wait(tensor, pf_free[par][0])
                                sDV.wait(tensor, pf_free[par][1])
                                sDV.wait(tensor, xsum_free[0])
                                for kc in range(NKC):
                                    tensor.matmul(pF[par][:, 0:512],
                                                  s_gw[wb][:, kc, icl * 128:(icl + 1) * 128],
                                                  s_x3[:, kc, :],
                                                  start=(kc == 0), stop=False)
                                tensor.matmul(pF[par][:, 0:512], s_n128q[:], s_xsum[:],
                                              start=False, stop=True)
                                for kc in range(NKC):
                                    tensor.matmul(pF[par][:, 512:1024],
                                                  s_uw[wb][:, kc, icl * 128:(icl + 1) * 128],
                                                  s_x3[:, kc, :],
                                                  start=(kc == 0), stop=False)
                                ins = tensor.matmul(pF[par][:, 512:1024], s_n128q[:],
                                                    s_xsum[:], start=False, stop=True)
                                sPE.inc(ins)
                            mm_pe = sPE.n
                            if icl == 3:
                                gw_free[wb] = mm_pe
                                uw_free[wb] = mm_pe
                            x3_free[0] = mm_pe

                            @block.scalar
                            def _(scalar, icg=icg, par=par, mm=mm_pe):
                                sPE.wait(scalar, mm)
                                sDV.wait(scalar, t1_free[0])
                                sAC.inc(scalar.activation(
                                    s_t1[:], pF[par][:, 0:512], AF.Silu,
                                    bias=0.0, scale=s_gs[:, icg:icg + 1]))
                            ac_done = sAC.n
                            pf_free[par][0] = ac_done

                            @block.vector
                            def _(vector, icg=icg, par=par, mm=mm_pe, ac=ac_done):
                                sPE.wait(vector, mm)
                                vector.tensor_scalar(s_t2[:], pF[par][:, 512:1024],
                                                     s_us[:, icg:icg + 1], None, ALU.mult)
                                sAC.wait(vector, ac)
                                sPE.wait(vector, h_free[0])
                                sDV.inc(vector.tensor_tensor(
                                    s_h[:, icg, :], s_t1[:], s_t2[:], ALU.mult))
                            h_dv = sDV.n
                            pf_free[par][1] = h_dv
                            t1_free[0] = h_dv

                            @block.tensor
                            def _(tensor, icg=icg, h_dv=h_dv):
                                sDV.wait(tensor, h_dv)
                                sPE.inc(tensor.matmul(pRH[:], s_ones[:], s_h[:, icg, :],
                                                      start=not hs_started[0],
                                                      stop=(icg == NIC - 1)))
                                hs_started[0] = True

                    if "h0" in DBG and ft == 0:
                        @block.gpsimd
                        def _(gpsimd):
                            sDV.wait(gpsimd)
                            sW.inc(gpsimd.dma_start(out=DBG["h0"], in_=s_h[:, 0, :]), 16)

                    @block.vector
                    def _(vector):
                        sPE.wait(vector)
                        vector.tensor_copy(s_hsum[:], pRH[:])
                        vector.tensor_copy(s_junk[:], s_ones[:])
                        vector.tensor_copy(s_hsbf[:, 0:512], s_hsum[:])
                        vector.tensor_copy(s_junk[:], s_ones[:])
                        vector.tensor_tensor(s_hsbf[:, 512:1024], s_hsum[:],
                                             s_hsbf[:, 0:512], ALU.subtract)
                        vector.tensor_copy(s_junk[:], s_ones[:])
                        sDV.inc(vector.tensor_copy(s_junk[:], s_ones[:]))
                    hsum_dv = sDV.n

                    @block.scalar
                    def _(scalar):
                        act_loadset(scalar, 6)

                dw_free[0] = dw_free[1] = sPE.n
                with nc.psum_tensor(f"pN0_{ft}", [128, 1024]) as pN0, \
                     nc.psum_tensor(f"pN1_{ft}", [128, 1024]) as pN1, \
                     nc.psum_tensor(f"pN2_{ft}", [128, 1024]) as pN2, \
                     nc.psum_tensor(f"pN3_{ft}", [128, 1024]) as pN3:
                    pN = [pN0, pN1, pN2, pN3]
                    pn_free = [0, 0, 0, 0]

                    for tp2 in range(nfc // 2):
                        tcl0 = tp2 * 2

                        @block.tensor
                        def _(tensor, hsum_dv=hsum_dv):
                            sDV.wait(tensor, hsum_dv)
                            sAC.wait(tensor)
                        for ic2 in range(NIC // 2):
                            dp = ic2 % 2

                            @block.gpsimd
                            def _(gpsimd, ic2=ic2, dp=dp):
                                sPE.wait(gpsimd, dw_free[dp])
                                sW.inc(gpsimd.dma_start(
                                    out=s_dw[dp][:],
                                    in_=dw_r[:, ic2 * 2:(ic2 + 1) * 2, :]), 16)
                            dw_ld = sW.n

                            @block.tensor
                            def _(tensor, ft=ft, tcl0=tcl0, ic2=ic2, dp=dp, dw_ld=dw_ld):
                                sW.wait(tensor, dw_ld)
                                if ic2 == 0:
                                    for q in range(4):
                                        sDV.wait(tensor, pn_free[q])
                                ins = None
                                for dj in range(2):
                                    ic = ic2 * 2 + dj
                                    for tl in range(2):
                                        tcl = tcl0 + tl
                                        for half in range(2):
                                            for ns in range(2):
                                                c0 = half * 1024 + ns * 512
                                                ins = tensor.matmul(
                                                    pN[tl * 2 + half][:, ns * 512:(ns + 1) * 512],
                                                    s_h[:, ic, tcl * 128:(tcl + 1) * 128],
                                                    s_dw[dp][:, dj, c0:c0 + 512],
                                                    start=(ic == 0), stop=False)
                                sPE.inc(ins)
                            dw_free[dp] = sPE.n
                            if ic2 == NIC // 2 - 1:
                                h_free[0] = sPE.n

                        @block.tensor
                        def _(tensor, ft=ft, tcl0=tcl0):
                            ins = None
                            for tl in range(2):
                                tcl = tcl0 + tl
                                for half in range(2):
                                    for ns in range(2):
                                        for hilo in range(2):
                                            ins = tensor.matmul(
                                                pN[tl * 2 + half][:, ns * 512:(ns + 1) * 512],
                                                s_hsbf[0:1, hilo * 512 + tcl * 128:
                                                       hilo * 512 + (tcl + 1) * 128],
                                                s_n128r[:],
                                                start=False,
                                                stop=(hilo == 1))
                            sPE.inc(ins)
                        dn_pe = sPE.n

                        for tl in range(2):
                            tcl = tcl0 + tl
                            tc = ft * nfc + tcl
                            xp = tc % 2
                            pp = tc % 2

                            @block.sync
                            def _(sync, tc=tc):
                                sSp.wait(sync, xp_chunk_sp[tc])
                                sDV.wait(sync, xd_free[0])
                                sLd.inc(sync.dma_start(
                                    out=s_xd[:],
                                    in_=xp_r[tc * 128:(tc + 1) * 128, :]), 16)
                            x_ld = sLd.n

                            @block.vector
                            def _(vector, tl=tl, dn_pe=dn_pe):
                                sPE.wait(vector, dn_pe)
                                sLd.wait(vector, dsb_ld)
                                sAC.wait(vector, v14_ac_free[0])
                                for half in range(2):
                                    n = sDV.inc(vector.tensor_tensor(
                                        s_v14[:, half * 1024:(half + 1) * 1024],
                                        pN[tl * 2 + half][:],
                                        s_dsb[:, half * 1024:(half + 1) * 1024],
                                        ALU.mult))
                                    pn_free[tl * 2 + half] = n
                            v_dv = sDV.n
                            down_evac_dv[0] = v_dv

                            b = 8 * pp

                            @block.scalar
                            def _(scalar, b=b, v_dv=v_dv, pp=pp):
                                sDV.wait(scalar, v_dv)
                                sDV.wait(scalar, o4_free[0])
                                sOut.wait(scalar, o4_free[1])
                                sDV.wait(scalar, stat_free[pp])
                                scalar.activation(
                                    s_o4[:], s_v14[:], AF.Square, bias=0.0, scale=1.0,
                                    accum_out=s_stat[:, b + 1:b + 2])
                                scalar.activation(
                                    s_o4[:], s_v14[:], AF.Identity, bias=0.0, scale=1.0,
                                    accum_out=s_stat[:, b + 0:b + 1])
                                sAC.inc(scalar.copy(s_stat[:, b + 5:b + 7],
                                                    s_stat[:, b:b + 2]))
                                v14_ac_free[0] = sAC.n

                            @block.vector
                            def _(vector, b=b):
                                sAC.wait(vector)
                                vector.tensor_scalar_mul(s_stat[:, b + 2:b + 3],
                                                         s_stat[:, b:b + 1], -1.0 / DM)
                                vector.tensor_scalar_mul(s_stat[:, b + 7:b + 8],
                                                         s_stat[:, b:b + 1], 1.0 / DM)
                                vector.tensor_scalar(s_stat[:, b + 3:b + 4],
                                                     s_stat[:, b + 1:b + 2],
                                                     1.0 / DM, EPS, ALU.mult, ALU.add)
                                vector.tensor_tensor(s_stat[:, b + 4:b + 5],
                                                     s_stat[:, b + 7:b + 8],
                                                     s_stat[:, b + 7:b + 8], ALU.mult)
                                vector.tensor_copy(s_junk[:], s_stat[:, b + 2:b + 3])
                                sDV.inc(vector.tensor_tensor(s_stat[:, b + 3:b + 4],
                                                             s_stat[:, b + 3:b + 4],
                                                             s_stat[:, b + 4:b + 5],
                                                             ALU.subtract))

                            @block.scalar
                            def _(scalar, b=b):
                                sDV.wait(scalar)
                                scalar.activation(s_stat[:, b + 5:b + 6],
                                                  s_stat[:, b + 3:b + 4],
                                                  AF.Ln, bias=0.0, scale=1.0)
                                sAC.inc(scalar.activation(s_stat[:, b + 6:b + 7],
                                                          s_stat[:, b + 5:b + 6],
                                                          AF.Exp, bias=0.0, scale=-0.5))

                            @block.vector
                            def _(vector, b=b, x_ld=x_ld, pp=pp):
                                sAC.wait(vector)
                                vector.tensor_scalar(s_o4[:], s_v14[:],
                                                     s_stat[:, b + 2:b + 3],
                                                     s_stat[:, b + 6:b + 7],
                                                     ALU.add, ALU.mult)
                                sLd.wait(vector, x_ld)
                                sOut.wait(vector, o4_free[1])
                                n = sDV.inc(vector.tensor_tensor(
                                    s_o4[:], s_o4[:], s_xd[:], ALU.add))
                                xd_free[0] = n
                                stat_free[pp] = n
                            o_dv = sDV.n
                            o4_free[0] = o_dv

                            @block.sync
                            def _(sync, tc=tc, o_dv=o_dv):
                                sDV.wait(sync, o_dv)
                                sOut.inc(sync.dma_start(
                                    out=out_d[tc * 128:(tc + 1) * 128, :],
                                    in_=s_o4[:]), 16)
                            o4_free[1] = sOut.n

            @block.sync
            def _(sync):
                sOut.wait(sync)
            if DBG:
                @block.gpsimd
                def _(gpsimd):
                    sW.wait(gpsimd)
    return nc, DBG



def _u8(a):
    a = np.ascontiguousarray(np.asarray(a))
    assert a.dtype == np.int32
    return np.ascontiguousarray(a.view("<u1")[..., 0::4])


def prep_inputs(L0_attn_x, L0_attn_w_qkv, L0_attn_w_out, L0_attn_decay_param,
                L0_ffn_gate_w, L0_ffn_gate_scale, L0_ffn_up_w, L0_ffn_up_scale,
                L0_ffn_down_w, L0_ffn_down_scale, tok=2048):
    x = np.ascontiguousarray(np.asarray(L0_attn_x), dtype=np.float32)
    B, T, _ = x.shape
    wq = np.ascontiguousarray(np.asarray(L0_attn_w_qkv), dtype=np.float32)
    wo = np.ascontiguousarray(np.asarray(L0_attn_w_out), dtype=np.float32)
    dec = np.ascontiguousarray(
        np.broadcast_to(np.asarray(L0_attn_decay_param, dtype=np.float32), (128, H)))
    gw = np.ascontiguousarray(_u8(L0_ffn_gate_w).T)
    uw = np.ascontiguousarray(_u8(L0_ffn_up_w).T)
    dw = np.ascontiguousarray(_u8(L0_ffn_down_w).T)
    gs = np.ascontiguousarray(
        np.asarray(L0_ffn_gate_scale, dtype=np.float32).reshape(NIC, 128).T)
    us = np.ascontiguousarray(
        np.asarray(L0_ffn_up_scale, dtype=np.float32).reshape(NIC, 128).T)
    dsb = np.ascontiguousarray(
        np.broadcast_to(np.asarray(L0_ffn_down_scale, dtype=np.float32), (128, DM)))
    iota = np.ascontiguousarray(
        np.broadcast_to(-np.arange(1, tok + 1, dtype=np.float16), (128, tok)))
    ident = np.eye(128, dtype=ml_dtypes.bfloat16)
    n128q = np.full((1, 128), -128.0, dtype=ml_dtypes.bfloat16)
    n128r = np.full((1, 512), -128.0, dtype=ml_dtypes.bfloat16)
    ones = np.ones((128, 1), dtype=ml_dtypes.bfloat16)
    onesr = np.ones((1, 128), dtype=ml_dtypes.bfloat16)

    halves = T // tok
    in_maps = []
    for c in range(NCORE):
        s, hf = c // halves, c % halves
        oh = np.zeros((NCORE, 1), np.float32)
        if hf > 0:
            oh[c - 1, 0] = 1.0
        in_maps.append({
            "x": np.ascontiguousarray(x[s, hf * tok:(hf + 1) * tok, :]),
            "wq": wq, "wo": wo, "dec": dec, "oh": oh,
            "gw": gw, "uw": uw, "dw": dw, "gs": gs, "us": us, "dsb": dsb,
            "iota": iota, "ident": ident, "n128q": n128q, "n128r": n128r,
            "ones": ones, "onesr": onesr,
        })
    return in_maps, (B, T)


def kernel(**inputs):
    tok = 2048
    in_maps, (B, T) = prep_inputs(tok=tok, **inputs)
    nc, _ = build_nc(tok=tok)
    res = run_bass_kernel_spmd(nc, in_maps, core_ids=list(range(NCORE)))
    halves = T // tok
    out = np.empty((B, T, DM), dtype=np.float32)
    for c in range(NCORE):
        s, hf = c // halves, c % halves
        out[s, hf * tok:(hf + 1) * tok, :] = res.results[c]["out"]
    return out
